# version 14
# baseline (speedup 1.0000x reference)
import numpy as np
import ml_dtypes
from concourse.bass_utils import run_bass_kernel_spmd

import numpy as np
import concourse.bass as bass
import concourse.mybir as mybir
from contextlib import ExitStack

F32 = mybir.dt.float32
BF16 = mybir.dt.bfloat16
AF = mybir.ActivationFunctionType
ALU = mybir.AluOpType
AX = mybir.AxisListType

H, D, L, EMB, II, S, TC = 16, 128, 64, 2048, 8192, 4096, 2048
NTILE = 16
NSEQT = 32
NJ = EMB // 128
EXT = [32 - 2 * j for j in range(16)]
NEG = -1.0e30
NPASS = 4
NHEADS = 16
NSLOTS = 16
NMIC = 4
NB = 8
MW = 256


class Prog:
    ENGS = ("t", "v", "s", "g", "y")

    def __init__(self, nc, stk):
        self.nc, self.stk = nc, stk
        self.q = {e: [] for e in self.ENGS}
        self.sems, self.cnt = {}, {}

    def sem(self, name):
        self.sems[name] = self.stk.enter_context(self.nc.semaphore(name))
        self.cnt[name] = 0

    import os
    CHAIN = {"v": "dv", "s": "ac"}
    if os.environ.get("KB_CHAIN_PE"):
        CHAIN = {"v": "dv", "s": "ac", "t": "pe"}

    def op(self, eng, fn, waits=(), inc=None, incby=1):
        waits = list(waits)
        ch = self.CHAIN.get(eng)
        if ch is not None and inc == ch and self.cnt[ch] > 0:
            waits.append((ch, self.cnt[ch]))
        val = None
        if inc is not None:
            self.cnt[inc] += incby
            val = self.cnt[inc]
        sems = self.sems
        wl = tuple((sems[s], v) for s, v in waits)
        s_inc = sems[inc] if inc is not None else None

        def run(e, fn=fn, wl=wl, s_inc=s_inc, incby=incby):
            for s, v in wl:
                e.wait_ge(s, v)
            inst = fn(e)
            if s_inc is not None:
                if inst is None:
                    inst = e.nop()
                inst.then_inc(s_inc, incby)
        self.q[eng].append(run)
        return val

    def c(self, name):
        return self.cnt[name]

    def emit(self):
        nc = self.nc
        with nc.Block() as block:
            @block.tensor
            def _(t):
                for f in self.q["t"]:
                    f(t)

            @block.vector
            def _(v):
                for f in self.q["v"]:
                    f(v)

            @block.scalar
            def _(s):
                for f in self.q["s"]:
                    f(s)

            @block.gpsimd
            def _(g):
                for f in self.q["g"]:
                    f(g)

            @block.sync
            def _(y):
                for f in self.q["y"]:
                    f(y)


def build_program(dbg=(), phases=("p0", "p1", "p2", "p3", "p4")):
    import os
    nc = bass.Bass("TRN2", target_bir_lowering=False,
                   detect_race_conditions=not os.environ.get("KB_NORACE"))
    stk = ExitStack()

    def din(name, shape, dt=F32):
        return nc.dram_tensor(name, shape, dt, kind="ExternalInput").ap()

    x_seq = din("x_seq", [S, EMB])
    x_own = din("x_own", [TC, EMB])
    Wq = din("Wq", [EMB, EMB])
    Wk = din("Wk", [L, EMB])
    Wv = din("Wv", [L, EMB])
    q2l = din("q2l", [H, D, L])
    vup = din("vup", [H, L, D])
    Wo = din("Wo", [EMB, EMB])
    wg_p = din("wg_p", [EMB, II])
    wu_p = din("wu_p", [EMB, II])
    wd_p = din("wd_p", [II, EMB])
    ident = din("ident", [128, 128], BF16)
    band3 = din("band3", [128, 384])
    caus2 = din("caus2", [128, 256])
    ones_f = din("ones_f", [1, 128])
    ones_b = din("ones_b", [1, 128], BF16)
    identf = din("identf", [128, 128])
    ident16 = din("ident16", [16, 16])

    out = nc.dram_tensor("out", [TC, EMB], F32, kind="ExternalOutput").ap()
    g_scr = nc.dram_tensor("g_scr", [TC, EMB], BF16, kind="Internal").ap()
    kp_scr = nc.dram_tensor("kp_scr", [128, S], BF16, kind="Internal").ap()
    bk_scr = nc.dram_tensor("bk_scr", [128, 16 * 384], BF16, kind="Internal").ap()
    wd_bf = nc.dram_tensor("wd_bf", [II, EMB], BF16, kind="Internal").ap()
    dbg_out = {}
    if "kpack" in dbg:
        dbg_out["d_kpack"] = nc.dram_tensor("d_kpack", [128, S], BF16, kind="ExternalOutput").ap()
        dbg_out["d_ksum"] = nc.dram_tensor("d_ksum", [1, S], BF16, kind="ExternalOutput").ap()
        dbg_out["d_xn"] = nc.dram_tensor("d_xn", [128, EMB], BF16, kind="ExternalOutput").ap()
    if "g" in dbg:
        dbg_out["d_g"] = nc.dram_tensor("d_g", [TC, EMB], BF16, kind="ExternalOutput").ap()
    if "q" in dbg:
        dbg_out["d_q"] = nc.dram_tensor("d_q", [128, H * TC], BF16, kind="ExternalOutput").ap()
        dbg_out["d_qsb"] = nc.dram_tensor("d_qsb", [128, 256], F32, kind="ExternalOutput").ap()
        dbg_out["d_negc"] = nc.dram_tensor("d_negc", [128, 256], F32, kind="ExternalOutput").ap()
    if "ctx" in dbg:
        dbg_out["d_ctx"] = nc.dram_tensor("d_ctx", [128, H * TC], BF16, kind="ExternalOutput").ap()

    pstk = stk.enter_context(ExitStack())

    def sb(st, name, shape, dt):
        return st.enter_context(nc.sbuf_tensor(name, shape, dt))

    A1 = sb(pstk, "A1", [128, 32768], BF16)
    A2 = sb(pstk, "A2", [128, 32768], BF16)
    A2f = A2.ap().bitcast(F32)

    ident_sb = sb(pstk, "ident_sb", [128, 128], BF16)
    band3_sb = sb(pstk, "band3_sb", [128, 384], F32)
    caus2_sb = sb(pstk, "caus2_sb", [128, 256], F32)
    ones_sb = sb(pstk, "ones_sb", [1, 128], F32)
    ones_bf_sb = sb(pstk, "ones_bf_sb", [1, 128], BF16)
    identf_sb = sb(pstk, "identf_sb", [128, 128], F32)
    ident16_sb = sb(pstk, "ident16_sb", [16, 16], F32)
    q2lt = sb(pstk, "q2lt", [64, H * 128], BF16)
    q2lrs = sb(pstk, "q2lrs", [128, H], BF16)
    vup_sb = sb(pstk, "vup_sb", [128, H * 128], BF16)
    qsb = sb(pstk, "qsb", [128, 256], F32)
    negc_bc = sb(pstk, "negc_bc", [128, 256], F32)
    sumg = [sb(pstk, f"sumg{i}", [128, 8], F32) for i in range(2)]
    sumtot = [sb(pstk, f"sumtot{i}", [128, 1], F32) for i in range(2)]
    rinv = [sb(pstk, f"rinv{i}", [128, 1], F32) for i in range(2)]
    negc15 = sb(pstk, "negc15", [128, 1], F32)
    cmaxr = sb(pstk, "cmaxr", [1, 16], F32)
    cminr = sb(pstk, "cminr", [1, 16], F32)
    cmax_bc = sb(pstk, "cmax_bc", [128, 16], F32)
    cmin_bc = sb(pstk, "cmin_bc", [128, 16], F32)
    tnc = sb(pstk, "tnc", [128, 16], F32)
    tnc2 = sb(pstk, "tnc2", [128, 16], F32)
    s1 = sb(pstk, "s1", [128, 1], F32)
    s2 = sb(pstk, "s2", [128, 1], F32)
    mvec = sb(pstk, "mvec", [128, 1], F32)
    msq = sb(pstk, "msq", [128, 1], F32)
    varv = sb(pstk, "varv", [128, 1], F32)
    rstd = sb(pstk, "rstd", [128, 1], F32)
    bias_ln = sb(pstk, "bias_ln", [128, 1], F32)

    PS = [pstk.enter_context(nc.psum_tensor(f"ps{i}", [128, 512], F32)) for i in range(8)]
    PSb = [p.ap().bitcast(BF16) for p in PS]

    kstk = stk.enter_context(ExitStack())
    kpack = sb(kstk, "kpack", [128, S], BF16)
    BK_sb = sb(kstk, "BK_sb", [128, 16 * 384], BF16)

    P = Prog(nc, stk)
    for s_ in ("pe", "dv", "ac", "xq0", "xq1", "cq", "qq", "vq", "kvq", "wgq0", "wgq1", "wgq2", "wdq", "oq", "wcq0", "wcq1", "bar"):
        P.sem(s_)

    def pe(fn, waits=()):
        return P.op("t", fn, waits=waits, inc="pe")

    def dv(fn, waits=()):
        return P.op("v", fn, waits=waits, inc="dv")

    def dvr(fn, waits=()):
        t = P.op("v", fn, waits=waits, inc="dv")
        P.op("v", lambda e, t=t: e.wait_ge(P.sems["dv"], t))
        return t

    def ac(fn, waits=()):
        return P.op("s", fn, waits=waits, inc="ac")

    def gq(fn, sem, waits=()):
        return P.op("g", fn, waits=waits, inc=sem, incby=16)

    def yq(fn, sem, waits=()):
        return P.op("y", fn, waits=waits, inc=sem, incby=16)

    def oq(fn, waits=()):
        return P.op("y", fn, waits=waits, inc="oq", incby=16)

    DMA_SEMS = ("xq0", "xq1", "cq", "qq", "vq", "kvq", "wgq0", "wgq1", "wgq2", "wdq", "oq", "wcq0", "wcq1")

    def barrier():
        tgt = P.cnt["bar"] + 5
        tail = {"t": ("pe",), "v": ("dv",), "s": ("ac",), "g": (), "y": DMA_SEMS}
        for e in Prog.ENGS:
            w = tuple((s_, P.c(s_)) for s_ in tail[e])
            P.op(e, lambda en: en.nop(), waits=w, inc="bar")
        for e in Prog.ENGS:
            P.op(e, lambda en, tgt=tgt: en.wait_ge(P.sems["bar"], tgt))

    yq(lambda e: e.dma_start(out=ident_sb[:], in_=ident), "cq")
    yq(lambda e: e.dma_start(out=band3_sb[:], in_=band3), "cq")
    yq(lambda e: e.dma_start(out=caus2_sb[:], in_=caus2), "cq")
    yq(lambda e: e.dma_start(out=ones_sb[:], in_=ones_f), "cq")
    yq(lambda e: e.dma_start(out=ones_bf_sb[:], in_=ones_b), "cq")
    yq(lambda e: e.dma_start(out=identf_sb[:], in_=identf), "cq")
    w_const = yq(lambda e: e.dma_start(out=ident16_sb[:], in_=ident16), "cq")

    p0stk = stk.enter_context(ExitStack())
    ksum = sb(p0stk, "ksum", [1, S], BF16)
    wkte = sb(p0stk, "wkte", [128, NJ * 65], BF16)
    wvt = sb(p0stk, "wvt", [128, NJ * 64], BF16)
    wk_bf = sb(p0stk, "wk_bf", [64, EMB], BF16)
    wv_bf = sb(p0stk, "wv_bf", [64, EMB], BF16)
    wkcs = sb(p0stk, "wkcs", [128, NJ], F32)
    gq(lambda e: e.dma_start(out=wk_bf[:], in_=Wk), "kvq")
    w_kv = gq(lambda e: e.dma_start(out=wv_bf[:], in_=Wv), "kvq")

    ev_prev = {}
    for j in range(NJ):
        b = j % 2
        w = [("kvq", w_kv), ("cq", w_const)] if j == 0 else []
        if j >= 2:
            w.append(("dv", ev_prev[j - 2]))
        tk = pe(lambda e, j=j, b=b: e.transpose(PSb[b][:, 0:64], wk_bf[0:64, j*128:(j+1)*128], ident_sb[0:64, 0:64]),
                waits=tuple(w))
        dv(lambda e, j=j, b=b: e.tensor_copy(wkte[:, j*65:j*65+64], PSb[b][:, 0:64]), waits=(("pe", tk),))
        dvr(lambda e, j=j, b=b: e.tensor_reduce(wkcs[:, j:j+1], PSb[b][:, 0:64], AX.X, ALU.add))
        ev_prev[j] = dv(lambda e, j=j: e.tensor_copy(wkte[:, j*65+64:j*65+65], wkcs[:, j:j+1]))
    for j in range(NJ):
        b = j % 2
        w = [("dv", ev_prev[j - 2])] if j >= 2 else [("dv", ev_prev[NJ - 1])]
        tk = pe(lambda e, j=j, b=b: e.transpose(PSb[b][:, 0:64], wv_bf[0:64, j*128:(j+1)*128], ident_sb[0:64, 0:64]),
                waits=tuple(w))
        ev_prev[j] = dv(lambda e, j=j, b=b: e.tensor_copy(wvt[:, j*64:(j+1)*64], PSb[b][:, 0:64]),
                        waits=(("pe", tk),))

    q2l_f = sb(p0stk, "q2l_f", [128, 64], F32)
    q2l_bf = sb(p0stk, "q2l_bf", [128, 64], BF16)
    rs_f = sb(p0stk, "rs_f", [128, 1], F32)
    last_q2l = ev_prev[NJ - 1]
    d1 = None
    for h in range(H):
        w_l = (("dv", d1),) if d1 is not None else ()
        lw = yq(lambda e, h=h: e.dma_start(out=q2l_f[:], in_=q2l[h]), "qq", waits=w_l)
        d1 = dv(lambda e: e.tensor_copy(q2l_bf[:], q2l_f[:]), waits=(("qq", lw), ("dv", last_q2l)))
        dvr(lambda e: e.tensor_reduce(rs_f[:], q2l_bf[:], AX.X, ALU.add))
        d2 = dv(lambda e, h=h: e.tensor_copy(q2lrs[:, h:h+1], rs_f[:]))
        tk = pe(lambda e, h=h: e.transpose(PSb[h % 2][0:64, 0:128], q2l_bf[:], ident_sb[:]), waits=(("dv", d2),))
        last_q2l = dv(lambda e, h=h: e.tensor_copy(q2lt[0:64, h*128:(h+1)*128], PSb[h % 2][0:64, 0:128]),
                      waits=(("pe", tk),))
    gq(lambda e: e.dma_start(out=vup_sb.ap()[64:128, :].rearrange("l (h d) -> l h d", d=128), in_=vup.rearrange("h l d -> l h d")), "vq")

    xstage = [sb(p0stk, "xs0", [128, EMB], F32), sb(p0stk, "xs1", [128, EMB], F32)]
    xn_st = sb(p0stk, "xn_st", [128, EMB], BF16)
    xnfm = sb(p0stk, "xnfm", [128, EMB], BF16)
    trash = A2.ap()[:, 0:2048]

    def ln_tile(src_ap, buf, prev):
        w_load = []
        if prev.get("axn2"):
            w_load.append(("ac", prev["axn2"]))
        if prev.get("s1red2"):
            w_load.append(("dv", prev["s1red2"]))
        xsem = "xq0" if buf == 0 else "xq1"
        lw = yq(lambda e, src_ap=src_ap, buf=buf: e.dma_start(out=xstage[buf][:], in_=src_ap),
                xsem, waits=tuple(w_load))
        s1red = dvr(lambda e, buf=buf: e.tensor_reduce(s1[:], xstage[buf][:], AX.X, ALU.add),
                    waits=((xsem, lw),))
        w_sq = [(xsem, lw)]
        if prev.get("msq1"):
            w_sq.append(("dv", prev["msq1"]))
        a_sq = ac(lambda e, buf=buf: e.activation(trash[:], xstage[buf][:], AF.Square, accum_out=s2[:]),
                  waits=tuple(w_sq))
        dv(lambda e: e.tensor_scalar(mvec[:], s1[:], 1.0 / EMB, None, ALU.mult))
        msq_t = dv(lambda e: e.tensor_scalar(msq[:], s2[:], 1.0 / EMB, None, ALU.mult), waits=(("ac", a_sq),))
        w_tt = [("ac", prev["a21"])] if prev.get("a21") else []
        dv(lambda e: e.tensor_tensor(varv[:], mvec[:], mvec[:], ALU.mult), waits=tuple(w_tt))
        d_v = dv(lambda e: e.scalar_tensor_tensor(varv[:], msq[:], 1.0e-5, varv[:], ALU.add, ALU.subtract))
        ac(lambda e: e.activation(varv[:], varv[:], AF.Ln), waits=(("dv", d_v),))
        a2_ = ac(lambda e: e.activation(rstd[:], varv[:], AF.Exp, scale=-0.5))
        d2_ = dv(lambda e: e.scalar_tensor_tensor(bias_ln[:], mvec[:], -1.0, rstd[:], ALU.mult, ALU.mult),
                 waits=(("ac", a2_),))
        w_xn = [("dv", d2_)]
        if prev.get("lastT1"):
            w_xn.append(("pe", prev["lastT1"]))
        axn = ac(lambda e, buf=buf: e.activation(xn_st[:], xstage[buf][:], AF.Identity,
                                                 bias=bias_ln[:], scale=rstd[:]),
                 waits=tuple(w_xn))
        return {"axn": axn, "s1red": s1red, "msq": msq_t, "a2": a2_, "lw": lw}

    if "p0" in phases:
        hist = []
        kv_prev = None
        wdstage = [A2.ap()[:, 16384:18432], A2.ap()[:, 18432:20480]]
        wd_rr = wd_p.rearrange("(c p) e -> c p e", p=128)
        wd_st = {}

        def prev_of(i):
            p = {}
            if i >= 1:
                p["msq1"] = hist[i-1]["msq"]
                p["a21"] = hist[i-1]["a2"]
                p["lastT1"] = hist[i-1]["lastT"]
            if i >= 2:
                p["axn2"] = hist[i-2]["axn"]
                p["s1red2"] = hist[i-2]["s1red"]
            return p

        for it in range(NSEQT):
            buf = it % 2
            tks = ln_tile(x_seq[it*128:(it+1)*128, :], buf, prev_of(it))
            axn = tks["axn"]
            evs = []
            for j in range(NJ):
                b = 2 + (j % 2)
                w = [("ac", axn)]
                if j >= 2:
                    w.append(("dv", evs[j - 2]))
                tk = pe(lambda e, j=j, b=b: e.transpose(PSb[b][:, 0:128], xn_st[:, j*128:(j+1)*128], ident_sb[:]),
                        waits=tuple(w))
                evs.append(dv(lambda e, j=j, b=b: e.tensor_copy(xnfm[:, j*128:(j+1)*128], PSb[b][:, 0:128]),
                              waits=(("pe", tk),)))
            tks["lastT"] = tk
            for j in range(NJ):
                w = [("dv", evs[j])]
                if j == 0 and kv_prev is not None:
                    w.append(("dv", kv_prev))
                pe(lambda e, j=j: e.matmul(PS[4][0:65, 0:128], wkte[:, j*65:(j+1)*65],
                                           xnfm[:, j*128:(j+1)*128], start=(j == 0), stop=(j == NJ - 1)),
                   waits=tuple(w))
                kv_t = pe(lambda e, j=j: e.matmul(PS[5][0:64, 0:128], wvt[:, j*64:(j+1)*64],
                                                  xnfm[:, j*128:(j+1)*128], start=(j == 0), stop=(j == NJ - 1)))
            dv(lambda e, it=it: e.tensor_copy(kpack[0:64, it*128:(it+1)*128], PS[4][0:64, 0:128]),
               waits=(("pe", kv_t),))
            dv(lambda e, it=it: e.tensor_copy(ksum[0:1, it*128:(it+1)*128], PS[4][64:65, 0:128]))
            kv_prev = dv(lambda e, it=it: e.tensor_copy(kpack[64:128, it*128:(it+1)*128], PS[5][0:64, 0:128]))
            hist.append(tks)
            for rr in range(2):
                row = it * 2 + rr
                wsem = "wdq" if row % 2 == 0 else "vq"
                ssem = "wcq0" if row % 2 == 0 else "wcq1"
                w_l = [(ssem, wd_st[row % 2])] if row >= 2 else []
                wl_t = gq(lambda e, row=row: e.dma_start(out=wdstage[row % 2][:], in_=wd_rr[row]),
                          wsem, waits=tuple(w_l))
                wd_st[row % 2] = yq(lambda e, row=row: e.dma_start(
                    out=wd_bf[row*128:(row+1)*128, :], in_=wdstage[row % 2][:]),
                    ssem, waits=((wsem, wl_t),))

        for j in range(16):
            w0 = max(0, EXT[j] - 3) * 128
            w1 = EXT[j] * 128
            dvr(lambda e, j=j, w0=w0, w1=w1: e.tensor_reduce(cmaxr[:, j:j+1], ksum[:, w0:w1], AX.X, ALU.max))
            dvr(lambda e, j=j, w0=w0, w1=w1: e.tensor_reduce(cminr[:, j:j+1], ksum[:, w0:w1], AX.X, ALU.min))
        d_cm = P.c("dv")
        pe(lambda e: e.matmul(PS[6][0:128, 0:16], ones_sb[:], cmaxr[:, 0:16], start=True, stop=True),
           waits=(("dv", d_cm),))
        t2 = pe(lambda e: e.matmul(PS[7][0:128, 0:16], ones_sb[:], cminr[:, 0:16], start=True, stop=True))
        dv(lambda e: e.tensor_copy(cmax_bc[:], PS[6][0:128, 0:16]), waits=(("pe", t2),))
        dv(lambda e: e.tensor_copy(cmin_bc[:], PS[7][0:128, 0:16]))

        g0 = {}
        for j in range(16):
            e0 = max(0, EXT[j] - 3)
            nwc = EXT[j] - e0
            woff = (3 - nwc) * 128
            for wci in range(nwc):
                w = g0.pop(("bk", wci % 2), [])
                if j == 0 and wci == 0:
                    w += [("cq", P.c("cq")), ("dv", P.c("dv"))]
                bk_mm = pe(lambda e, j=j, wci=wci, e0=e0: e.matmul(
                    PS[0][0:128, 0:128] if wci % 2 == 0 else PS[1][0:128, 0:128],
                    ones_bf_sb[:], ksum[:, (e0+wci)*128:(e0+wci+1)*128],
                    start=True, stop=True), waits=tuple(w))
                d_bk = dv(lambda e, j=j, wci=wci, woff=woff: e.tensor_tensor(
                    BK_sb[:, j*384 + woff + wci*128: j*384 + woff + (wci+1)*128],
                    band3_sb[:, woff + wci*128: woff + (wci+1)*128],
                    (PS[0][0:128, 0:128] if wci % 2 == 0 else PS[1][0:128, 0:128]), ALU.mult),
                    waits=(("pe", bk_mm),))
                g0[("bk", wci % 2)] = [("dv", d_bk)]
            if woff > 0:
                dv(lambda e, j=j, woff=woff: e.tensor_scalar(
                    BK_sb[:, j*384: j*384 + woff], band3_sb[:, 0:woff], 0.0, None, ALU.mult))

        d_sp0 = P.c("dv")
        oq(lambda e: e.dma_start(out=kp_scr, in_=kpack[:]), waits=(("dv", d_sp0),))
        oq(lambda e: e.dma_start(out=bk_scr, in_=BK_sb[:]))

        if "kpack" in dbg:
            dq = P.c("dv")
            oq(lambda e: e.dma_start(out=dbg_out["d_kpack"], in_=kpack[:]), waits=(("dv", dq),))
            oq(lambda e: e.dma_start(out=dbg_out["d_ksum"], in_=ksum[:]))

        for it2 in range(NTILE):
            i = NSEQT + it2
            buf = i % 2
            tks = ln_tile(x_own[it2*128:(it2+1)*128, :], buf, prev_of(i))
            axn = tks["axn"]
            evs = []
            for j in range(NJ):
                b = 2 + (j % 2)
                w = [("ac", axn)]
                if j >= 2:
                    w.append(("dv", evs[j - 2]))
                tk = pe(lambda e, j=j, b=b: e.transpose(PSb[b][:, 0:128], xn_st[:, j*128:(j+1)*128], ident_sb[:]),
                        waits=tuple(w))
                evs.append(dv(lambda e, j=j, b=b, it2=it2: e.tensor_copy(
                    A1[:, j*2048 + it2*128: j*2048 + (it2+1)*128], PSb[b][:, 0:128]),
                    waits=(("pe", tk),)))
            tks["lastT"] = tk
            hist.append(tks)

        if "kpack" in dbg:
            dq = P.c("dv")
            oq(lambda e: e.dma_start(out=dbg_out["d_xn"], in_=A1[:, 0:2048]), waits=(("dv", dq),))

    barrier()
    p0stk.close()
    kstk.close()

    if "p1" in phases:
        p1stk = stk.enter_context(ExitStack())
        acc = A2f
        tb = 16384
        h_all = [A2.ap()[:, tb:tb + 4096], A2.ap()[:, tb + 4096:tb + 8192]]
        habs = A2.ap()[:, tb + 8192:tb + 9216]
        hsp = A2.ap()[:, tb + 9216:tb + 10240]
        hsT = [A2.ap()[:, tb + 10240:tb + 11264], A2.ap()[:, tb + 11264:tb + 12288]]
        spt = [A2.ap()[:, tb + 12288 + k*256: tb + 12288 + (k+1)*256] for k in range(3)]
        gstage = A2.ap()[:, tb + 13056:tb + 15104]
        tmpg = [sb(p1stk, f"tmpg{i}", [128, 128], F32) for i in range(2)]
        SLOTS = 3
        gsems = ("wgq0", "wgq1", "wgq2")
        wgbuf = [sb(p1stk, f"wgb{i}", [128, NJ * 128], BF16) for i in range(SLOTS)]
        wubuf = [sb(p1stk, f"wub{i}", [128, NJ * 128], BF16) for i in range(SLOTS)]
        wdbuf = [sb(p1stk, f"wdb{i}", [128, EMB], BF16) for i in range(8)]

        wg_r = wg_p.rearrange("(j p) i -> p j i", p=128)
        wu_r = wu_p.rearrange("(j p) i -> p j i", p=128)

        NBAT = NPASS * NB
        pend_gu = {}
        guards = {}
        Z = {"prev_sp": None, "prev_tev": None, "h_done": {}, "sp_done": {},
             "pt_done": {}, "gadd": {}, "pair_entry": {}, "wd_cur": None,
             "dn_last": None, "a_g": {}, "og_last": None, "tmpg_last": {}}

        def emit_sp(bp, mi):
            hb = h_all[bp % 2]
            w_ab = [("dv", Z["h_done"][bp % 2])]
            if Z["prev_sp"] is not None:
                w_ab.append(("dv", Z["prev_sp"]))
            a_ab = ac(lambda e, hb=hb, mi=mi: e.activation(habs[:], hb[:, mi*1024:(mi+1)*1024], AF.Abs),
                      waits=tuple(w_ab))
            avs = [habs[:, m_*256:(m_+1)*256] for m_ in range(4)]
            hvs = [hb[:, mi*1024 + m_*256: mi*1024 + (m_+1)*256] for m_ in range(4)]
            dv(lambda e, avs=avs: e.tensor_tensor(spt[0][:], avs[0], avs[1], ALU.min), waits=(("ac", a_ab),))
            dv(lambda e, avs=avs: e.tensor_tensor(spt[1][:], avs[0], avs[1], ALU.max))
            dv(lambda e, avs=avs: e.tensor_tensor(spt[2][:], avs[2], avs[3], ALU.min))
            dv(lambda e: e.tensor_tensor(spt[0][:], spt[0][:], spt[2][:], ALU.max))
            dv(lambda e, avs=avs: e.tensor_tensor(spt[2][:], avs[2], avs[3], ALU.max))
            dv(lambda e: e.tensor_tensor(spt[1][:], spt[1][:], spt[2][:], ALU.min))
            dv(lambda e: e.tensor_tensor(spt[0][:], spt[0][:], spt[1][:], ALU.max))
            sp_d = None
            for m_ in range(4):
                dv(lambda e, a=avs[m_]: e.tensor_tensor(spt[1][:], a, spt[0][:], ALU.is_ge))
                w_sp = ()
                if m_ == 0 and Z["prev_tev"] is not None:
                    w_sp = (("pe", Z["prev_tev"]),)
                sp_d = dv(lambda e, hh=hvs[m_], m_=m_: e.tensor_tensor(hsp[:, m_*256:(m_+1)*256],
                                                                       hh, spt[1][:], ALU.mult), waits=w_sp)
            Z["prev_sp"] = sp_d
            Z["sp_done"][mi] = sp_d

        def emit_PT(bp, mi):
            pair = (4, 5) if mi % 2 == 0 else (6, 7)
            pte = {}
            tk = None
            for cb in range(8):
                bnk = pair[cb % 2]
                w = []
                if cb < 2:
                    w.append(("dv", Z["sp_done"][mi]))
                    w += Z["pair_entry"].pop((pair, cb % 2), [])
                else:
                    w.append(("dv", pte[cb - 2]))
                tk = pe(lambda e, cb=cb, bnk=bnk: e.transpose(PSb[bnk][:, 0:128],
                                                              hsp[:, cb*128:(cb+1)*128], ident_sb[:]),
                        waits=tuple(w))
                pte[cb] = dv(lambda e, cb=cb, bnk=bnk, mi=mi: e.tensor_copy(
                    hsT[mi % 2][:, cb*128:(cb+1)*128], PSb[bnk][:, 0:128]), waits=(("pe", tk),))
            Z["prev_tev"] = tk
            Z["pt_done"][mi] = pte[7]

        def emit_dn(bp, mi, setk):
            pair = (4, 5) if mi % 2 == 0 else (6, 7)
            last = None
            for gsub in range(4):
                for ci in range(8):
                    for pi in range(2):
                        bnk = pair[pi]
                        ch = setk*8 + pi*4 + gsub
                        w = []
                        if gsub == 0 and ci == 0:
                            if setk == 0:
                                w.append(("dv", Z["pt_done"][mi]))
                            else:
                                w.append(("dv", Z["gadd"][(mi, 0, bnk)]))
                        if mi == 0 and setk == 0 and gsub == 0 and ci == 0 and pi == 0:
                            w.append(("wcq0", Z["wd_cur"]))
                        last = pe(lambda e, mi=mi, ci=ci, bnk=bnk, ch=ch, gsub=gsub:
                                  e.matmul(PS[bnk][:, gsub*128:(gsub+1)*128],
                                           hsT[mi % 2][:, ci*128:(ci+1)*128],
                                           wdbuf[ci][:, ch*128:(ch+1)*128],
                                           start=(ci == 0), stop=(ci == 7)), waits=tuple(w))
            Z["dn_last"] = last

        def emit_gadd(bp, mi, setk):
            pair = (4, 5) if mi % 2 == 0 else (6, 7)
            b_ = bp % NB
            ticks = {}
            for pi in range(2):
                bnk = pair[pi]
                e0 = setk*1024 + pi*512
                w = []
                if pi == 0:
                    w.append(("pe", Z["dn_last"]))
                if b_ == 0 and setk == 0 and pi == 0 and (mi in Z["a_g"]):
                    w.append(("ac", Z["a_g"][mi]))
                if b_ == 0:
                    t = dv(lambda e, mi=mi, e0=e0, bnk=bnk: e.tensor_copy(
                        acc[:, mi*2048 + e0: mi*2048 + e0 + 512], PS[bnk][:, 0:512]), waits=tuple(w))
                else:
                    t = dv(lambda e, mi=mi, e0=e0, bnk=bnk: e.tensor_tensor(
                        acc[:, mi*2048 + e0: mi*2048 + e0 + 512],
                        acc[:, mi*2048 + e0: mi*2048 + e0 + 512], PS[bnk][:, 0:512], ALU.add),
                        waits=tuple(w))
                Z["gadd"][(mi, setk, bnk)] = t
                ticks[pi] = t
            if setk == 1:
                for pi in range(2):
                    Z["pair_entry"][(pair, pi)] = [("dv", ticks[pi])]

        def emit_spill(p_i):
            for mi in range(NMIC):
                mt = p_i * NMIC + mi
                pair = (4, 5) if mi % 2 == 0 else (6, 7)
                w = [("dv", Z["gadd"][(mi, 1, pair[1])])]
                if Z["og_last"] is not None:
                    w.append(("oq", Z["og_last"]))
                a_g = ac(lambda e, mi=mi: e.activation(gstage[:], acc[:, mi*2048:(mi+1)*2048], AF.Silu),
                         waits=tuple(w))
                og = oq(lambda e, mt=mt: e.dma_start(out=g_scr[mt*128:(mt+1)*128, :], in_=gstage[:]),
                        waits=(("ac", a_g),))
                Z["a_g"][mi] = a_g
                Z["og_last"] = og

        def emit_down_pos(bp, s):
            if s == 0:
                emit_sp(bp, 0); emit_PT(bp, 0); emit_sp(bp, 1)
            elif s == 1:
                emit_dn(bp, 0, 0); emit_gadd(bp, 0, 0); emit_PT(bp, 1); emit_sp(bp, 2)
            elif s == 2:
                emit_dn(bp, 0, 1); emit_gadd(bp, 0, 1)
                emit_dn(bp, 1, 0); emit_gadd(bp, 1, 0)
            elif s == 3:
                emit_PT(bp, 2); emit_sp(bp, 3); emit_dn(bp, 1, 1); emit_gadd(bp, 1, 1)
            elif s == 4:
                emit_dn(bp, 2, 0); emit_gadd(bp, 2, 0); emit_PT(bp, 3)
            elif s == 5:
                emit_dn(bp, 2, 1); emit_gadd(bp, 2, 1)
                emit_dn(bp, 3, 0); emit_gadd(bp, 3, 0)
            elif s == 6:
                emit_dn(bp, 3, 1); emit_gadd(bp, 3, 1)
                if bp % NB == NB - 1:
                    emit_spill(bp // NB)

        DPRV = None
        slice_ctr = 0
        for bat in range(NBAT):
            p_i, b = divmod(bat, NB)
            m0 = p_i * NMIC
            hb = h_all[bat % 2]
            last_h = None
            for s in range(8):
                m, c = s // 2, s % 2
                slot = slice_ctr % SLOTS
                gsem = gsems[slot]
                c0 = m*2048 + 256*b + c*128
                w = [("pe", pend_gu[slot])] if slot in pend_gu else []
                gq(lambda e, slot=slot, c0=c0: e.dma_start(
                    out=wgbuf[slot].ap().rearrange("p (j i) -> p j i", i=128),
                    in_=wg_r[:, :, c0:c0+128]), gsem, waits=tuple(w))
                lt = gq(lambda e, slot=slot, c0=c0: e.dma_start(
                    out=wubuf[slot].ap().rearrange("p (j i) -> p j i", i=128),
                    in_=wu_r[:, :, c0:c0+128]), gsem)
                for mi in range(NMIC):
                    mt = m0 + mi
                    gbank = (mi % 2) * 2
                    ubank = gbank + 1
                    for j in range(NJ):
                        w = []
                        if j == 0:
                            if mi == 0:
                                w.append((gsem, lt))
                            w += guards.pop((gbank, c), [])
                        pe(lambda e, c=c, j=j, mt=mt, slot=slot, gbank=gbank:
                           e.matmul(PS[gbank][:, c*128:(c+1)*128],
                                    A1[:, j*2048 + mt*128: j*2048 + (mt+1)*128],
                                    wgbuf[slot][:, j*128:(j+1)*128],
                                    start=(j == 0), stop=(j == NJ - 1)), waits=tuple(w))
                        w2 = tuple(guards.pop((ubank, c), [])) if j == 0 else ()
                        gu_mm = pe(lambda e, c=c, j=j, mt=mt, slot=slot, ubank=ubank:
                                   e.matmul(PS[ubank][:, c*128:(c+1)*128],
                                            A1[:, j*2048 + mt*128: j*2048 + (mt+1)*128],
                                            wubuf[slot][:, j*128:(j+1)*128],
                                            start=(j == 0), stop=(j == NJ - 1)), waits=w2)
                    w_si = [("pe", gu_mm)]
                    if (mi % 2) in Z["tmpg_last"]:
                        w_si.append(("dv", Z["tmpg_last"][mi % 2]))
                    a_si = ac(lambda e, c=c, gbank=gbank, mi=mi: e.activation(
                        tmpg[mi % 2][:], PS[gbank][:, c*128:(c+1)*128], AF.Silu), waits=tuple(w_si))
                    hslice = hb[:, mi*1024 + m*256 + c*128: mi*1024 + m*256 + (c+1)*128]
                    last_h = dv(lambda e, hslice=hslice, c=c, ubank=ubank, mi=mi:
                                e.tensor_tensor(hslice, tmpg[mi % 2][:],
                                                PS[ubank][:, c*128:(c+1)*128], ALU.mult),
                                waits=(("ac", a_si),))
                    Z["tmpg_last"][mi % 2] = last_h
                    guards[(gbank, c)] = [("ac", a_si)]
                    guards[(ubank, c)] = [("dv", last_h)]
                pend_gu[slot] = gu_mm
                slice_ctr += 1
                if DPRV is not None:
                    emit_down_pos(DPRV, s)
                if s == 6:
                    wt = None
                    for ci in range(8):
                        row = 16 * (ci // 2) + 2 * b + (ci % 2)
                        w = [("pe", Z["dn_last"])] if Z["dn_last"] is not None else []
                        wt = yq(lambda e, ci=ci, row=row: e.dma_start(
                            out=wdbuf[ci][:], in_=wd_bf[row*128:(row+1)*128, :]),
                            "wcq0", waits=tuple(w))
                    Z["wd_cur"] = wt
            Z["h_done"][bat % 2] = last_h
            DPRV = bat
        for s in range(8):
            emit_down_pos(DPRV, s)

        if "g" in dbg:
            ov = P.c("oq")
            for mt in range(NTILE):
                lw = yq(lambda e, mt=mt: e.dma_start(out=gstage[:], in_=g_scr[mt*128:(mt+1)*128, :]),
                        "xq0", waits=(("oq", ov),))
                ov = oq(lambda e, mt=mt: e.dma_start(out=dbg_out["d_g"][mt*128:(mt+1)*128, :], in_=gstage[:]),
                        waits=(("xq0", lw),))
        barrier()
        p1stk.close()

    if ("p2" in phases) or ("p3" in phases):
        k2stk = stk.enter_context(ExitStack())
        kp3 = sb(k2stk, "kpack2", [128, S], BF16)
        BK3 = sb(k2stk, "BK_sb2", [128, 16 * 384], BF16)
        yq(lambda e: e.dma_start(out=kp3[:], in_=kp_scr), "cq")
        rl2 = yq(lambda e: e.dma_start(out=BK3[:], in_=bk_scr), "cq")

    if "p2" in phases:
        p2stk = stk.enter_context(ExitStack())
        wqh = [sb(p2stk, f"wqh{i}", [128, EMB], BF16) for i in range(2)]
        wqT = sb(p2stk, "wqT", [128, EMB], BF16)
        qs_col = sb(p2stk, "qs_col", [128, 16], F32)
        qsT = sb(p2stk, "qsT", [16, 128], F32)
        qtmax = sb(p2stk, "qtmax", [16, 1], F32)
        qtmin = sb(p2stk, "qtmin", [16, 1], F32)
        tb1 = sb(p2stk, "tb1", [16, 1], F32)
        tb2 = sb(p2stk, "tb2", [16, 1], F32)
        negc_row = sb(p2stk, "negc_row", [1, 16], F32)
        guards2 = {}
        pend_wq = {}
        pend_wqT = None
        for h in range(NHEADS):
            slot = h % 2
            gsem = "wgq0" if slot == 0 else "wgq1"
            w = [("pe", pend_wq[slot])] if slot in pend_wq else []
            lw = gq(lambda e, h=h, slot=slot: e.dma_start(out=wqh[slot][:], in_=Wq[h*128:(h+1)*128, :]),
                    gsem, waits=tuple(w))
            tevq = []
            for j in range(NJ):
                bnk = j % 2
                w = [(gsem, lw)] if j == 0 else []
                w += guards2.pop(("t", bnk), [])
                if pend_wqT is not None and j < 2:
                    w.append(("pe", pend_wqT))
                tk = pe(lambda e, j=j, bnk=bnk, slot=slot: e.transpose(
                    PSb[bnk][:, 0:128], wqh[slot][:, j*128:(j+1)*128], ident_sb[:]), waits=tuple(w))
                tevq.append(dv(lambda e, j=j, bnk=bnk: e.tensor_copy(wqT[:, j*128:(j+1)*128], PSb[bnk][:, 0:128]),
                               waits=(("pe", tk),)))
                guards2[("t", bnk)] = [("dv", tevq[-1])]
            for mp in range(8):
                for j in range(NJ):
                    w = []
                    if j == 0:
                        w += guards2.pop(("q", 2), []) + [("dv", tevq[-1])]
                    pe(lambda e, j=j, mp=mp, h=h: e.matmul(
                        PS[2][:, 0:128], wqT[:, j*128:(j+1)*128],
                        A1[:, j*2048 + (2*mp)*128: j*2048 + (2*mp+1)*128],
                        start=(j == 0), stop=(j == NJ - 1)), waits=tuple(w))
                    w2 = tuple(guards2.pop(("q", 3), [])) if j == 0 else ()
                    q_mm = pe(lambda e, j=j, mp=mp, h=h: e.matmul(
                        PS[3][:, 0:128], wqT[:, j*128:(j+1)*128],
                        A1[:, j*2048 + (2*mp+1)*128: j*2048 + (2*mp+2)*128],
                        start=(j == 0), stop=(j == NJ - 1)), waits=w2)
                ev1 = dv(lambda e, mp=mp, h=h: e.tensor_copy(A2[:, h*2048 + (2*mp)*128: h*2048 + (2*mp+1)*128],
                                                             PS[2][:, 0:128]), waits=(("pe", q_mm),))
                ev2 = dv(lambda e, mp=mp, h=h: e.tensor_copy(A2[:, h*2048 + (2*mp+1)*128: h*2048 + (2*mp+2)*128],
                                                             PS[3][:, 0:128]))
                guards2[("q", 2)] = [("dv", ev1)]
                guards2[("q", 3)] = [("dv", ev2)]
                pend_wqT = q_mm
            pend_wq[slot] = q_mm
            for mt in range(NTILE):
                w = [("dv", P.c("dv"))] if mt == 0 else []
                if mt == 0:
                    w += guards2.pop(("qs",), [])
                qs_mm = pe(lambda e, mt=mt, h=h: e.matmul(
                    PS[4][:, mt:mt+1], A2[:, h*2048 + mt*128: h*2048 + (mt+1)*128],
                    q2lrs[:, h:h+1], start=True, stop=True), waits=tuple(w))
            d_qs = dv(lambda e: e.tensor_copy(qs_col[:], PS[4][:, 0:16]), waits=(("pe", qs_mm),))
            guards2[("qs",)] = [("dv", d_qs)]
            dv(lambda e, h=h: e.tensor_scalar(qsb[:, h*16:(h+1)*16], qs_col[:], 0.0625, None, ALU.mult))
            d_q16 = P.c("dv")
            dv(lambda e, h=h: e.tensor_tensor(tnc[:], qsb[:, h*16:(h+1)*16], cmax_bc[:], ALU.mult))
            dv(lambda e, h=h: e.tensor_tensor(tnc2[:], qsb[:, h*16:(h+1)*16], cmin_bc[:], ALU.mult))
            dv(lambda e: e.tensor_tensor(tnc[:], tnc[:], tnc2[:], ALU.max))
            dv(lambda e, h=h: e.tensor_scalar(negc_bc[:, h*16:(h+1)*16], tnc[:], -1.0, -10.0,
                                              ALU.mult, ALU.add))
        if "q" in dbg:
            dq = P.c("dv")
            oq(lambda e: e.dma_start(out=dbg_out["d_q"][:, 0:NHEADS*2048], in_=A2[:, 0:NHEADS*2048]), waits=(("dv", dq),))
            oq(lambda e: e.dma_start(out=dbg_out["d_qsb"][:, 0:NHEADS*16], in_=qsb[:, 0:NHEADS*16]))
            oq(lambda e: e.dma_start(out=dbg_out["d_negc"][:, 0:NHEADS*16], in_=negc_bc[:, 0:NHEADS*16]))
        barrier()
        p2stk.close()

    if "p3" in phases:
        p3stk = stk.enter_context(ExitStack())
        A_sb = sb(p3stk, "A_sb", [128, S], BF16)
        Vx_sb = sb(p3stk, "Vx_sb", [128, S], BF16)
        probs = [sb(p3stk, f"probs{i}", [128, S], BF16) for i in range(2)]
        pts_all = sb(p3stk, "pts_all", [128, S], BF16)
        ctxt_f = sb(p3stk, "ctxt_f", [128, 128], F32)
        g3 = {("bk", 0): [("cq", rl2)], ("bk", 1): [("cq", rl2)]}
        for h in range(NHEADS):
            for grp in range(8):
                bnk = grp % 2
                for cc in range(4):
                    c = grp * 4 + cc
                    w = g3.pop(("av", bnk), []) if cc == 0 else []
                    if h == 0 and grp < 2 and cc == 0:
                        w += g3.pop(("bk", bnk), [])
                    av_mm = pe(lambda e, h=h, c=c, cc=cc, bnk=bnk: e.matmul(
                        PS[bnk][:, cc*128:(cc+1)*128], q2lt[0:64, h*128:(h+1)*128],
                        kp3[0:64, c*128:(c+1)*128], start=(cc == 0), stop=(cc == 3)), waits=tuple(w))
                d_av = dv(lambda e, grp=grp, bnk=bnk: e.tensor_scalar(
                    A_sb[:, grp*512:(grp+1)*512], PS[bnk][:], 0.125, None, ALU.mult),
                    waits=(("pe", av_mm),))
                g3[("av", bnk)] = [("dv", d_av)]
            for grp in range(8):
                bnk = grp % 2
                for cc in range(4):
                    c = grp * 4 + cc
                    w = g3.pop(("av", bnk), []) if cc == 0 else []
                    av_mm = pe(lambda e, h=h, c=c, cc=cc, bnk=bnk: e.matmul(
                        PS[bnk][:, cc*128:(cc+1)*128], kp3[64:128, c*128:(c+1)*128],
                        vup_sb[64:128, h*128:(h+1)*128], start=(cc == 0), stop=(cc == 3)), waits=tuple(w))
                d_av = dv(lambda e, grp=grp, bnk=bnk: e.tensor_copy(
                    Vx_sb[:, grp*512:(grp+1)*512], PS[bnk][:]), waits=(("pe", av_mm),))
                g3[("av", bnk)] = [("dv", d_av)]
            def emit_scores_mm(h, j):
                E = EXT[j]
                ngrp = (E + 3) // 4
                pp = j % 2
                last_ex = None
                for grp in range(ngrp):
                    bnk = 2 + grp % 2
                    nc_in_g = min(4, E - grp * 4)
                    w = g3.pop(("sc", bnk), [])
                    if grp == 0:
                        w += g3.pop(("ptw", pp), [])
                        w += g3.pop(("sg", pp), [])
                    for cc in range(nc_in_g):
                        c = grp * 4 + cc
                        sc_mm = pe(lambda e, h=h, j=j, c=c, cc=cc, bnk=bnk, nc_in_g=nc_in_g: e.matmul(
                            PS[bnk][:, cc*128:(cc+1)*128],
                            A2[:, h*2048 + j*128:(h*2048 + (j+1)*128)],
                            A_sb[:, c*128:(c+1)*128], start=(cc == 0), stop=(cc == nc_in_g - 1)),
                            waits=tuple(w) if cc == 0 else ())
                        w = []
                    lo_g, hi_g = grp * 4, grp * 4 + nc_in_g
                    e0 = max(0, E - 3)
                    blo, bhi = max(lo_g, e0), hi_g
                    dvt = None
                    if blo < bhi:
                        woff3 = (3 - (E - e0)) * 128
                        dvt = dv(lambda e, h=h, j=j, blo=blo, bhi=bhi, lo_g=lo_g, e0=e0, woff3=woff3, grp=grp:
                                 e.scalar_tensor_tensor(
                                     PS[2 + grp % 2][:, (blo-lo_g)*128:(bhi-lo_g)*128],
                                     BK3[:, j*384 + woff3 + (blo-e0)*128: j*384 + woff3 + (bhi-e0)*128],
                                     qsb[:, h*16+j:h*16+j+1],
                                     PS[2 + grp % 2][:, (blo-lo_g)*128:(bhi-lo_g)*128],
                                     ALU.mult, ALU.add), waits=(("pe", sc_mm),))
                    if hi_g == E:
                        dvt = dv(lambda e, j=j, nc_in_g=nc_in_g, grp=grp: e.tensor_tensor(
                            PS[2 + grp % 2][:, (nc_in_g-2)*128: nc_in_g*128],
                            PS[2 + grp % 2][:, (nc_in_g-2)*128: nc_in_g*128],
                            caus2_sb[:], ALU.add),
                            waits=(("pe", sc_mm),) if dvt is None else ())
                    if j == 15:
                        dvt = dvr(lambda e, nc_in_g=nc_in_g, grp=grp: e.tensor_reduce(
                            negc15[:], PS[2 + grp % 2][:, 0:nc_in_g*128], AX.X, ALU.max),
                            waits=(("dv", dvt),) if dvt is not None else (("pe", sc_mm),))
                        dvt = dv(lambda e: e.tensor_scalar(negc15[:], negc15[:], -1.0, None, ALU.mult))
                        bias_ap = negc15
                    else:
                        bias_ap = None
                    w_exp = [("pe", sc_mm)] if dvt is None else [("dv", dvt)]
                    a_ex = ac(lambda e, h=h, j=j, grp=grp, nc_in_g=nc_in_g, bias_ap=bias_ap, pp=pp:
                              e.activation(
                                  probs[pp][:, grp*512: grp*512 + nc_in_g*128],
                                  PS[2 + grp % 2][:, 0:nc_in_g*128],
                                  AF.Exp, bias=(bias_ap[:] if bias_ap is not None else
                                                negc_bc[:, h*16+j:h*16+j+1]),
                                  accum_out=sumg[pp][:, grp:grp+1]),
                              waits=tuple(w_exp))
                    last_ex = a_ex
                    g3[("sc", bnk)] = [("ac", a_ex)] + ([("dv", dvt)] if dvt is not None else [])
                return last_ex

            def emit_rtail(h, j, a_last):
                E = EXT[j]
                ngrp = (E + 3) // 4
                pp = j % 2
                rd = dvr(lambda e, ngrp=ngrp, pp=pp: e.tensor_reduce(sumtot[pp][:], sumg[pp][:, 0:ngrp],
                                                                     AX.X, ALU.add),
                         waits=(("ac", a_last),))
                g3[("sg", pp)] = [("dv", rd)]
                ac(lambda e, pp=pp: e.activation(sumtot[pp][:], sumtot[pp][:], AF.Ln),
                   waits=(("dv", rd),))
                w_r = g3.pop(("rctx", pp), [])
                a_r = ac(lambda e, pp=pp: e.activation(rinv[pp][:], sumtot[pp][:], AF.Exp, scale=-1.0),
                         waits=tuple(w_r))
                g3[("rinv", pp)] = a_r

            def emit_ptpv(h, j, a_last):
                E = EXT[j]
                pp = j % 2
                R = 256 + pp * 128
                pte = {}
                last_tk = None
                lastmm = None

                def one_pv(c):
                    nonlocal lastmm
                    bnkp = 6 + (c % 2)
                    w = [("dv", pte[c])]
                    if c < 2:
                        w = g3.pop(("ctx", bnkp, R), []) + w
                    lastmm = pe(lambda e, c=c, bnkp=bnkp, R=R, E=E: e.matmul(
                        PS[bnkp][:, R:R+128], pts_all[:, c*128:(c+1)*128], Vx_sb[:, c*128:(c+1)*128],
                        start=(c < 2), stop=(c >= E - 2)), waits=tuple(w))

                for c in range(E):
                    tbnk = 4 + (c % 2)
                    w = list(g3.get(("pt", c % 2), []))
                    if c < 2:
                        w.append(("ac", a_last))
                    tk = pe(lambda e, c=c, tbnk=tbnk, pp=pp: e.transpose(
                        PSb[tbnk][:, 0:128], probs[pp][:, c*128:(c+1)*128], ident_sb[:]), waits=tuple(w))
                    pte[c] = dv(lambda e, c=c, tbnk=tbnk: e.tensor_copy(pts_all[:, c*128:(c+1)*128],
                                                                       PSb[tbnk][:, 0:128]),
                                waits=(("pe", tk),))
                    g3[("pt", c % 2)] = [("dv", pte[c])]
                    last_tk = tk
                    if c >= 2:
                        one_pv(c - 2)
                one_pv(E - 2)
                one_pv(E - 1)
                g3[("ptw", pp)] = [("pe", last_tk)]
                a_r = g3[("rinv", pp)]
                dv(lambda e, R=R, pp=pp: e.tensor_scalar(ctxt_f[:], PS[7][:, R:R+128], rinv[pp][:],
                                                         None, ALU.mult),
                   waits=(("pe", lastmm), ("ac", a_r)))
                d_ctx = dv(lambda e, h=h, j=j, pp=pp, R=R: e.scalar_tensor_tensor(
                    A1[:, h*2048 + j*128: h*2048 + (j+1)*128], PS[6][:, R:R+128], rinv[pp][:],
                    ctxt_f[:], ALU.mult, ALU.add))
                g3[("ctx", 6, R)] = [("dv", d_ctx)]
                g3[("ctx", 7, R)] = [("dv", d_ctx)]
                g3[("rctx", pp)] = [("dv", d_ctx)]

            pend = None
            for j in range(NSLOTS):
                ax_j = emit_scores_mm(h, j)
                if pend is not None:
                    emit_ptpv(h, pend[0], pend[1])
                emit_rtail(h, j, ax_j)
                pend = (j, ax_j)
            emit_ptpv(h, pend[0], pend[1])

        if "ctx" in dbg:
            dq = P.c("dv")
            oq(lambda e: e.dma_start(out=dbg_out["d_ctx"][:, 0:NHEADS*2048], in_=A1[:, 0:NHEADS*2048]), waits=(("dv", dq),))
        barrier()
        p3stk.close()

    if ("p2" in phases) or ("p3" in phases):
        k2stk.close()

    if "p4" in phases:
        p4stk = stk.enter_context(ExitStack())
        woh = [sb(p4stk, f"woh{i}", [128, EMB], BF16) for i in range(2)]
        gbuf = [sb(p4stk, f"gbuf{i}", [128, EMB], BF16) for i in range(2)]
        xbuf = [sb(p4stk, f"xbuf{i}", [128, EMB], F32) for i in range(2)]
        obuf = [sb(p4stk, f"obuf{i}", [128, EMB], F32) for i in range(2)]
        tmp4 = sb(p4stk, "tmp4", [128, 512], F32)
        cstage = [sb(p4stk, f"cst{i}", [128, EMB], BF16) for i in range(2)]
        g4 = {}
        for ec in range(16):
            slot = ec % 2
            gsem = "wgq0" if slot == 0 else "wgq1"
            w = g4.pop(("wo", slot), [])
            lw = gq(lambda e, ec=ec, slot=slot: e.dma_start(out=woh[slot][:], in_=Wo[ec*128:(ec+1)*128, :]),
                    gsem, waits=tuple(w))
            last = None
            for hh in range(16):
                bnk = hh % 2
                w = [(gsem, lw)] if hh == 0 else []
                w += g4.pop(("t", bnk), [])
                tk = pe(lambda e, hh=hh, ec=ec, bnk=bnk, slot=slot: e.transpose(
                    PSb[bnk][:, 0:128], woh[slot][:, hh*128:(hh+1)*128], ident_sb[:]), waits=tuple(w))
                last = dv(lambda e, hh=hh, ec=ec, bnk=bnk: e.tensor_copy(
                    A2[:, hh*2048 + ec*128: hh*2048 + (ec+1)*128], PSb[bnk][:, 0:128]),
                    waits=(("pe", tk),))
                g4[("t", bnk)] = [("dv", last)]
            g4[("wo", slot)] = [("pe", tk)]
        d_wot = P.c("dv")
        for mt in range(NTILE):
            pbuf = mt % 2
            w = g4.pop(("g", pbuf), [])
            lg = yq(lambda e, mt=mt, pbuf=pbuf: e.dma_start(out=gbuf[pbuf][:], in_=g_scr[mt*128:(mt+1)*128, :]),
                    "xq0" if pbuf == 0 else "xq1", waits=tuple(w))
            lx = yq(lambda e, mt=mt, pbuf=pbuf: e.dma_start(out=xbuf[pbuf][:], in_=x_own[mt*128:(mt+1)*128, :]),
                    "xq0" if pbuf == 0 else "xq1")
            d_cst = None
            for hh in range(16):
                bnk = hh % 2
                w = g4.pop(("t", bnk), [])
                if hh < 2 and mt == 0:
                    w += [("dv", d_wot)]
                tk = pe(lambda e, hh=hh, mt=mt, bnk=bnk: e.transpose(
                    PSb[bnk][:, 0:128], A1[:, hh*2048 + mt*128: hh*2048 + (mt+1)*128],
                    ident_sb[:]), waits=tuple(w))
                d_cst = dv(lambda e, hh=hh, bnk=bnk, pbuf=pbuf: e.tensor_copy(
                    cstage[pbuf][:, hh*128:(hh+1)*128], PSb[bnk][:, 0:128]),
                    waits=(("pe", tk),))
                g4[("t", bnk)] = [("dv", d_cst)]
            for half in range(2):
                for rr in range(4):
                    for bo in range(2):
                        bnk = 2 + bo
                        ec = half * 8 + bo * 4 + rr
                        for hh in range(16):
                            w = []
                            if hh == 0:
                                w = g4.pop(("s", bnk, rr), [])
                                if rr == 0 and bo == 0 and half == 0:
                                    w += [("dv", d_cst)]
                                    if mt == 0:
                                        w += [("dv", d_wot)]
                            s_mm = pe(lambda e, ec=ec, hh=hh, bnk=bnk, rr=rr, mt=mt, pbuf=pbuf: e.matmul(
                                PS[bnk][:, rr*128:(rr+1)*128],
                                cstage[pbuf][:, hh*128:(hh+1)*128],
                                A2[:, hh*2048 + ec*128: hh*2048 + (ec+1)*128],
                                start=(hh == 0), stop=(hh == 15)), waits=tuple(w))
                        g4[("smm", bnk, rr)] = s_mm
                for bo in range(2):
                    bnk = 2 + bo
                    e0c = half * 1024 + bo * 512
                    w = [("pe", g4[("smm", bnk, 3)]), ((("xq0" if pbuf == 0 else "xq1"), lx))]
                    dv(lambda e, bnk=bnk, e0c=e0c, pbuf=pbuf: e.tensor_tensor(
                        tmp4[:], PS[bnk][:], gbuf[pbuf][:, e0c:e0c+512], ALU.mult), waits=tuple(w))
                    d_cb = dv(lambda e, bnk=bnk, e0c=e0c, pbuf=pbuf: e.tensor_tensor(
                        obuf[pbuf][:, e0c:e0c+512], tmp4[:], xbuf[pbuf][:, e0c:e0c+512], ALU.add))
                    for rr in range(4):
                        g4[("s", bnk, rr)] = [("dv", d_cb)]
            so = oq(lambda e, mt=mt, pbuf=pbuf: e.dma_start(out=out[mt*128:(mt+1)*128, :], in_=obuf[pbuf][:]),
                    waits=(("dv", P.c("dv")),))
            g4[("g", pbuf)] = [("oq", so)]

    for s_ in ("oq", "xq0", "xq1", "cq", "qq", "vq", "kvq", "wgq0", "wgq1", "wgq2", "wdq", "wcq0", "wcq1"):
        P.op("y", lambda e, s_=s_: e.wait_ge(P.sems[s_], P.c(s_)))

    P.emit()
    return nc, dbg_out



import numpy as np
import ml_dtypes

def perm_idx():
    p = np.arange(8192)
    return 4 * (p % 2048) + p // 2048

def make_consts():
    i_, j_ = np.mgrid[0:128, 0:128]
    m_prev = (j_ >= i_ + 64).astype(np.float32)
    m_diag = ((i_ - j_ >= 0) & (i_ - j_ <= 64)).astype(np.float32)
    caus = np.where(j_ <= i_, 0.0, -1.0e30).astype(np.float32)
    z = np.zeros((128, 128), np.float32)
    negf = np.full((128, 128), -1.0e30, np.float32)
    band3 = {
        0: np.concatenate([z, m_prev, m_diag], axis=1),
        1: np.concatenate([m_prev, m_diag, z], axis=1),
    }
    caus2 = {
        0: np.concatenate([z, caus], axis=1),
        1: np.concatenate([caus, negf], axis=1),
    }
    ident = np.eye(128, dtype=ml_dtypes.bfloat16)
    ones_f = np.ones((1, 128), np.float32)
    extras = {
        "ones_b": np.ones((1, 128), ml_dtypes.bfloat16),
        "identf": np.eye(128, dtype=np.float32),
        "ident16": np.eye(16, dtype=np.float32),
    }
    return band3, caus2, ident, ones_f, extras

def own_tiles(role):
    if role == 0:
        return [31 - 2 * k for k in range(16)]
    return [30 - 2 * k for k in range(16)]

def core_inputs(core, x, Wq, Wk, Wv, q2l, vup, Wo, wg, wu, wd):
    b, role = core // 2, core % 2
    band3, caus2, ident, ones_f, extras = make_consts()
    idx = perm_idx()
    tiles = own_tiles(role)
    xs = np.ascontiguousarray(x[b])
    xo = np.ascontiguousarray(
        xs.reshape(32, 128, 2048)[tiles].reshape(2048, 2048))
    return {
        "x_seq": xs, "x_own": xo,
        "Wq": Wq, "Wk": Wk, "Wv": Wv, "q2l": q2l, "vup": vup, "Wo": Wo,
        "wg_p": np.ascontiguousarray(wg[:, idx]),
        "wu_p": np.ascontiguousarray(wu[:, idx]),
        "wd_p": np.ascontiguousarray(wd[idx, :]),
        "ident": ident, "band3": band3[role], "caus2": caus2[role],
        "ones_f": ones_f, **extras,
    }



_PROG_CACHE = {}

def kernel(L0_attn_hidden_states, L0_attn_Wq, L0_attn_Wk, L0_attn_Wv, L0_attn_q_to_latent,
           L0_attn_v_up, L0_attn_Wo, L0_ffn_w_gate, L0_ffn_w_up, L0_ffn_w_down):
    x = np.ascontiguousarray(np.asarray(L0_attn_hidden_states, dtype=np.float32))
    args = (np.ascontiguousarray(np.asarray(L0_attn_Wq, dtype=np.float32)),
            np.ascontiguousarray(np.asarray(L0_attn_Wk, dtype=np.float32)),
            np.ascontiguousarray(np.asarray(L0_attn_Wv, dtype=np.float32)),
            np.ascontiguousarray(np.asarray(L0_attn_q_to_latent, dtype=np.float32)),
            np.ascontiguousarray(np.asarray(L0_attn_v_up, dtype=np.float32)),
            np.ascontiguousarray(np.asarray(L0_attn_Wo, dtype=np.float32)),
            np.ascontiguousarray(np.asarray(L0_ffn_w_gate, dtype=np.float32)),
            np.ascontiguousarray(np.asarray(L0_ffn_w_up, dtype=np.float32)),
            np.ascontiguousarray(np.asarray(L0_ffn_w_down, dtype=np.float32)))
    if "nc" not in _PROG_CACHE:
        nc, _ = build_program(dbg=(), phases=("p0", "p1", "p2", "p3", "p4"))
        _PROG_CACHE["nc"] = nc
    nc = _PROG_CACHE["nc"]
    in_maps = [core_inputs(c, x, *args) for c in range(8)]
    res = run_bass_kernel_spmd(nc, in_maps, core_ids=list(range(8)))
    out = np.empty((4, 4096, 2048), dtype=np.float32)
    for c in range(8):
        b, role = c // 2, c % 2
        o = np.asarray(res.results[c]["out"]).reshape(16, 128, 2048)
        for k, tau in enumerate(own_tiles(role)):
            out[b, tau*128:(tau+1)*128, :] = o[k]
    return out


# version 15
# speedup vs baseline: 1.0373x; 1.0373x over previous
import numpy as np
import ml_dtypes
from concourse.bass_utils import run_bass_kernel_spmd

import numpy as np
import concourse.bass as bass
import concourse.mybir as mybir
from contextlib import ExitStack

F32 = mybir.dt.float32
BF16 = mybir.dt.bfloat16
AF = mybir.ActivationFunctionType
ALU = mybir.AluOpType
AX = mybir.AxisListType

H, D, L, EMB, II, S, TC = 16, 128, 64, 2048, 8192, 4096, 2048
NTILE = 16
NSEQT = 32
NJ = EMB // 128
EXT = [32 - 2 * j for j in range(16)]
NEG = -1.0e30
NPASS = 4
NHEADS = 16
NSLOTS = 16
NMIC = 4
NB = 8
MW = 256


class Prog:
    ENGS = ("t", "v", "s", "g", "y")

    def __init__(self, nc, stk):
        self.nc, self.stk = nc, stk
        self.q = {e: [] for e in self.ENGS}
        self.sems, self.cnt = {}, {}

    def sem(self, name):
        self.sems[name] = self.stk.enter_context(self.nc.semaphore(name))
        self.cnt[name] = 0

    import os
    CHAIN = {"v": "dv", "s": "ac"}
    if os.environ.get("KB_CHAIN_PE"):
        CHAIN = {"v": "dv", "s": "ac", "t": "pe"}

    def op(self, eng, fn, waits=(), inc=None, incby=1):
        waits = list(waits)
        ch = self.CHAIN.get(eng)
        if ch is not None and inc == ch and self.cnt[ch] > 0:
            waits.append((ch, self.cnt[ch]))
        val = None
        if inc is not None:
            self.cnt[inc] += incby
            val = self.cnt[inc]
        sems = self.sems
        wl = tuple((sems[s], v) for s, v in waits)
        s_inc = sems[inc] if inc is not None else None

        def run(e, fn=fn, wl=wl, s_inc=s_inc, incby=incby):
            for s, v in wl:
                e.wait_ge(s, v)
            inst = fn(e)
            if s_inc is not None:
                if inst is None:
                    inst = e.nop()
                inst.then_inc(s_inc, incby)
        self.q[eng].append(run)
        return val

    def c(self, name):
        return self.cnt[name]

    def emit(self):
        nc = self.nc
        with nc.Block() as block:
            @block.tensor
            def _(t):
                for f in self.q["t"]:
                    f(t)

            @block.vector
            def _(v):
                for f in self.q["v"]:
                    f(v)

            @block.scalar
            def _(s):
                for f in self.q["s"]:
                    f(s)

            @block.gpsimd
            def _(g):
                for f in self.q["g"]:
                    f(g)

            @block.sync
            def _(y):
                for f in self.q["y"]:
                    f(y)


def build_program(dbg=(), phases=("p0", "p1", "p2", "p3", "p4")):
    import os
    nc = bass.Bass("TRN2", target_bir_lowering=False,
                   detect_race_conditions=not os.environ.get("KB_NORACE"))
    stk = ExitStack()

    def din(name, shape, dt=F32):
        return nc.dram_tensor(name, shape, dt, kind="ExternalInput").ap()

    x_seq = din("x_seq", [S, EMB])
    x_own = din("x_own", [TC, EMB])
    Wq = din("Wq", [EMB, EMB])
    Wk = din("Wk", [L, EMB])
    Wv = din("Wv", [L, EMB])
    q2l = din("q2l", [H, D, L])
    vup = din("vup", [H, L, D])
    Wo = din("Wo", [EMB, EMB])
    wg_p = din("wg_p", [EMB, II])
    wu_p = din("wu_p", [EMB, II])
    wd_p = din("wd_p", [II, EMB])
    ident = din("ident", [128, 128], BF16)
    band3 = din("band3", [128, 384])
    caus2 = din("caus2", [128, 256])
    ones_f = din("ones_f", [1, 128])
    ones_b = din("ones_b", [1, 128], BF16)
    identf = din("identf", [128, 128])
    ident16 = din("ident16", [16, 16])

    out = nc.dram_tensor("out", [TC, EMB], F32, kind="ExternalOutput").ap()
    g_scr = nc.dram_tensor("g_scr", [TC, EMB], BF16, kind="Internal").ap()
    kp_scr = nc.dram_tensor("kp_scr", [128, S], BF16, kind="Internal").ap()
    bk_scr = nc.dram_tensor("bk_scr", [128, 16 * 384], BF16, kind="Internal").ap()
    wd_bf = nc.dram_tensor("wd_bf", [II, EMB], BF16, kind="Internal").ap()
    dbg_out = {}
    if "kpack" in dbg:
        dbg_out["d_kpack"] = nc.dram_tensor("d_kpack", [128, S], BF16, kind="ExternalOutput").ap()
        dbg_out["d_ksum"] = nc.dram_tensor("d_ksum", [1, S], BF16, kind="ExternalOutput").ap()
        dbg_out["d_xn"] = nc.dram_tensor("d_xn", [128, EMB], BF16, kind="ExternalOutput").ap()
    if "g" in dbg:
        dbg_out["d_g"] = nc.dram_tensor("d_g", [TC, EMB], BF16, kind="ExternalOutput").ap()
    if "q" in dbg:
        dbg_out["d_q"] = nc.dram_tensor("d_q", [128, H * TC], BF16, kind="ExternalOutput").ap()
        dbg_out["d_qsb"] = nc.dram_tensor("d_qsb", [128, 256], F32, kind="ExternalOutput").ap()
        dbg_out["d_negc"] = nc.dram_tensor("d_negc", [128, 256], F32, kind="ExternalOutput").ap()
    if "ctx" in dbg:
        dbg_out["d_ctx"] = nc.dram_tensor("d_ctx", [128, H * TC], BF16, kind="ExternalOutput").ap()

    pstk = stk.enter_context(ExitStack())

    def sb(st, name, shape, dt):
        return st.enter_context(nc.sbuf_tensor(name, shape, dt))

    A1 = sb(pstk, "A1", [128, 32768], BF16)
    A2 = sb(pstk, "A2", [128, 32768], BF16)
    A2f = A2.ap().bitcast(F32)

    ident_sb = sb(pstk, "ident_sb", [128, 128], BF16)
    band3_sb = sb(pstk, "band3_sb", [128, 384], F32)
    caus2_sb = sb(pstk, "caus2_sb", [128, 256], F32)
    ones_sb = sb(pstk, "ones_sb", [1, 128], F32)
    ones_bf_sb = sb(pstk, "ones_bf_sb", [1, 128], BF16)
    identf_sb = sb(pstk, "identf_sb", [128, 128], F32)
    ident16_sb = sb(pstk, "ident16_sb", [16, 16], F32)
    q2lt = sb(pstk, "q2lt", [64, H * 128], BF16)
    q2lrs = sb(pstk, "q2lrs", [128, H], BF16)
    vup_sb = sb(pstk, "vup_sb", [128, H * 128], BF16)
    qsb = sb(pstk, "qsb", [128, 256], F32)
    negc_bc = sb(pstk, "negc_bc", [128, 256], F32)
    sumg = [sb(pstk, f"sumg{i}", [128, 8], F32) for i in range(2)]
    sumtot = [sb(pstk, f"sumtot{i}", [128, 1], F32) for i in range(2)]
    rinv = [sb(pstk, f"rinv{i}", [128, 1], F32) for i in range(2)]
    negc15 = sb(pstk, "negc15", [128, 1], F32)
    cmaxr = sb(pstk, "cmaxr", [1, 16], F32)
    cminr = sb(pstk, "cminr", [1, 16], F32)
    cmax_bc = sb(pstk, "cmax_bc", [128, 16], F32)
    cmin_bc = sb(pstk, "cmin_bc", [128, 16], F32)
    tnc = sb(pstk, "tnc", [128, 16], F32)
    tnc2 = sb(pstk, "tnc2", [128, 16], F32)
    s1 = sb(pstk, "s1", [128, 1], F32)
    s2 = sb(pstk, "s2", [128, 1], F32)
    mvec = sb(pstk, "mvec", [128, 1], F32)
    msq = sb(pstk, "msq", [128, 1], F32)
    varv = sb(pstk, "varv", [128, 1], F32)
    rstd = sb(pstk, "rstd", [128, 1], F32)
    bias_ln = sb(pstk, "bias_ln", [128, 1], F32)

    PS = [pstk.enter_context(nc.psum_tensor(f"ps{i}", [128, 512], F32)) for i in range(8)]
    PSb = [p.ap().bitcast(BF16) for p in PS]

    kstk = stk.enter_context(ExitStack())
    kpack = sb(kstk, "kpack", [128, S], BF16)
    BK_sb = sb(kstk, "BK_sb", [128, 16 * 384], BF16)

    P = Prog(nc, stk)
    for s_ in ("pe", "dv", "ac", "xq0", "xq1", "cq", "qq", "vq", "kvq", "wgq0", "wgq1", "wgq2", "wdq", "oq", "wcq0", "wcq1", "bar"):
        P.sem(s_)

    def pe(fn, waits=()):
        return P.op("t", fn, waits=waits, inc="pe")

    def dv(fn, waits=()):
        return P.op("v", fn, waits=waits, inc="dv")

    def dvr(fn, waits=()):
        t = P.op("v", fn, waits=waits, inc="dv")
        P.op("v", lambda e, t=t: e.wait_ge(P.sems["dv"], t))
        return t

    def ac(fn, waits=()):
        return P.op("s", fn, waits=waits, inc="ac")

    def gq(fn, sem, waits=()):
        return P.op("g", fn, waits=waits, inc=sem, incby=16)

    def yq(fn, sem, waits=()):
        return P.op("y", fn, waits=waits, inc=sem, incby=16)

    def oq(fn, waits=()):
        return P.op("y", fn, waits=waits, inc="oq", incby=16)

    DMA_SEMS = ("xq0", "xq1", "cq", "qq", "vq", "kvq", "wgq0", "wgq1", "wgq2", "wdq", "oq", "wcq0", "wcq1")

    def barrier():
        tgt = P.cnt["bar"] + 5
        tail = {"t": ("pe",), "v": ("dv",), "s": ("ac",), "g": (), "y": DMA_SEMS}
        for e in Prog.ENGS:
            w = tuple((s_, P.c(s_)) for s_ in tail[e])
            P.op(e, lambda en: en.nop(), waits=w, inc="bar")
        for e in Prog.ENGS:
            P.op(e, lambda en, tgt=tgt: en.wait_ge(P.sems["bar"], tgt))

    yq(lambda e: e.dma_start(out=ident_sb[:], in_=ident), "cq")
    yq(lambda e: e.dma_start(out=band3_sb[:], in_=band3), "cq")
    yq(lambda e: e.dma_start(out=caus2_sb[:], in_=caus2), "cq")
    yq(lambda e: e.dma_start(out=ones_sb[:], in_=ones_f), "cq")
    yq(lambda e: e.dma_start(out=ones_bf_sb[:], in_=ones_b), "cq")
    yq(lambda e: e.dma_start(out=identf_sb[:], in_=identf), "cq")
    w_const = yq(lambda e: e.dma_start(out=ident16_sb[:], in_=ident16), "cq")

    p0stk = stk.enter_context(ExitStack())
    ksum = sb(p0stk, "ksum", [1, S], BF16)
    wkte = sb(p0stk, "wkte", [128, NJ * 65], BF16)
    wvt = sb(p0stk, "wvt", [128, NJ * 64], BF16)
    wk_bf = sb(p0stk, "wk_bf", [64, EMB], BF16)
    wv_bf = sb(p0stk, "wv_bf", [64, EMB], BF16)
    wkcs = sb(p0stk, "wkcs", [128, NJ], F32)
    gq(lambda e: e.dma_start(out=wk_bf[:], in_=Wk), "kvq")
    w_kv = gq(lambda e: e.dma_start(out=wv_bf[:], in_=Wv), "kvq")

    ev_prev = {}
    for j in range(NJ):
        b = j % 2
        w = [("kvq", w_kv), ("cq", w_const)] if j == 0 else []
        if j >= 2:
            w.append(("dv", ev_prev[j - 2]))
        tk = pe(lambda e, j=j, b=b: e.transpose(PSb[b][:, 0:64], wk_bf[0:64, j*128:(j+1)*128], ident_sb[0:64, 0:64]),
                waits=tuple(w))
        dv(lambda e, j=j, b=b: e.tensor_copy(wkte[:, j*65:j*65+64], PSb[b][:, 0:64]), waits=(("pe", tk),))
        dvr(lambda e, j=j, b=b: e.tensor_reduce(wkcs[:, j:j+1], PSb[b][:, 0:64], AX.X, ALU.add))
        ev_prev[j] = dv(lambda e, j=j: e.tensor_copy(wkte[:, j*65+64:j*65+65], wkcs[:, j:j+1]))
    for j in range(NJ):
        b = j % 2
        w = [("dv", ev_prev[j - 2])] if j >= 2 else [("dv", ev_prev[NJ - 1])]
        tk = pe(lambda e, j=j, b=b: e.transpose(PSb[b][:, 0:64], wv_bf[0:64, j*128:(j+1)*128], ident_sb[0:64, 0:64]),
                waits=tuple(w))
        ev_prev[j] = dv(lambda e, j=j, b=b: e.tensor_copy(wvt[:, j*64:(j+1)*64], PSb[b][:, 0:64]),
                        waits=(("pe", tk),))

    q2l_f = sb(p0stk, "q2l_f", [128, 64], F32)
    q2l_bf = sb(p0stk, "q2l_bf", [128, 64], BF16)
    rs_f = sb(p0stk, "rs_f", [128, 1], F32)
    last_q2l = ev_prev[NJ - 1]
    d1 = None
    for h in range(H):
        w_l = (("dv", d1),) if d1 is not None else ()
        lw = yq(lambda e, h=h: e.dma_start(out=q2l_f[:], in_=q2l[h]), "qq", waits=w_l)
        d1 = dv(lambda e: e.tensor_copy(q2l_bf[:], q2l_f[:]), waits=(("qq", lw), ("dv", last_q2l)))
        dvr(lambda e: e.tensor_reduce(rs_f[:], q2l_bf[:], AX.X, ALU.add))
        d2 = dv(lambda e, h=h: e.tensor_copy(q2lrs[:, h:h+1], rs_f[:]))
        tk = pe(lambda e, h=h: e.transpose(PSb[h % 2][0:64, 0:128], q2l_bf[:], ident_sb[:]), waits=(("dv", d2),))
        last_q2l = dv(lambda e, h=h: e.tensor_copy(q2lt[0:64, h*128:(h+1)*128], PSb[h % 2][0:64, 0:128]),
                      waits=(("pe", tk),))
    gq(lambda e: e.dma_start(out=vup_sb.ap()[64:128, :].rearrange("l (h d) -> l h d", d=128), in_=vup.rearrange("h l d -> l h d")), "vq")

    xstage = [sb(p0stk, "xs0", [128, EMB], F32), sb(p0stk, "xs1", [128, EMB], F32)]
    xn_st2 = [A2.ap()[:, 2048:4096], A2.ap()[:, 4096:6144]]
    xnfm2 = [A2.ap()[:, 6144:8192], A2.ap()[:, 8192:10240]]
    trash = A2.ap()[:, 0:2048]
    LNT = [{n: sb(p0stk, f"ln_{n}{k}", [128, 1], F32)
            for n in ("s1", "s2", "mvec", "msq", "varv", "rstd", "bias")} for k in range(2)]

    def ln_tile(src_ap, buf, prev):
        T = LNT[buf]
        xn_dst = xn_st2[buf]
        w_load = []
        if prev.get("axn2"):
            w_load.append(("ac", prev["axn2"]))
        if prev.get("s1red2"):
            w_load.append(("dv", prev["s1red2"]))
        xsem = "xq0" if buf == 0 else "xq1"
        lw = yq(lambda e, src_ap=src_ap, buf=buf: e.dma_start(out=xstage[buf][:], in_=src_ap),
                xsem, waits=tuple(w_load))
        s1red = dvr(lambda e, buf=buf, T=T: e.tensor_reduce(T["s1"][:], xstage[buf][:], AX.X, ALU.add),
                    waits=((xsem, lw),))
        w_sq = [(xsem, lw)]
        if prev.get("msq2"):
            w_sq.append(("dv", prev["msq2"]))
        a_sq = ac(lambda e, buf=buf, T=T: e.activation(trash[:], xstage[buf][:], AF.Square,
                                                       accum_out=T["s2"][:]),
                  waits=tuple(w_sq))
        dv(lambda e, T=T: e.tensor_scalar(T["mvec"][:], T["s1"][:], 1.0 / EMB, None, ALU.mult))
        msq_t = dv(lambda e, T=T: e.tensor_scalar(T["msq"][:], T["s2"][:], 1.0 / EMB, None, ALU.mult),
                   waits=(("ac", a_sq),))
        w_tt = [("ac", prev["a22"])] if prev.get("a22") else []
        dv(lambda e, T=T: e.tensor_tensor(T["varv"][:], T["mvec"][:], T["mvec"][:], ALU.mult),
           waits=tuple(w_tt))
        d_v = dv(lambda e, T=T: e.scalar_tensor_tensor(T["varv"][:], T["msq"][:], 1.0e-5, T["varv"][:],
                                                       ALU.add, ALU.subtract))
        ac(lambda e, T=T: e.activation(T["varv"][:], T["varv"][:], AF.Ln), waits=(("dv", d_v),))
        a2_ = ac(lambda e, T=T: e.activation(T["rstd"][:], T["varv"][:], AF.Exp, scale=-0.5))
        d2_ = dv(lambda e, T=T: e.scalar_tensor_tensor(T["bias"][:], T["mvec"][:], -1.0, T["rstd"][:],
                                                       ALU.mult, ALU.mult),
                 waits=(("ac", a2_),))
        w_xn = [("dv", d2_)]
        if prev.get("lastT2"):
            w_xn.append(("pe", prev["lastT2"]))
        axn = ac(lambda e, buf=buf, T=T, xn_dst=xn_dst: e.activation(
            xn_dst[:], xstage[buf][:], AF.Identity, bias=T["bias"][:], scale=T["rstd"][:]),
            waits=tuple(w_xn))
        return {"axn": axn, "s1red": s1red, "msq": msq_t, "a2": a2_, "lw": lw}

    if "p0" in phases:
        hist = []
        kv_prev = None
        ptguard = {}
        PTB = (2, 3, 6, 7)
        wdstage = [A2.ap()[:, 16384:18432], A2.ap()[:, 18432:20480]]
        wd_rr = wd_p.rearrange("(c p) e -> c p e", p=128)
        wd_st = {}

        def prev_of(i):
            p = {}
            if i >= 2:
                p["axn2"] = hist[i-2]["axn"]
                p["s1red2"] = hist[i-2]["s1red"]
                p["msq2"] = hist[i-2]["msq"]
                p["a22"] = hist[i-2]["a2"]
                p["lastT2"] = hist[i-2]["lastT"]
            return p

        def emit_ptkv(i):
            nonlocal kv_prev
            buf = i % 2
            tks = hist[i]
            axn = tks["axn"]
            seq = i < NSEQT
            it2 = i - NSEQT
            evs = []
            tk = None
            for j in range(NJ):
                b = PTB[j % 4]
                w = list(ptguard.get(j % 4, []))
                if j < 4:
                    w.append(("ac", axn))
                tk = pe(lambda e, j=j, b=b, buf=buf: e.transpose(
                    PSb[b][:, 0:128], xn_st2[buf][:, j*128:(j+1)*128], ident_sb[:]), waits=tuple(w))
                if seq:
                    ev = dv(lambda e, j=j, b=b, buf=buf: e.tensor_copy(
                        xnfm2[buf][:, j*128:(j+1)*128], PSb[b][:, 0:128]), waits=(("pe", tk),))
                else:
                    ev = dv(lambda e, j=j, b=b, it2=it2: e.tensor_copy(
                        A1[:, j*2048 + it2*128: j*2048 + (it2+1)*128], PSb[b][:, 0:128]),
                        waits=(("pe", tk),))
                ptguard[j % 4] = [("dv", ev)]
                evs.append(ev)
            tks["lastT"] = tk
            if seq:
                for j in range(NJ):
                    w = [("dv", evs[j])]
                    if j == 0 and kv_prev is not None:
                        w.append(("dv", kv_prev))
                    pe(lambda e, j=j, buf=buf: e.matmul(PS[4][0:65, 0:128], wkte[:, j*65:(j+1)*65],
                                                        xnfm2[buf][:, j*128:(j+1)*128],
                                                        start=(j == 0), stop=(j == NJ - 1)),
                       waits=tuple(w))
                    kv_t = pe(lambda e, j=j, buf=buf: e.matmul(PS[5][0:64, 0:128], wvt[:, j*64:(j+1)*64],
                                                               xnfm2[buf][:, j*128:(j+1)*128],
                                                               start=(j == 0), stop=(j == NJ - 1)))
                dv(lambda e, i=i: e.tensor_copy(kpack[0:64, i*128:(i+1)*128], PS[4][0:64, 0:128]),
                   waits=(("pe", kv_t),))
                dv(lambda e, i=i: e.tensor_copy(ksum[0:1, i*128:(i+1)*128], PS[4][64:65, 0:128]))
                kv_prev = dv(lambda e, i=i: e.tensor_copy(kpack[64:128, i*128:(i+1)*128], PS[5][0:64, 0:128]))

        def emit_cmax_bk():
            for j in range(16):
                w0 = max(0, EXT[j] - 3) * 128
                w1 = EXT[j] * 128
                dvr(lambda e, j=j, w0=w0, w1=w1: e.tensor_reduce(cmaxr[:, j:j+1], ksum[:, w0:w1], AX.X, ALU.max))
                dvr(lambda e, j=j, w0=w0, w1=w1: e.tensor_reduce(cminr[:, j:j+1], ksum[:, w0:w1], AX.X, ALU.min))
            d_cm = P.c("dv")
            pe(lambda e: e.matmul(PS[0][0:128, 0:16], ones_sb[:], cmaxr[:, 0:16], start=True, stop=True),
               waits=(("dv", d_cm),))
            t2 = pe(lambda e: e.matmul(PS[1][0:128, 0:16], ones_sb[:], cminr[:, 0:16], start=True, stop=True))
            dv(lambda e: e.tensor_copy(cmax_bc[:], PS[0][0:128, 0:16]), waits=(("pe", t2),))
            e2 = dv(lambda e: e.tensor_copy(cmin_bc[:], PS[1][0:128, 0:16]))
            g0 = {("bk", 0): [("dv", e2)], ("bk", 1): [("dv", e2)]}
            for j in range(16):
                e0 = max(0, EXT[j] - 3)
                nwc = EXT[j] - e0
                woff = (3 - nwc) * 128
                for wci in range(nwc):
                    w = g0.pop(("bk", wci % 2), [])
                    if j == 0 and wci == 0:
                        w += [("cq", P.c("cq"))]
                    bk_mm = pe(lambda e, j=j, wci=wci, e0=e0: e.matmul(
                        PS[0][0:128, 0:128] if wci % 2 == 0 else PS[1][0:128, 0:128],
                        ones_bf_sb[:], ksum[:, (e0+wci)*128:(e0+wci+1)*128],
                        start=True, stop=True), waits=tuple(w))
                    d_bk = dv(lambda e, j=j, wci=wci, woff=woff: e.tensor_tensor(
                        BK_sb[:, j*384 + woff + wci*128: j*384 + woff + (wci+1)*128],
                        band3_sb[:, woff + wci*128: woff + (wci+1)*128],
                        (PS[0][0:128, 0:128] if wci % 2 == 0 else PS[1][0:128, 0:128]), ALU.mult),
                        waits=(("pe", bk_mm),))
                    g0[("bk", wci % 2)] = [("dv", d_bk)]
                if woff > 0:
                    dv(lambda e, j=j, woff=woff: e.tensor_scalar(
                        BK_sb[:, j*384: j*384 + woff], band3_sb[:, 0:woff], 0.0, None, ALU.mult))
            d_sp0 = P.c("dv")
            oq(lambda e: e.dma_start(out=kp_scr, in_=kpack[:]), waits=(("dv", d_sp0),))
            oq(lambda e: e.dma_start(out=bk_scr, in_=BK_sb[:]))

        pend = None
        for i in range(NSEQT + NTILE):
            buf = i % 2
            src = x_seq[i*128:(i+1)*128, :] if i < NSEQT else \
                x_own[(i-NSEQT)*128:(i-NSEQT+1)*128, :]
            tks = ln_tile(src, buf, prev_of(i))
            hist.append(tks)
            if pend is not None:
                emit_ptkv(pend)
            if i == NSEQT:
                emit_cmax_bk()
            if i < NSEQT:
                for rr in range(2):
                    row = i * 2 + rr
                    wsem = "wdq" if row % 2 == 0 else "vq"
                    ssem = "wcq0" if row % 2 == 0 else "wcq1"
                    w_l = [(ssem, wd_st[row % 2])] if row >= 2 else []
                    wl_t = gq(lambda e, row=row: e.dma_start(out=wdstage[row % 2][:], in_=wd_rr[row]),
                              wsem, waits=tuple(w_l))
                    wd_st[row % 2] = yq(lambda e, row=row: e.dma_start(
                        out=wd_bf[row*128:(row+1)*128, :], in_=wdstage[row % 2][:]),
                        ssem, waits=((wsem, wl_t),))
            pend = i
        emit_ptkv(pend)

        if "kpack" in dbg:
            dq = P.c("dv")
            oq(lambda e: e.dma_start(out=dbg_out["d_kpack"], in_=kpack[:]), waits=(("dv", dq),))
            oq(lambda e: e.dma_start(out=dbg_out["d_ksum"], in_=ksum[:]))
            oq(lambda e: e.dma_start(out=dbg_out["d_xn"], in_=A1[:, 0:2048]))


    barrier()
    p0stk.close()
    kstk.close()

    if "p1" in phases:
        p1stk = stk.enter_context(ExitStack())
        acc = A2f
        tb = 16384
        h_all = [A2.ap()[:, tb:tb + 4096], A2.ap()[:, tb + 4096:tb + 8192]]
        habs = A2.ap()[:, tb + 8192:tb + 9216]
        hsp = A2.ap()[:, tb + 9216:tb + 10240]
        hsT = [A2.ap()[:, tb + 10240:tb + 11264], A2.ap()[:, tb + 11264:tb + 12288]]
        spt = [A2.ap()[:, tb + 12288 + k*256: tb + 12288 + (k+1)*256] for k in range(3)]
        gstage = A2.ap()[:, tb + 13056:tb + 15104]
        tmpg = [sb(p1stk, f"tmpg{i}", [128, 128], F32) for i in range(2)]
        SLOTS = 3
        gsems = ("wgq0", "wgq1", "wgq2")
        wgbuf = [sb(p1stk, f"wgb{i}", [128, NJ * 128], BF16) for i in range(SLOTS)]
        wubuf = [sb(p1stk, f"wub{i}", [128, NJ * 128], BF16) for i in range(SLOTS)]
        wdbuf = [sb(p1stk, f"wdb{i}", [128, EMB], BF16) for i in range(8)]

        wg_r = wg_p.rearrange("(j p) i -> p j i", p=128)
        wu_r = wu_p.rearrange("(j p) i -> p j i", p=128)

        NBAT = NPASS * NB
        pend_gu = {}
        guards = {}
        Z = {"prev_sp": None, "prev_tev": None, "h_done": {}, "sp_done": {},
             "pt_done": {}, "gadd": {}, "pair_entry": {}, "wd_cur": None,
             "dn_last": None, "a_g": {}, "og_last": None, "tmpg_last": {}}

        def emit_sp(bp, mi):
            hb = h_all[bp % 2]
            w_ab = [("dv", Z["h_done"][bp % 2])]
            if Z["prev_sp"] is not None:
                w_ab.append(("dv", Z["prev_sp"]))
            a_ab = ac(lambda e, hb=hb, mi=mi: e.activation(habs[:], hb[:, mi*1024:(mi+1)*1024], AF.Abs),
                      waits=tuple(w_ab))
            avs = [habs[:, m_*256:(m_+1)*256] for m_ in range(4)]
            hvs = [hb[:, mi*1024 + m_*256: mi*1024 + (m_+1)*256] for m_ in range(4)]
            dv(lambda e, avs=avs: e.tensor_tensor(spt[0][:], avs[0], avs[1], ALU.min), waits=(("ac", a_ab),))
            dv(lambda e, avs=avs: e.tensor_tensor(spt[1][:], avs[0], avs[1], ALU.max))
            dv(lambda e, avs=avs: e.tensor_tensor(spt[2][:], avs[2], avs[3], ALU.min))
            dv(lambda e: e.tensor_tensor(spt[0][:], spt[0][:], spt[2][:], ALU.max))
            dv(lambda e, avs=avs: e.tensor_tensor(spt[2][:], avs[2], avs[3], ALU.max))
            dv(lambda e: e.tensor_tensor(spt[1][:], spt[1][:], spt[2][:], ALU.min))
            dv(lambda e: e.tensor_tensor(spt[0][:], spt[0][:], spt[1][:], ALU.max))
            sp_d = None
            for m_ in range(4):
                dv(lambda e, a=avs[m_]: e.tensor_tensor(spt[1][:], a, spt[0][:], ALU.is_ge))
                w_sp = ()
                if m_ == 0 and Z["prev_tev"] is not None:
                    w_sp = (("pe", Z["prev_tev"]),)
                sp_d = dv(lambda e, hh=hvs[m_], m_=m_: e.tensor_tensor(hsp[:, m_*256:(m_+1)*256],
                                                                       hh, spt[1][:], ALU.mult), waits=w_sp)
            Z["prev_sp"] = sp_d
            Z["sp_done"][mi] = sp_d

        def emit_PT(bp, mi):
            pair = (4, 5) if mi % 2 == 0 else (6, 7)
            pte = {}
            tk = None
            for cb in range(8):
                bnk = pair[cb % 2]
                w = []
                if cb < 2:
                    w.append(("dv", Z["sp_done"][mi]))
                    w += Z["pair_entry"].pop((pair, cb % 2), [])
                else:
                    w.append(("dv", pte[cb - 2]))
                tk = pe(lambda e, cb=cb, bnk=bnk: e.transpose(PSb[bnk][:, 0:128],
                                                              hsp[:, cb*128:(cb+1)*128], ident_sb[:]),
                        waits=tuple(w))
                pte[cb] = dv(lambda e, cb=cb, bnk=bnk, mi=mi: e.tensor_copy(
                    hsT[mi % 2][:, cb*128:(cb+1)*128], PSb[bnk][:, 0:128]), waits=(("pe", tk),))
            Z["prev_tev"] = tk
            Z["pt_done"][mi] = pte[7]

        def emit_dn(bp, mi, setk):
            pair = (4, 5) if mi % 2 == 0 else (6, 7)
            last = None
            for gsub in range(4):
                for ci in range(8):
                    for pi in range(2):
                        bnk = pair[pi]
                        ch = setk*8 + pi*4 + gsub
                        w = []
                        if gsub == 0 and ci == 0:
                            if setk == 0:
                                w.append(("dv", Z["pt_done"][mi]))
                            else:
                                w.append(("dv", Z["gadd"][(mi, 0, bnk)]))
                        if mi == 0 and setk == 0 and gsub == 0 and ci == 0 and pi == 0:
                            w.append(("wcq0", Z["wd_cur"]))
                        last = pe(lambda e, mi=mi, ci=ci, bnk=bnk, ch=ch, gsub=gsub:
                                  e.matmul(PS[bnk][:, gsub*128:(gsub+1)*128],
                                           hsT[mi % 2][:, ci*128:(ci+1)*128],
                                           wdbuf[ci][:, ch*128:(ch+1)*128],
                                           start=(ci == 0), stop=(ci == 7)), waits=tuple(w))
            Z["dn_last"] = last

        def emit_gadd(bp, mi, setk):
            pair = (4, 5) if mi % 2 == 0 else (6, 7)
            b_ = bp % NB
            ticks = {}
            for pi in range(2):
                bnk = pair[pi]
                e0 = setk*1024 + pi*512
                w = []
                if pi == 0:
                    w.append(("pe", Z["dn_last"]))
                if b_ == 0 and setk == 0 and pi == 0 and (mi in Z["a_g"]):
                    w.append(("ac", Z["a_g"][mi]))
                if b_ == 0:
                    t = dv(lambda e, mi=mi, e0=e0, bnk=bnk: e.tensor_copy(
                        acc[:, mi*2048 + e0: mi*2048 + e0 + 512], PS[bnk][:, 0:512]), waits=tuple(w))
                else:
                    t = dv(lambda e, mi=mi, e0=e0, bnk=bnk: e.tensor_tensor(
                        acc[:, mi*2048 + e0: mi*2048 + e0 + 512],
                        acc[:, mi*2048 + e0: mi*2048 + e0 + 512], PS[bnk][:, 0:512], ALU.add),
                        waits=tuple(w))
                Z["gadd"][(mi, setk, bnk)] = t
                ticks[pi] = t
            if setk == 1:
                for pi in range(2):
                    Z["pair_entry"][(pair, pi)] = [("dv", ticks[pi])]

        def emit_spill(p_i):
            for mi in range(NMIC):
                mt = p_i * NMIC + mi
                pair = (4, 5) if mi % 2 == 0 else (6, 7)
                w = [("dv", Z["gadd"][(mi, 1, pair[1])])]
                if Z["og_last"] is not None:
                    w.append(("oq", Z["og_last"]))
                a_g = ac(lambda e, mi=mi: e.activation(gstage[:], acc[:, mi*2048:(mi+1)*2048], AF.Silu),
                         waits=tuple(w))
                og = oq(lambda e, mt=mt: e.dma_start(out=g_scr[mt*128:(mt+1)*128, :], in_=gstage[:]),
                        waits=(("ac", a_g),))
                Z["a_g"][mi] = a_g
                Z["og_last"] = og

        def emit_down_pos(bp, s):
            if s == 0:
                emit_sp(bp, 0); emit_PT(bp, 0); emit_sp(bp, 1)
            elif s == 1:
                emit_dn(bp, 0, 0); emit_gadd(bp, 0, 0); emit_PT(bp, 1); emit_sp(bp, 2)
            elif s == 2:
                emit_dn(bp, 0, 1); emit_gadd(bp, 0, 1)
                emit_dn(bp, 1, 0); emit_gadd(bp, 1, 0)
            elif s == 3:
                emit_PT(bp, 2); emit_sp(bp, 3); emit_dn(bp, 1, 1); emit_gadd(bp, 1, 1)
            elif s == 4:
                emit_dn(bp, 2, 0); emit_gadd(bp, 2, 0); emit_PT(bp, 3)
            elif s == 5:
                emit_dn(bp, 2, 1); emit_gadd(bp, 2, 1)
                emit_dn(bp, 3, 0); emit_gadd(bp, 3, 0)
            elif s == 6:
                emit_dn(bp, 3, 1); emit_gadd(bp, 3, 1)
                if bp % NB == NB - 1:
                    emit_spill(bp // NB)

        DPRV = None
        slice_ctr = 0
        for bat in range(NBAT):
            p_i, b = divmod(bat, NB)
            m0 = p_i * NMIC
            hb = h_all[bat % 2]
            last_h = None
            for s in range(8):
                m, c = s // 2, s % 2
                slot = slice_ctr % SLOTS
                gsem = gsems[slot]
                c0 = m*2048 + 256*b + c*128
                w = [("pe", pend_gu[slot])] if slot in pend_gu else []
                gq(lambda e, slot=slot, c0=c0: e.dma_start(
                    out=wgbuf[slot].ap().rearrange("p (j i) -> p j i", i=128),
                    in_=wg_r[:, :, c0:c0+128]), gsem, waits=tuple(w))
                lt = gq(lambda e, slot=slot, c0=c0: e.dma_start(
                    out=wubuf[slot].ap().rearrange("p (j i) -> p j i", i=128),
                    in_=wu_r[:, :, c0:c0+128]), gsem)
                for mi in range(NMIC):
                    mt = m0 + mi
                    gbank = (mi % 2) * 2
                    ubank = gbank + 1
                    for j in range(NJ):
                        w = []
                        if j == 0:
                            if mi == 0:
                                w.append((gsem, lt))
                            w += guards.pop((gbank, c), [])
                        pe(lambda e, c=c, j=j, mt=mt, slot=slot, gbank=gbank:
                           e.matmul(PS[gbank][:, c*128:(c+1)*128],
                                    A1[:, j*2048 + mt*128: j*2048 + (mt+1)*128],
                                    wgbuf[slot][:, j*128:(j+1)*128],
                                    start=(j == 0), stop=(j == NJ - 1)), waits=tuple(w))
                        w2 = tuple(guards.pop((ubank, c), [])) if j == 0 else ()
                        gu_mm = pe(lambda e, c=c, j=j, mt=mt, slot=slot, ubank=ubank:
                                   e.matmul(PS[ubank][:, c*128:(c+1)*128],
                                            A1[:, j*2048 + mt*128: j*2048 + (mt+1)*128],
                                            wubuf[slot][:, j*128:(j+1)*128],
                                            start=(j == 0), stop=(j == NJ - 1)), waits=w2)
                    w_si = [("pe", gu_mm)]
                    if (mi % 2) in Z["tmpg_last"]:
                        w_si.append(("dv", Z["tmpg_last"][mi % 2]))
                    a_si = ac(lambda e, c=c, gbank=gbank, mi=mi: e.activation(
                        tmpg[mi % 2][:], PS[gbank][:, c*128:(c+1)*128], AF.Silu), waits=tuple(w_si))
                    hslice = hb[:, mi*1024 + m*256 + c*128: mi*1024 + m*256 + (c+1)*128]
                    last_h = dv(lambda e, hslice=hslice, c=c, ubank=ubank, mi=mi:
                                e.tensor_tensor(hslice, tmpg[mi % 2][:],
                                                PS[ubank][:, c*128:(c+1)*128], ALU.mult),
                                waits=(("ac", a_si),))
                    Z["tmpg_last"][mi % 2] = last_h
                    guards[(gbank, c)] = [("ac", a_si)]
                    guards[(ubank, c)] = [("dv", last_h)]
                pend_gu[slot] = gu_mm
                slice_ctr += 1
                if DPRV is not None:
                    emit_down_pos(DPRV, s)
                if s == 6:
                    wt = None
                    for ci in range(8):
                        row = 16 * (ci // 2) + 2 * b + (ci % 2)
                        w = [("pe", Z["dn_last"])] if Z["dn_last"] is not None else []
                        wt = yq(lambda e, ci=ci, row=row: e.dma_start(
                            out=wdbuf[ci][:], in_=wd_bf[row*128:(row+1)*128, :]),
                            "wcq0", waits=tuple(w))
                    Z["wd_cur"] = wt
            Z["h_done"][bat % 2] = last_h
            DPRV = bat
        for s in range(8):
            emit_down_pos(DPRV, s)

        if "g" in dbg:
            ov = P.c("oq")
            for mt in range(NTILE):
                lw = yq(lambda e, mt=mt: e.dma_start(out=gstage[:], in_=g_scr[mt*128:(mt+1)*128, :]),
                        "xq0", waits=(("oq", ov),))
                ov = oq(lambda e, mt=mt: e.dma_start(out=dbg_out["d_g"][mt*128:(mt+1)*128, :], in_=gstage[:]),
                        waits=(("xq0", lw),))
        barrier()
        p1stk.close()

    if ("p2" in phases) or ("p3" in phases):
        k2stk = stk.enter_context(ExitStack())
        kp3 = sb(k2stk, "kpack2", [128, S], BF16)
        BK3 = sb(k2stk, "BK_sb2", [128, 16 * 384], BF16)
        yq(lambda e: e.dma_start(out=kp3[:], in_=kp_scr), "cq")
        rl2 = yq(lambda e: e.dma_start(out=BK3[:], in_=bk_scr), "cq")

    if "p2" in phases:
        p2stk = stk.enter_context(ExitStack())
        wqh = [sb(p2stk, f"wqh{i}", [128, EMB], BF16) for i in range(2)]
        wqT = sb(p2stk, "wqT", [128, EMB], BF16)
        qs_col = sb(p2stk, "qs_col", [128, 16], F32)
        qsT = sb(p2stk, "qsT", [16, 128], F32)
        qtmax = sb(p2stk, "qtmax", [16, 1], F32)
        qtmin = sb(p2stk, "qtmin", [16, 1], F32)
        tb1 = sb(p2stk, "tb1", [16, 1], F32)
        tb2 = sb(p2stk, "tb2", [16, 1], F32)
        negc_row = sb(p2stk, "negc_row", [1, 16], F32)
        guards2 = {}
        pend_wq = {}
        pend_wqT = None
        for h in range(NHEADS):
            slot = h % 2
            gsem = "wgq0" if slot == 0 else "wgq1"
            w = [("pe", pend_wq[slot])] if slot in pend_wq else []
            lw = gq(lambda e, h=h, slot=slot: e.dma_start(out=wqh[slot][:], in_=Wq[h*128:(h+1)*128, :]),
                    gsem, waits=tuple(w))
            tevq = []
            for j in range(NJ):
                bnk = j % 2
                w = [(gsem, lw)] if j == 0 else []
                w += guards2.pop(("t", bnk), [])
                if pend_wqT is not None and j < 2:
                    w.append(("pe", pend_wqT))
                tk = pe(lambda e, j=j, bnk=bnk, slot=slot: e.transpose(
                    PSb[bnk][:, 0:128], wqh[slot][:, j*128:(j+1)*128], ident_sb[:]), waits=tuple(w))
                tevq.append(dv(lambda e, j=j, bnk=bnk: e.tensor_copy(wqT[:, j*128:(j+1)*128], PSb[bnk][:, 0:128]),
                               waits=(("pe", tk),)))
                guards2[("t", bnk)] = [("dv", tevq[-1])]
            for mp in range(8):
                for j in range(NJ):
                    w = []
                    if j == 0:
                        w += guards2.pop(("q", 2), []) + [("dv", tevq[-1])]
                    pe(lambda e, j=j, mp=mp, h=h: e.matmul(
                        PS[2][:, 0:128], wqT[:, j*128:(j+1)*128],
                        A1[:, j*2048 + (2*mp)*128: j*2048 + (2*mp+1)*128],
                        start=(j == 0), stop=(j == NJ - 1)), waits=tuple(w))
                    w2 = tuple(guards2.pop(("q", 3), [])) if j == 0 else ()
                    q_mm = pe(lambda e, j=j, mp=mp, h=h: e.matmul(
                        PS[3][:, 0:128], wqT[:, j*128:(j+1)*128],
                        A1[:, j*2048 + (2*mp+1)*128: j*2048 + (2*mp+2)*128],
                        start=(j == 0), stop=(j == NJ - 1)), waits=w2)
                ev1 = dv(lambda e, mp=mp, h=h: e.tensor_copy(A2[:, h*2048 + (2*mp)*128: h*2048 + (2*mp+1)*128],
                                                             PS[2][:, 0:128]), waits=(("pe", q_mm),))
                ev2 = dv(lambda e, mp=mp, h=h: e.tensor_copy(A2[:, h*2048 + (2*mp+1)*128: h*2048 + (2*mp+2)*128],
                                                             PS[3][:, 0:128]))
                guards2[("q", 2)] = [("dv", ev1)]
                guards2[("q", 3)] = [("dv", ev2)]
                pend_wqT = q_mm
            pend_wq[slot] = q_mm
            for mt in range(NTILE):
                w = [("dv", P.c("dv"))] if mt == 0 else []
                if mt == 0:
                    w += guards2.pop(("qs",), [])
                qs_mm = pe(lambda e, mt=mt, h=h: e.matmul(
                    PS[4][:, mt:mt+1], A2[:, h*2048 + mt*128: h*2048 + (mt+1)*128],
                    q2lrs[:, h:h+1], start=True, stop=True), waits=tuple(w))
            d_qs = dv(lambda e: e.tensor_copy(qs_col[:], PS[4][:, 0:16]), waits=(("pe", qs_mm),))
            guards2[("qs",)] = [("dv", d_qs)]
            dv(lambda e, h=h: e.tensor_scalar(qsb[:, h*16:(h+1)*16], qs_col[:], 0.0625, None, ALU.mult))
            d_q16 = P.c("dv")
            dv(lambda e, h=h: e.tensor_tensor(tnc[:], qsb[:, h*16:(h+1)*16], cmax_bc[:], ALU.mult))
            dv(lambda e, h=h: e.tensor_tensor(tnc2[:], qsb[:, h*16:(h+1)*16], cmin_bc[:], ALU.mult))
            dv(lambda e: e.tensor_tensor(tnc[:], tnc[:], tnc2[:], ALU.max))
            dv(lambda e, h=h: e.tensor_scalar(negc_bc[:, h*16:(h+1)*16], tnc[:], -1.0, -10.0,
                                              ALU.mult, ALU.add))
        if "q" in dbg:
            dq = P.c("dv")
            oq(lambda e: e.dma_start(out=dbg_out["d_q"][:, 0:NHEADS*2048], in_=A2[:, 0:NHEADS*2048]), waits=(("dv", dq),))
            oq(lambda e: e.dma_start(out=dbg_out["d_qsb"][:, 0:NHEADS*16], in_=qsb[:, 0:NHEADS*16]))
            oq(lambda e: e.dma_start(out=dbg_out["d_negc"][:, 0:NHEADS*16], in_=negc_bc[:, 0:NHEADS*16]))
        barrier()
        p2stk.close()

    if "p3" in phases:
        p3stk = stk.enter_context(ExitStack())
        A_sb = sb(p3stk, "A_sb", [128, S], BF16)
        Vx_sb = sb(p3stk, "Vx_sb", [128, S], BF16)
        probs = [sb(p3stk, f"probs{i}", [128, S], BF16) for i in range(2)]
        pts_all = sb(p3stk, "pts_all", [128, S], BF16)
        ctxt_f = sb(p3stk, "ctxt_f", [128, 128], F32)
        g3 = {("bk", 0): [("cq", rl2)], ("bk", 1): [("cq", rl2)]}
        for h in range(NHEADS):
            for grp in range(8):
                bnk = grp % 2
                for cc in range(4):
                    c = grp * 4 + cc
                    w = g3.pop(("av", bnk), []) if cc == 0 else []
                    if h == 0 and grp < 2 and cc == 0:
                        w += g3.pop(("bk", bnk), [])
                    av_mm = pe(lambda e, h=h, c=c, cc=cc, bnk=bnk: e.matmul(
                        PS[bnk][:, cc*128:(cc+1)*128], q2lt[0:64, h*128:(h+1)*128],
                        kp3[0:64, c*128:(c+1)*128], start=(cc == 0), stop=(cc == 3)), waits=tuple(w))
                d_av = dv(lambda e, grp=grp, bnk=bnk: e.tensor_scalar(
                    A_sb[:, grp*512:(grp+1)*512], PS[bnk][:], 0.125, None, ALU.mult),
                    waits=(("pe", av_mm),))
                g3[("av", bnk)] = [("dv", d_av)]
            for grp in range(8):
                bnk = grp % 2
                for cc in range(4):
                    c = grp * 4 + cc
                    w = g3.pop(("av", bnk), []) if cc == 0 else []
                    av_mm = pe(lambda e, h=h, c=c, cc=cc, bnk=bnk: e.matmul(
                        PS[bnk][:, cc*128:(cc+1)*128], kp3[64:128, c*128:(c+1)*128],
                        vup_sb[64:128, h*128:(h+1)*128], start=(cc == 0), stop=(cc == 3)), waits=tuple(w))
                d_av = dv(lambda e, grp=grp, bnk=bnk: e.tensor_copy(
                    Vx_sb[:, grp*512:(grp+1)*512], PS[bnk][:]), waits=(("pe", av_mm),))
                g3[("av", bnk)] = [("dv", d_av)]
            def emit_scores_mm(h, j):
                E = EXT[j]
                ngrp = (E + 3) // 4
                pp = j % 2
                last_ex = None
                for grp in range(ngrp):
                    bnk = 2 + grp % 2
                    nc_in_g = min(4, E - grp * 4)
                    w = g3.pop(("sc", bnk), [])
                    if grp == 0:
                        w += g3.pop(("ptw", pp), [])
                        w += g3.pop(("sg", pp), [])
                    for cc in range(nc_in_g):
                        c = grp * 4 + cc
                        sc_mm = pe(lambda e, h=h, j=j, c=c, cc=cc, bnk=bnk, nc_in_g=nc_in_g: e.matmul(
                            PS[bnk][:, cc*128:(cc+1)*128],
                            A2[:, h*2048 + j*128:(h*2048 + (j+1)*128)],
                            A_sb[:, c*128:(c+1)*128], start=(cc == 0), stop=(cc == nc_in_g - 1)),
                            waits=tuple(w) if cc == 0 else ())
                        w = []
                    lo_g, hi_g = grp * 4, grp * 4 + nc_in_g
                    e0 = max(0, E - 3)
                    blo, bhi = max(lo_g, e0), hi_g
                    dvt = None
                    if blo < bhi:
                        woff3 = (3 - (E - e0)) * 128
                        dvt = dv(lambda e, h=h, j=j, blo=blo, bhi=bhi, lo_g=lo_g, e0=e0, woff3=woff3, grp=grp:
                                 e.scalar_tensor_tensor(
                                     PS[2 + grp % 2][:, (blo-lo_g)*128:(bhi-lo_g)*128],
                                     BK3[:, j*384 + woff3 + (blo-e0)*128: j*384 + woff3 + (bhi-e0)*128],
                                     qsb[:, h*16+j:h*16+j+1],
                                     PS[2 + grp % 2][:, (blo-lo_g)*128:(bhi-lo_g)*128],
                                     ALU.mult, ALU.add), waits=(("pe", sc_mm),))
                    if hi_g == E:
                        dvt = dv(lambda e, j=j, nc_in_g=nc_in_g, grp=grp: e.tensor_tensor(
                            PS[2 + grp % 2][:, (nc_in_g-2)*128: nc_in_g*128],
                            PS[2 + grp % 2][:, (nc_in_g-2)*128: nc_in_g*128],
                            caus2_sb[:], ALU.add),
                            waits=(("pe", sc_mm),) if dvt is None else ())
                    if j == 15:
                        dvt = dvr(lambda e, nc_in_g=nc_in_g, grp=grp: e.tensor_reduce(
                            negc15[:], PS[2 + grp % 2][:, 0:nc_in_g*128], AX.X, ALU.max),
                            waits=(("dv", dvt),) if dvt is not None else (("pe", sc_mm),))
                        dvt = dv(lambda e: e.tensor_scalar(negc15[:], negc15[:], -1.0, None, ALU.mult))
                        bias_ap = negc15
                    else:
                        bias_ap = None
                    w_exp = [("pe", sc_mm)] if dvt is None else [("dv", dvt)]
                    a_ex = ac(lambda e, h=h, j=j, grp=grp, nc_in_g=nc_in_g, bias_ap=bias_ap, pp=pp:
                              e.activation(
                                  probs[pp][:, grp*512: grp*512 + nc_in_g*128],
                                  PS[2 + grp % 2][:, 0:nc_in_g*128],
                                  AF.Exp, bias=(bias_ap[:] if bias_ap is not None else
                                                negc_bc[:, h*16+j:h*16+j+1]),
                                  accum_out=sumg[pp][:, grp:grp+1]),
                              waits=tuple(w_exp))
                    last_ex = a_ex
                    g3[("sc", bnk)] = [("ac", a_ex)] + ([("dv", dvt)] if dvt is not None else [])
                return last_ex

            def emit_rtail(h, j, a_last):
                E = EXT[j]
                ngrp = (E + 3) // 4
                pp = j % 2
                rd = dvr(lambda e, ngrp=ngrp, pp=pp: e.tensor_reduce(sumtot[pp][:], sumg[pp][:, 0:ngrp],
                                                                     AX.X, ALU.add),
                         waits=(("ac", a_last),))
                g3[("sg", pp)] = [("dv", rd)]
                ac(lambda e, pp=pp: e.activation(sumtot[pp][:], sumtot[pp][:], AF.Ln),
                   waits=(("dv", rd),))
                w_r = g3.pop(("rctx", pp), [])
                a_r = ac(lambda e, pp=pp: e.activation(rinv[pp][:], sumtot[pp][:], AF.Exp, scale=-1.0),
                         waits=tuple(w_r))
                g3[("rinv", pp)] = a_r

            def emit_pt(h, j, a_last):
                E = EXT[j]
                pp = j % 2
                last_tk = None
                d_pt = None
                for c in range(E):
                    tbnk = 4 + (c % 4)
                    w = list(g3.get(("pt", c % 4), []))
                    if c < 4:
                        w.append(("ac", a_last))
                    if c in (2, 3):
                        w += g3.get(("rctx", 1 - pp), [])
                    tk = pe(lambda e, c=c, tbnk=tbnk, pp=pp: e.transpose(
                        PSb[tbnk][:, 0:128], probs[pp][:, c*128:(c+1)*128], ident_sb[:]), waits=tuple(w))
                    d_pt = dv(lambda e, c=c, tbnk=tbnk: e.tensor_copy(pts_all[:, c*128:(c+1)*128],
                                                                      PSb[tbnk][:, 0:128]),
                              waits=(("pe", tk),))
                    g3[("pt", c % 4)] = [("dv", d_pt)]
                    last_tk = tk
                g3[("ptw", pp)] = [("pe", last_tk)]
                g3[("ptd", pp)] = d_pt

            def emit_pvmm(h, j):
                E = EXT[j]
                pp = j % 2
                R = 256 + pp * 128
                d_pt = g3.pop(("ptd", pp))
                lastmm = None
                for c in range(E):
                    bnkp = 6 + (c % 2)
                    w = ()
                    if c < 2:
                        w = tuple(g3.pop(("ctx", bnkp, R), []) + [("dv", d_pt)])
                    lastmm = pe(lambda e, c=c, bnkp=bnkp, R=R, E=E: e.matmul(
                        PS[bnkp][:, R:R+128], pts_all[:, c*128:(c+1)*128], Vx_sb[:, c*128:(c+1)*128],
                        start=(c < 2), stop=(c >= E - 2)), waits=w)
                a_r = g3[("rinv", pp)]
                dv(lambda e, R=R, pp=pp: e.tensor_scalar(ctxt_f[:], PS[7][:, R:R+128], rinv[pp][:],
                                                         None, ALU.mult),
                   waits=(("pe", lastmm), ("ac", a_r)))
                d_ctx = dv(lambda e, h=h, j=j, pp=pp, R=R: e.scalar_tensor_tensor(
                    A1[:, h*2048 + j*128: h*2048 + (j+1)*128], PS[6][:, R:R+128], rinv[pp][:],
                    ctxt_f[:], ALU.mult, ALU.add))
                g3[("ctx", 6, R)] = [("dv", d_ctx)]
                g3[("ctx", 7, R)] = [("dv", d_ctx)]
                g3[("rctx", pp)] = [("dv", d_ctx)]

            pend = None
            for j in range(NSLOTS):
                ax_j = emit_scores_mm(h, j)
                if pend is not None:
                    emit_pt(h, pend[0], pend[1])
                emit_rtail(h, j, ax_j)
                if pend is not None:
                    emit_pvmm(h, pend[0])
                pend = (j, ax_j)
            emit_pt(h, pend[0], pend[1])
            emit_pvmm(h, pend[0])

        if "ctx" in dbg:
            dq = P.c("dv")
            oq(lambda e: e.dma_start(out=dbg_out["d_ctx"][:, 0:NHEADS*2048], in_=A1[:, 0:NHEADS*2048]), waits=(("dv", dq),))
        barrier()
        p3stk.close()

    if ("p2" in phases) or ("p3" in phases):
        k2stk.close()

    if "p4" in phases:
        p4stk = stk.enter_context(ExitStack())
        woh = [sb(p4stk, f"woh{i}", [128, EMB], BF16) for i in range(2)]
        gbuf = [sb(p4stk, f"gbuf{i}", [128, EMB], BF16) for i in range(2)]
        xbuf = [sb(p4stk, f"xbuf{i}", [128, EMB], F32) for i in range(2)]
        obuf = [sb(p4stk, f"obuf{i}", [128, EMB], F32) for i in range(2)]
        tmp4 = sb(p4stk, "tmp4", [128, 512], F32)
        cstage = [sb(p4stk, f"cst{i}", [128, EMB], BF16) for i in range(2)]
        g4 = {}
        for ec in range(16):
            slot = ec % 2
            gsem = "wgq0" if slot == 0 else "wgq1"
            w = g4.pop(("wo", slot), [])
            lw = gq(lambda e, ec=ec, slot=slot: e.dma_start(out=woh[slot][:], in_=Wo[ec*128:(ec+1)*128, :]),
                    gsem, waits=tuple(w))
            last = None
            for hh in range(16):
                bnk = hh % 2
                w = [(gsem, lw)] if hh == 0 else []
                w += g4.pop(("t", bnk), [])
                tk = pe(lambda e, hh=hh, ec=ec, bnk=bnk, slot=slot: e.transpose(
                    PSb[bnk][:, 0:128], woh[slot][:, hh*128:(hh+1)*128], ident_sb[:]), waits=tuple(w))
                last = dv(lambda e, hh=hh, ec=ec, bnk=bnk: e.tensor_copy(
                    A2[:, hh*2048 + ec*128: hh*2048 + (ec+1)*128], PSb[bnk][:, 0:128]),
                    waits=(("pe", tk),))
                g4[("t", bnk)] = [("dv", last)]
            g4[("wo", slot)] = [("pe", tk)]
        d_wot = P.c("dv")
        for mt in range(NTILE):
            pbuf = mt % 2
            w = g4.pop(("g", pbuf), [])
            lg = yq(lambda e, mt=mt, pbuf=pbuf: e.dma_start(out=gbuf[pbuf][:], in_=g_scr[mt*128:(mt+1)*128, :]),
                    "xq0" if pbuf == 0 else "xq1", waits=tuple(w))
            lx = yq(lambda e, mt=mt, pbuf=pbuf: e.dma_start(out=xbuf[pbuf][:], in_=x_own[mt*128:(mt+1)*128, :]),
                    "xq0" if pbuf == 0 else "xq1")
            d_cst = None
            for hh in range(16):
                bnk = hh % 2
                w = g4.pop(("t", bnk), [])
                if hh < 2 and mt == 0:
                    w += [("dv", d_wot)]
                tk = pe(lambda e, hh=hh, mt=mt, bnk=bnk: e.transpose(
                    PSb[bnk][:, 0:128], A1[:, hh*2048 + mt*128: hh*2048 + (mt+1)*128],
                    ident_sb[:]), waits=tuple(w))
                d_cst = dv(lambda e, hh=hh, bnk=bnk, pbuf=pbuf: e.tensor_copy(
                    cstage[pbuf][:, hh*128:(hh+1)*128], PSb[bnk][:, 0:128]),
                    waits=(("pe", tk),))
                g4[("t", bnk)] = [("dv", d_cst)]
            for half in range(2):
                for rr in range(4):
                    for bo in range(2):
                        bnk = 2 + bo
                        ec = half * 8 + bo * 4 + rr
                        for hh in range(16):
                            w = []
                            if hh == 0:
                                w = g4.pop(("s", bnk, rr), [])
                                if rr == 0 and bo == 0 and half == 0:
                                    w += [("dv", d_cst)]
                                    if mt == 0:
                                        w += [("dv", d_wot)]
                            s_mm = pe(lambda e, ec=ec, hh=hh, bnk=bnk, rr=rr, mt=mt, pbuf=pbuf: e.matmul(
                                PS[bnk][:, rr*128:(rr+1)*128],
                                cstage[pbuf][:, hh*128:(hh+1)*128],
                                A2[:, hh*2048 + ec*128: hh*2048 + (ec+1)*128],
                                start=(hh == 0), stop=(hh == 15)), waits=tuple(w))
                        g4[("smm", bnk, rr)] = s_mm
                for bo in range(2):
                    bnk = 2 + bo
                    e0c = half * 1024 + bo * 512
                    w = [("pe", g4[("smm", bnk, 3)]), ((("xq0" if pbuf == 0 else "xq1"), lx))]
                    dv(lambda e, bnk=bnk, e0c=e0c, pbuf=pbuf: e.tensor_tensor(
                        tmp4[:], PS[bnk][:], gbuf[pbuf][:, e0c:e0c+512], ALU.mult), waits=tuple(w))
                    d_cb = dv(lambda e, bnk=bnk, e0c=e0c, pbuf=pbuf: e.tensor_tensor(
                        obuf[pbuf][:, e0c:e0c+512], tmp4[:], xbuf[pbuf][:, e0c:e0c+512], ALU.add))
                    for rr in range(4):
                        g4[("s", bnk, rr)] = [("dv", d_cb)]
            so = oq(lambda e, mt=mt, pbuf=pbuf: e.dma_start(out=out[mt*128:(mt+1)*128, :], in_=obuf[pbuf][:]),
                    waits=(("dv", P.c("dv")),))
            g4[("g", pbuf)] = [("oq", so)]

    for s_ in ("oq", "xq0", "xq1", "cq", "qq", "vq", "kvq", "wgq0", "wgq1", "wgq2", "wdq", "wcq0", "wcq1"):
        P.op("y", lambda e, s_=s_: e.wait_ge(P.sems[s_], P.c(s_)))

    P.emit()
    return nc, dbg_out



import numpy as np
import ml_dtypes

def perm_idx():
    p = np.arange(8192)
    return 4 * (p % 2048) + p // 2048

def make_consts():
    i_, j_ = np.mgrid[0:128, 0:128]
    m_prev = (j_ >= i_ + 64).astype(np.float32)
    m_diag = ((i_ - j_ >= 0) & (i_ - j_ <= 64)).astype(np.float32)
    caus = np.where(j_ <= i_, 0.0, -1.0e30).astype(np.float32)
    z = np.zeros((128, 128), np.float32)
    negf = np.full((128, 128), -1.0e30, np.float32)
    band3 = {
        0: np.concatenate([z, m_prev, m_diag], axis=1),
        1: np.concatenate([m_prev, m_diag, z], axis=1),
    }
    caus2 = {
        0: np.concatenate([z, caus], axis=1),
        1: np.concatenate([caus, negf], axis=1),
    }
    ident = np.eye(128, dtype=ml_dtypes.bfloat16)
    ones_f = np.ones((1, 128), np.float32)
    extras = {
        "ones_b": np.ones((1, 128), ml_dtypes.bfloat16),
        "identf": np.eye(128, dtype=np.float32),
        "ident16": np.eye(16, dtype=np.float32),
    }
    return band3, caus2, ident, ones_f, extras

def own_tiles(role):
    if role == 0:
        return [31 - 2 * k for k in range(16)]
    return [30 - 2 * k for k in range(16)]

def core_inputs(core, x, Wq, Wk, Wv, q2l, vup, Wo, wg, wu, wd):
    b, role = core // 2, core % 2
    band3, caus2, ident, ones_f, extras = make_consts()
    idx = perm_idx()
    tiles = own_tiles(role)
    xs = np.ascontiguousarray(x[b])
    xo = np.ascontiguousarray(
        xs.reshape(32, 128, 2048)[tiles].reshape(2048, 2048))
    return {
        "x_seq": xs, "x_own": xo,
        "Wq": Wq, "Wk": Wk, "Wv": Wv, "q2l": q2l, "vup": vup, "Wo": Wo,
        "wg_p": np.ascontiguousarray(wg[:, idx]),
        "wu_p": np.ascontiguousarray(wu[:, idx]),
        "wd_p": np.ascontiguousarray(wd[idx, :]),
        "ident": ident, "band3": band3[role], "caus2": caus2[role],
        "ones_f": ones_f, **extras,
    }



_PROG_CACHE = {}

def kernel(L0_attn_hidden_states, L0_attn_Wq, L0_attn_Wk, L0_attn_Wv, L0_attn_q_to_latent,
           L0_attn_v_up, L0_attn_Wo, L0_ffn_w_gate, L0_ffn_w_up, L0_ffn_w_down):
    x = np.ascontiguousarray(np.asarray(L0_attn_hidden_states, dtype=np.float32))
    args = (np.ascontiguousarray(np.asarray(L0_attn_Wq, dtype=np.float32)),
            np.ascontiguousarray(np.asarray(L0_attn_Wk, dtype=np.float32)),
            np.ascontiguousarray(np.asarray(L0_attn_Wv, dtype=np.float32)),
            np.ascontiguousarray(np.asarray(L0_attn_q_to_latent, dtype=np.float32)),
            np.ascontiguousarray(np.asarray(L0_attn_v_up, dtype=np.float32)),
            np.ascontiguousarray(np.asarray(L0_attn_Wo, dtype=np.float32)),
            np.ascontiguousarray(np.asarray(L0_ffn_w_gate, dtype=np.float32)),
            np.ascontiguousarray(np.asarray(L0_ffn_w_up, dtype=np.float32)),
            np.ascontiguousarray(np.asarray(L0_ffn_w_down, dtype=np.float32)))
    if "nc" not in _PROG_CACHE:
        nc, _ = build_program(dbg=(), phases=("p0", "p1", "p2", "p3", "p4"))
        _PROG_CACHE["nc"] = nc
    nc = _PROG_CACHE["nc"]
    in_maps = [core_inputs(c, x, *args) for c in range(8)]
    res = run_bass_kernel_spmd(nc, in_maps, core_ids=list(range(8)))
    out = np.empty((4, 4096, 2048), dtype=np.float32)
    for c in range(8):
        b, role = c // 2, c % 2
        o = np.asarray(res.results[c]["out"]).reshape(16, 128, 2048)
        for k, tau in enumerate(own_tiles(role)):
            out[b, tau*128:(tau+1)*128, :] = o[k]
    return out


# version 16
# speedup vs baseline: 1.2643x; 1.2188x over previous
import numpy as np
import ml_dtypes
from concourse.bass_utils import run_bass_kernel_spmd

import numpy as np
import concourse.bass as bass
import concourse.mybir as mybir
from contextlib import ExitStack

F32 = mybir.dt.float32
BF16 = mybir.dt.bfloat16
AF = mybir.ActivationFunctionType
ALU = mybir.AluOpType
AX = mybir.AxisListType

H, D, L, EMB, II, S, TC = 16, 128, 64, 2048, 8192, 4096, 2048
NTILE = 16
NSEQT = 32
NJ = EMB // 128
EXT = [32 - 2 * j for j in range(16)]
NEG = -1.0e30
NPASS = 4
NHEADS = 16
NSLOTS = 16
NMIC = 4
NB = 8
MW = 256


class Prog:
    ENGS = ("t", "v", "s", "g", "y")

    def __init__(self, nc, stk):
        self.nc, self.stk = nc, stk
        self.q = {e: [] for e in self.ENGS}
        self.sems, self.cnt = {}, {}

    def sem(self, name):
        self.sems[name] = self.stk.enter_context(self.nc.semaphore(name))
        self.cnt[name] = 0

    import os
    CHAIN = {"v": "dv", "s": "ac"}
    if os.environ.get("KB_CHAIN_PE"):
        CHAIN = {"v": "dv", "s": "ac", "t": "pe"}

    def op(self, eng, fn, waits=(), inc=None, incby=1):
        waits = list(waits)
        ch = self.CHAIN.get(eng)
        if ch is not None and inc == ch and self.cnt[ch] > 0:
            waits.append((ch, self.cnt[ch]))
        val = None
        if inc is not None:
            self.cnt[inc] += incby
            val = self.cnt[inc]
        sems = self.sems
        wl = tuple((sems[s], v) for s, v in waits)
        s_inc = sems[inc] if inc is not None else None

        def run(e, fn=fn, wl=wl, s_inc=s_inc, incby=incby):
            for s, v in wl:
                e.wait_ge(s, v)
            inst = fn(e)
            if s_inc is not None:
                if inst is None:
                    inst = e.nop()
                inst.then_inc(s_inc, incby)
        self.q[eng].append(run)
        return val

    def c(self, name):
        return self.cnt[name]

    def emit(self):
        nc = self.nc
        with nc.Block() as block:
            @block.tensor
            def _(t):
                for f in self.q["t"]:
                    f(t)

            @block.vector
            def _(v):
                for f in self.q["v"]:
                    f(v)

            @block.scalar
            def _(s):
                for f in self.q["s"]:
                    f(s)

            @block.gpsimd
            def _(g):
                for f in self.q["g"]:
                    f(g)

            @block.sync
            def _(y):
                for f in self.q["y"]:
                    f(y)


def build_program(dbg=(), phases=("p0", "p1", "p2", "p3", "p4")):
    import os
    nc = bass.Bass("TRN2", target_bir_lowering=False,
                   detect_race_conditions=not os.environ.get("KB_NORACE"))
    stk = ExitStack()

    def din(name, shape, dt=F32):
        return nc.dram_tensor(name, shape, dt, kind="ExternalInput").ap()

    x_seq = din("x_seq", [S, EMB])
    x_own = din("x_own", [TC, EMB])
    Wq = din("Wq", [EMB, EMB])
    Wk = din("Wk", [L, EMB])
    Wv = din("Wv", [L, EMB])
    q2l = din("q2l", [H, D, L])
    vup = din("vup", [H, L, D])
    Wo = din("Wo", [EMB, EMB])
    wg_p = din("wg_p", [EMB, II])
    wu_p = din("wu_p", [EMB, II])
    wd_p = din("wd_p", [II, EMB])
    ident = din("ident", [128, 128], BF16)
    band3 = din("band3", [128, 384])
    caus2 = din("caus2", [128, 256])
    ones_f = din("ones_f", [1, 128])
    ones_b = din("ones_b", [1, 128], BF16)
    identf = din("identf", [128, 128])
    ident16 = din("ident16", [16, 16])

    out = nc.dram_tensor("out", [TC, EMB], F32, kind="ExternalOutput").ap()
    g_scr = nc.dram_tensor("g_scr", [TC, EMB], BF16, kind="Internal").ap()
    kp_scr = nc.dram_tensor("kp_scr", [128, S], BF16, kind="Internal").ap()
    bk_scr = nc.dram_tensor("bk_scr", [128, 16 * 384], BF16, kind="Internal").ap()
    wd_bf = nc.dram_tensor("wd_bf", [II, EMB], BF16, kind="Internal").ap()
    dbg_out = {}
    if "kpack" in dbg:
        dbg_out["d_kpack"] = nc.dram_tensor("d_kpack", [128, S], BF16, kind="ExternalOutput").ap()
        dbg_out["d_ksum"] = nc.dram_tensor("d_ksum", [1, S], BF16, kind="ExternalOutput").ap()
        dbg_out["d_xn"] = nc.dram_tensor("d_xn", [128, EMB], BF16, kind="ExternalOutput").ap()
    if "g" in dbg:
        dbg_out["d_g"] = nc.dram_tensor("d_g", [TC, EMB], BF16, kind="ExternalOutput").ap()
    if "q" in dbg:
        dbg_out["d_q"] = nc.dram_tensor("d_q", [128, H * TC], BF16, kind="ExternalOutput").ap()
        dbg_out["d_qsb"] = nc.dram_tensor("d_qsb", [128, 256], F32, kind="ExternalOutput").ap()
        dbg_out["d_negc"] = nc.dram_tensor("d_negc", [128, 256], F32, kind="ExternalOutput").ap()
    if "ctx" in dbg:
        dbg_out["d_ctx"] = nc.dram_tensor("d_ctx", [128, H * TC], BF16, kind="ExternalOutput").ap()

    pstk = stk.enter_context(ExitStack())

    def sb(st, name, shape, dt):
        return st.enter_context(nc.sbuf_tensor(name, shape, dt))

    A1 = sb(pstk, "A1", [128, 32768], BF16)
    A2 = sb(pstk, "A2", [128, 32768], BF16)
    A2f = A2.ap().bitcast(F32)

    ident_sb = sb(pstk, "ident_sb", [128, 128], BF16)
    band3_sb = sb(pstk, "band3_sb", [128, 384], F32)
    caus2_sb = sb(pstk, "caus2_sb", [128, 256], F32)
    ones_sb = sb(pstk, "ones_sb", [1, 128], F32)
    ones_bf_sb = sb(pstk, "ones_bf_sb", [1, 128], BF16)
    identf_sb = sb(pstk, "identf_sb", [128, 128], F32)
    ident16_sb = sb(pstk, "ident16_sb", [16, 16], F32)
    q2lt = sb(pstk, "q2lt", [64, H * 128], BF16)
    q2lrs = sb(pstk, "q2lrs", [128, H], BF16)
    vup_sb = sb(pstk, "vup_sb", [128, H * 128], BF16)
    qsb = sb(pstk, "qsb", [128, 256], F32)
    negc_bc = sb(pstk, "negc_bc", [128, 256], F32)
    sumg = [sb(pstk, f"sumg{i}", [128, 8], F32) for i in range(2)]
    sumtot = [sb(pstk, f"sumtot{i}", [128, 1], F32) for i in range(2)]
    rinv = [sb(pstk, f"rinv{i}", [128, 1], F32) for i in range(2)]
    negc15 = sb(pstk, "negc15", [128, 1], F32)
    cmaxr = sb(pstk, "cmaxr", [1, 16], F32)
    cminr = sb(pstk, "cminr", [1, 16], F32)
    cmax_bc = sb(pstk, "cmax_bc", [128, 16], F32)
    cmin_bc = sb(pstk, "cmin_bc", [128, 16], F32)
    tnc = sb(pstk, "tnc", [128, 16], F32)
    tnc2 = sb(pstk, "tnc2", [128, 16], F32)
    s1 = sb(pstk, "s1", [128, 1], F32)
    s2 = sb(pstk, "s2", [128, 1], F32)
    mvec = sb(pstk, "mvec", [128, 1], F32)
    msq = sb(pstk, "msq", [128, 1], F32)
    varv = sb(pstk, "varv", [128, 1], F32)
    rstd = sb(pstk, "rstd", [128, 1], F32)
    bias_ln = sb(pstk, "bias_ln", [128, 1], F32)

    PS = [pstk.enter_context(nc.psum_tensor(f"ps{i}", [128, 512], F32)) for i in range(8)]
    PSb = [p.ap().bitcast(BF16) for p in PS]

    kstk = stk.enter_context(ExitStack())
    kpack = sb(kstk, "kpack", [128, S], BF16)
    BK_sb = sb(kstk, "BK_sb", [128, 16 * 384], BF16)

    P = Prog(nc, stk)
    for s_ in ("pe", "dv", "ac", "xq0", "xq1", "cq", "qq", "vq", "kvq", "wgq0", "wgq1", "wgq2", "wdq", "oq", "wcq0", "wcq1", "bar"):
        P.sem(s_)

    def pe(fn, waits=()):
        return P.op("t", fn, waits=waits, inc="pe")

    def dv(fn, waits=()):
        return P.op("v", fn, waits=waits, inc="dv")

    def dvr(fn, waits=()):
        t = P.op("v", fn, waits=waits, inc="dv")
        P.op("v", lambda e, t=t: e.wait_ge(P.sems["dv"], t))
        return t

    def ac(fn, waits=()):
        return P.op("s", fn, waits=waits, inc="ac")

    def gq(fn, sem, waits=()):
        return P.op("g", fn, waits=waits, inc=sem, incby=16)

    def yq(fn, sem, waits=()):
        return P.op("y", fn, waits=waits, inc=sem, incby=16)

    def oq(fn, waits=()):
        return P.op("y", fn, waits=waits, inc="oq", incby=16)

    DMA_SEMS = ("xq0", "xq1", "cq", "qq", "vq", "kvq", "wgq0", "wgq1", "wgq2", "wdq", "oq", "wcq0", "wcq1")

    def barrier():
        tgt = P.cnt["bar"] + 5
        tail = {"t": ("pe",), "v": ("dv",), "s": ("ac",), "g": (), "y": DMA_SEMS}
        for e in Prog.ENGS:
            w = tuple((s_, P.c(s_)) for s_ in tail[e])
            P.op(e, lambda en: en.nop(), waits=w, inc="bar")
        for e in Prog.ENGS:
            P.op(e, lambda en, tgt=tgt: en.wait_ge(P.sems["bar"], tgt))

    yq(lambda e: e.dma_start(out=ident_sb[:], in_=ident), "cq")
    yq(lambda e: e.dma_start(out=band3_sb[:], in_=band3), "cq")
    yq(lambda e: e.dma_start(out=caus2_sb[:], in_=caus2), "cq")
    yq(lambda e: e.dma_start(out=ones_sb[:], in_=ones_f), "cq")
    yq(lambda e: e.dma_start(out=ones_bf_sb[:], in_=ones_b), "cq")
    yq(lambda e: e.dma_start(out=identf_sb[:], in_=identf), "cq")
    w_const = yq(lambda e: e.dma_start(out=ident16_sb[:], in_=ident16), "cq")

    p0stk = stk.enter_context(ExitStack())
    ksum = sb(p0stk, "ksum", [1, S], BF16)
    wkte = sb(p0stk, "wkte", [128, NJ * 65], BF16)
    wvt = sb(p0stk, "wvt", [128, NJ * 64], BF16)
    wk_bf = sb(p0stk, "wk_bf", [64, EMB], BF16)
    wv_bf = sb(p0stk, "wv_bf", [64, EMB], BF16)
    wkcs = sb(p0stk, "wkcs", [128, NJ], F32)
    gq(lambda e: e.dma_start(out=wk_bf[:], in_=Wk), "kvq")
    w_kv = gq(lambda e: e.dma_start(out=wv_bf[:], in_=Wv), "kvq")

    ev_prev = {}
    for j in range(NJ):
        b = j % 2
        w = [("kvq", w_kv), ("cq", w_const)] if j == 0 else []
        if j >= 2:
            w.append(("dv", ev_prev[j - 2]))
        tk = pe(lambda e, j=j, b=b: e.transpose(PSb[b][:, 0:64], wk_bf[0:64, j*128:(j+1)*128], ident_sb[0:64, 0:64]),
                waits=tuple(w))
        dv(lambda e, j=j, b=b: e.tensor_copy(wkte[:, j*65:j*65+64], PSb[b][:, 0:64]), waits=(("pe", tk),))
        dvr(lambda e, j=j, b=b: e.tensor_reduce(wkcs[:, j:j+1], PSb[b][:, 0:64], AX.X, ALU.add))
        ev_prev[j] = dv(lambda e, j=j: e.tensor_copy(wkte[:, j*65+64:j*65+65], wkcs[:, j:j+1]))
    for j in range(NJ):
        b = j % 2
        w = [("dv", ev_prev[j - 2])] if j >= 2 else [("dv", ev_prev[NJ - 1])]
        tk = pe(lambda e, j=j, b=b: e.transpose(PSb[b][:, 0:64], wv_bf[0:64, j*128:(j+1)*128], ident_sb[0:64, 0:64]),
                waits=tuple(w))
        ev_prev[j] = dv(lambda e, j=j, b=b: e.tensor_copy(wvt[:, j*64:(j+1)*64], PSb[b][:, 0:64]),
                        waits=(("pe", tk),))

    q2l_f = sb(p0stk, "q2l_f", [128, 64], F32)
    q2l_bf = sb(p0stk, "q2l_bf", [128, 64], BF16)
    rs_f = sb(p0stk, "rs_f", [128, 1], F32)
    last_q2l = ev_prev[NJ - 1]
    d1 = None
    for h in range(H):
        w_l = (("dv", d1),) if d1 is not None else ()
        lw = yq(lambda e, h=h: e.dma_start(out=q2l_f[:], in_=q2l[h]), "qq", waits=w_l)
        d1 = dv(lambda e: e.tensor_copy(q2l_bf[:], q2l_f[:]), waits=(("qq", lw), ("dv", last_q2l)))
        dvr(lambda e: e.tensor_reduce(rs_f[:], q2l_bf[:], AX.X, ALU.add))
        d2 = dv(lambda e, h=h: e.tensor_copy(q2lrs[:, h:h+1], rs_f[:]))
        tk = pe(lambda e, h=h: e.transpose(PSb[h % 2][0:64, 0:128], q2l_bf[:], ident_sb[:]), waits=(("dv", d2),))
        last_q2l = dv(lambda e, h=h: e.tensor_copy(q2lt[0:64, h*128:(h+1)*128], PSb[h % 2][0:64, 0:128]),
                      waits=(("pe", tk),))
    gq(lambda e: e.dma_start(out=vup_sb.ap()[64:128, :].rearrange("l (h d) -> l h d", d=128), in_=vup.rearrange("h l d -> l h d")), "vq")

    xstage = [sb(p0stk, "xs0", [128, EMB], F32), sb(p0stk, "xs1", [128, EMB], F32)]
    xn_st2 = [A2.ap()[:, 2048:4096], A2.ap()[:, 4096:6144]]
    xnfm2 = [A2.ap()[:, 6144:8192], A2.ap()[:, 8192:10240]]
    trash = A2.ap()[:, 0:2048]
    LNT = [{n: sb(p0stk, f"ln_{n}{k}", [128, 1], F32)
            for n in ("s1", "s2", "mvec", "msq", "varv", "rstd", "bias")} for k in range(2)]

    def ln_tile(src_ap, buf, prev):
        T = LNT[buf]
        xn_dst = xn_st2[buf]
        w_load = []
        if prev.get("axn2"):
            w_load.append(("ac", prev["axn2"]))
        if prev.get("s1red2"):
            w_load.append(("dv", prev["s1red2"]))
        xsem = "xq0" if buf == 0 else "xq1"
        lw = yq(lambda e, src_ap=src_ap, buf=buf: e.dma_start(out=xstage[buf][:], in_=src_ap),
                xsem, waits=tuple(w_load))
        s1red = dvr(lambda e, buf=buf, T=T: e.tensor_reduce(T["s1"][:], xstage[buf][:], AX.X, ALU.add),
                    waits=((xsem, lw),))
        w_sq = [(xsem, lw)]
        if prev.get("msq2"):
            w_sq.append(("dv", prev["msq2"]))
        a_sq = ac(lambda e, buf=buf, T=T: e.activation(trash[:], xstage[buf][:], AF.Square,
                                                       accum_out=T["s2"][:]),
                  waits=tuple(w_sq))
        dv(lambda e, T=T: e.tensor_scalar(T["mvec"][:], T["s1"][:], 1.0 / EMB, None, ALU.mult))
        msq_t = dv(lambda e, T=T: e.tensor_scalar(T["msq"][:], T["s2"][:], 1.0 / EMB, None, ALU.mult),
                   waits=(("ac", a_sq),))
        w_tt = [("ac", prev["a22"])] if prev.get("a22") else []
        dv(lambda e, T=T: e.tensor_tensor(T["varv"][:], T["mvec"][:], T["mvec"][:], ALU.mult),
           waits=tuple(w_tt))
        d_v = dv(lambda e, T=T: e.scalar_tensor_tensor(T["varv"][:], T["msq"][:], 1.0e-5, T["varv"][:],
                                                       ALU.add, ALU.subtract))
        ac(lambda e, T=T: e.activation(T["varv"][:], T["varv"][:], AF.Ln), waits=(("dv", d_v),))
        a2_ = ac(lambda e, T=T: e.activation(T["rstd"][:], T["varv"][:], AF.Exp, scale=-0.5))
        d2_ = dv(lambda e, T=T: e.scalar_tensor_tensor(T["bias"][:], T["mvec"][:], -1.0, T["rstd"][:],
                                                       ALU.mult, ALU.mult),
                 waits=(("ac", a2_),))
        w_xn = [("dv", d2_)]
        if prev.get("lastT2"):
            w_xn.append(("pe", prev["lastT2"]))
        axn = ac(lambda e, buf=buf, T=T, xn_dst=xn_dst: e.activation(
            xn_dst[:], xstage[buf][:], AF.Identity, bias=T["bias"][:], scale=T["rstd"][:]),
            waits=tuple(w_xn))
        return {"axn": axn, "s1red": s1red, "msq": msq_t, "a2": a2_, "lw": lw}

    if "p0" in phases:
        hist = []
        kv_prev = None
        ptguard = {}
        PTB = (2, 3, 6, 7)
        wdstage = [A2.ap()[:, 16384:18432], A2.ap()[:, 18432:20480]]
        wd_rr = wd_p.rearrange("(c p) e -> c p e", p=128)
        wd_st = {}

        def prev_of(i):
            p = {}
            if i >= 2:
                p["axn2"] = hist[i-2]["axn"]
                p["s1red2"] = hist[i-2]["s1red"]
                p["msq2"] = hist[i-2]["msq"]
                p["a22"] = hist[i-2]["a2"]
                p["lastT2"] = hist[i-2]["lastT"]
            return p

        def emit_ptkv(i):
            nonlocal kv_prev
            buf = i % 2
            tks = hist[i]
            axn = tks["axn"]
            seq = i < NSEQT
            it2 = i - NSEQT
            evs = []
            tk = None
            for j in range(NJ):
                b = PTB[j % 4]
                w = list(ptguard.get(j % 4, []))
                if j < 4:
                    w.append(("ac", axn))
                tk = pe(lambda e, j=j, b=b, buf=buf: e.transpose(
                    PSb[b][:, 0:128], xn_st2[buf][:, j*128:(j+1)*128], ident_sb[:]), waits=tuple(w))
                if seq:
                    ev = dv(lambda e, j=j, b=b, buf=buf: e.tensor_copy(
                        xnfm2[buf][:, j*128:(j+1)*128], PSb[b][:, 0:128]), waits=(("pe", tk),))
                else:
                    ev = dv(lambda e, j=j, b=b, it2=it2: e.tensor_copy(
                        A1[:, j*2048 + it2*128: j*2048 + (it2+1)*128], PSb[b][:, 0:128]),
                        waits=(("pe", tk),))
                ptguard[j % 4] = [("dv", ev)]
                evs.append(ev)
            tks["lastT"] = tk
            if seq:
                for j in range(NJ):
                    w = [("dv", evs[j])]
                    if j == 0 and kv_prev is not None:
                        w.append(("dv", kv_prev))
                    pe(lambda e, j=j, buf=buf: e.matmul(PS[4][0:65, 0:128], wkte[:, j*65:(j+1)*65],
                                                        xnfm2[buf][:, j*128:(j+1)*128],
                                                        start=(j == 0), stop=(j == NJ - 1)),
                       waits=tuple(w))
                    kv_t = pe(lambda e, j=j, buf=buf: e.matmul(PS[5][0:64, 0:128], wvt[:, j*64:(j+1)*64],
                                                               xnfm2[buf][:, j*128:(j+1)*128],
                                                               start=(j == 0), stop=(j == NJ - 1)))
                dv(lambda e, i=i: e.tensor_copy(kpack[0:64, i*128:(i+1)*128], PS[4][0:64, 0:128]),
                   waits=(("pe", kv_t),))
                dv(lambda e, i=i: e.tensor_copy(ksum[0:1, i*128:(i+1)*128], PS[4][64:65, 0:128]))
                kv_prev = dv(lambda e, i=i: e.tensor_copy(kpack[64:128, i*128:(i+1)*128], PS[5][0:64, 0:128]))

        def emit_cmax_bk():
            for j in range(16):
                w0 = max(0, EXT[j] - 3) * 128
                w1 = EXT[j] * 128
                dvr(lambda e, j=j, w0=w0, w1=w1: e.tensor_reduce(cmaxr[:, j:j+1], ksum[:, w0:w1], AX.X, ALU.max))
                dvr(lambda e, j=j, w0=w0, w1=w1: e.tensor_reduce(cminr[:, j:j+1], ksum[:, w0:w1], AX.X, ALU.min))
            d_cm = P.c("dv")
            pe(lambda e: e.matmul(PS[0][0:128, 0:16], ones_sb[:], cmaxr[:, 0:16], start=True, stop=True),
               waits=(("dv", d_cm),))
            t2 = pe(lambda e: e.matmul(PS[1][0:128, 0:16], ones_sb[:], cminr[:, 0:16], start=True, stop=True))
            dv(lambda e: e.tensor_copy(cmax_bc[:], PS[0][0:128, 0:16]), waits=(("pe", t2),))
            e2 = dv(lambda e: e.tensor_copy(cmin_bc[:], PS[1][0:128, 0:16]))
            g0 = {("bk", 0): [("dv", e2)], ("bk", 1): [("dv", e2)]}
            for j in range(16):
                e0 = max(0, EXT[j] - 3)
                nwc = EXT[j] - e0
                woff = (3 - nwc) * 128
                for wci in range(nwc):
                    w = g0.pop(("bk", wci % 2), [])
                    if j == 0 and wci == 0:
                        w += [("cq", P.c("cq"))]
                    bk_mm = pe(lambda e, j=j, wci=wci, e0=e0: e.matmul(
                        PS[0][0:128, 0:128] if wci % 2 == 0 else PS[1][0:128, 0:128],
                        ones_bf_sb[:], ksum[:, (e0+wci)*128:(e0+wci+1)*128],
                        start=True, stop=True), waits=tuple(w))
                    d_bk = dv(lambda e, j=j, wci=wci, woff=woff: e.tensor_tensor(
                        BK_sb[:, j*384 + woff + wci*128: j*384 + woff + (wci+1)*128],
                        band3_sb[:, woff + wci*128: woff + (wci+1)*128],
                        (PS[0][0:128, 0:128] if wci % 2 == 0 else PS[1][0:128, 0:128]), ALU.mult),
                        waits=(("pe", bk_mm),))
                    g0[("bk", wci % 2)] = [("dv", d_bk)]
                if woff > 0:
                    dv(lambda e, j=j, woff=woff: e.tensor_scalar(
                        BK_sb[:, j*384: j*384 + woff], band3_sb[:, 0:woff], 0.0, None, ALU.mult))
            d_sp0 = P.c("dv")
            oq(lambda e: e.dma_start(out=kp_scr, in_=kpack[:]), waits=(("dv", d_sp0),))
            oq(lambda e: e.dma_start(out=bk_scr, in_=BK_sb[:]))

        pend = None
        for i in range(NSEQT + NTILE):
            buf = i % 2
            src = x_seq[i*128:(i+1)*128, :] if i < NSEQT else \
                x_own[(i-NSEQT)*128:(i-NSEQT+1)*128, :]
            tks = ln_tile(src, buf, prev_of(i))
            hist.append(tks)
            if pend is not None:
                emit_ptkv(pend)
            if i == NSEQT:
                emit_cmax_bk()
            if i < NSEQT:
                for rr in range(2):
                    row = i * 2 + rr
                    wsem = "wdq" if row % 2 == 0 else "vq"
                    ssem = "wcq0" if row % 2 == 0 else "wcq1"
                    w_l = [(ssem, wd_st[row % 2])] if row >= 2 else []
                    wl_t = gq(lambda e, row=row: e.dma_start(out=wdstage[row % 2][:], in_=wd_rr[row]),
                              wsem, waits=tuple(w_l))
                    wd_st[row % 2] = yq(lambda e, row=row: e.dma_start(
                        out=wd_bf[row*128:(row+1)*128, :], in_=wdstage[row % 2][:]),
                        ssem, waits=((wsem, wl_t),))
            pend = i
        emit_ptkv(pend)

        if "kpack" in dbg:
            dq = P.c("dv")
            oq(lambda e: e.dma_start(out=dbg_out["d_kpack"], in_=kpack[:]), waits=(("dv", dq),))
            oq(lambda e: e.dma_start(out=dbg_out["d_ksum"], in_=ksum[:]))
            oq(lambda e: e.dma_start(out=dbg_out["d_xn"], in_=A1[:, 0:2048]))


    barrier()
    p0stk.close()
    kstk.close()

    if "p1" in phases:
        p1stk = stk.enter_context(ExitStack())
        acc = A2f
        tb = 16384
        h_all = [A2.ap()[:, tb:tb + 4096], A2.ap()[:, tb + 4096:tb + 8192]]
        habs = A2.ap()[:, tb + 8192:tb + 9216]
        hsp = A2.ap()[:, tb + 9216:tb + 10240]
        hsT = [A2.ap()[:, tb + 10240:tb + 11264], A2.ap()[:, tb + 11264:tb + 12288]]
        spt = [A2.ap()[:, tb + 12288 + k*256: tb + 12288 + (k+1)*256] for k in range(3)]
        gstage = A2.ap()[:, tb + 13056:tb + 15104]
        tmpg = [sb(p1stk, f"tmpg{i}", [128, 128], F32) for i in range(2)]
        SLOTS = 3
        gsems = ("wgq0", "wgq1", "wgq2")
        wgbuf = [sb(p1stk, f"wgb{i}", [128, NJ * 128], BF16) for i in range(SLOTS)]
        wubuf = [sb(p1stk, f"wub{i}", [128, NJ * 128], BF16) for i in range(SLOTS)]
        wdbuf = [sb(p1stk, f"wdb{i}", [128, EMB], BF16) for i in range(8)]

        wg_r = wg_p.rearrange("(j p) i -> p j i", p=128)
        wu_r = wu_p.rearrange("(j p) i -> p j i", p=128)

        NBAT = NPASS * NB
        pend_gu = {}
        guards = {}
        Z = {"prev_sp": None, "prev_tev": None, "h_done": {}, "sp_done": {},
             "pt_done": {}, "gadd": {}, "pair_entry": {}, "wd_cur": None,
             "dn_last": None, "a_g": {}, "og_last": None, "tmpg_last": {}}

        def emit_sp(bp, mi):
            hb = h_all[bp % 2]
            w_ab = [("dv", Z["h_done"][bp % 2])]
            if Z["prev_sp"] is not None:
                w_ab.append(("dv", Z["prev_sp"]))
            a_ab = ac(lambda e, hb=hb, mi=mi: e.activation(habs[:], hb[:, mi*1024:(mi+1)*1024], AF.Abs),
                      waits=tuple(w_ab))
            avs = [habs[:, m_*256:(m_+1)*256] for m_ in range(4)]
            hvs = [hb[:, mi*1024 + m_*256: mi*1024 + (m_+1)*256] for m_ in range(4)]
            dv(lambda e, avs=avs: e.tensor_tensor(spt[0][:], avs[0], avs[1], ALU.min), waits=(("ac", a_ab),))
            dv(lambda e, avs=avs: e.tensor_tensor(spt[1][:], avs[0], avs[1], ALU.max))
            dv(lambda e, avs=avs: e.tensor_tensor(spt[2][:], avs[2], avs[3], ALU.min))
            dv(lambda e: e.tensor_tensor(spt[0][:], spt[0][:], spt[2][:], ALU.max))
            dv(lambda e, avs=avs: e.tensor_tensor(spt[2][:], avs[2], avs[3], ALU.max))
            dv(lambda e: e.tensor_tensor(spt[1][:], spt[1][:], spt[2][:], ALU.min))
            dv(lambda e: e.tensor_tensor(spt[0][:], spt[0][:], spt[1][:], ALU.max))
            sp_d = None
            for m_ in range(4):
                dv(lambda e, a=avs[m_]: e.tensor_tensor(spt[1][:], a, spt[0][:], ALU.is_ge))
                w_sp = ()
                if m_ == 0 and Z["prev_tev"] is not None:
                    w_sp = (("pe", Z["prev_tev"]),)
                sp_d = dv(lambda e, hh=hvs[m_], m_=m_: e.tensor_tensor(hsp[:, m_*256:(m_+1)*256],
                                                                       hh, spt[1][:], ALU.mult), waits=w_sp)
            Z["prev_sp"] = sp_d
            Z["sp_done"][mi] = sp_d

        def emit_PT(bp, mi):
            pair = (4, 5) if mi % 2 == 0 else (6, 7)
            pte = {}
            tk = None
            for cb in range(8):
                bnk = pair[cb % 2]
                w = []
                if cb < 2:
                    w.append(("dv", Z["sp_done"][mi]))
                    w += Z["pair_entry"].pop((pair, cb % 2), [])
                else:
                    w.append(("dv", pte[cb - 2]))
                tk = pe(lambda e, cb=cb, bnk=bnk: e.transpose(PSb[bnk][:, 0:128],
                                                              hsp[:, cb*128:(cb+1)*128], ident_sb[:]),
                        waits=tuple(w))
                pte[cb] = dv(lambda e, cb=cb, bnk=bnk, mi=mi: e.tensor_copy(
                    hsT[mi % 2][:, cb*128:(cb+1)*128], PSb[bnk][:, 0:128]), waits=(("pe", tk),))
            Z["prev_tev"] = tk
            Z["pt_done"][mi] = pte[7]

        def emit_dn(bp, mi, setk):
            pair = (4, 5) if mi % 2 == 0 else (6, 7)
            last = None
            for pi in range(2):
                bnk = pair[pi]
                ch0 = (setk*8 + pi*4) * 128
                for ci in range(8):
                    w = []
                    if ci == 0:
                        if setk == 0:
                            if pi == 0:
                                w.append(("dv", Z["pt_done"][mi]))
                        else:
                            w.append(("dv", Z["gadd"][(mi, 0, bnk)]))
                        if mi == 0 and setk == 0 and pi == 0:
                            w.append(("wcq0", Z["wd_cur"]))
                    last = pe(lambda e, mi=mi, ci=ci, bnk=bnk, ch0=ch0:
                              e.matmul(PS[bnk][:, 0:512],
                                       hsT[mi % 2][:, ci*128:(ci+1)*128],
                                       wdbuf[ci][:, ch0:ch0+512],
                                       start=(ci == 0), stop=(ci == 7)), waits=tuple(w))
            Z["dn_last"] = last

        def emit_gadd(bp, mi, setk):
            pair = (4, 5) if mi % 2 == 0 else (6, 7)
            b_ = bp % NB
            ticks = {}
            for pi in range(2):
                bnk = pair[pi]
                e0 = setk*1024 + pi*512
                w = []
                if pi == 0:
                    w.append(("pe", Z["dn_last"]))
                if b_ == 0 and setk == 0 and pi == 0 and (mi in Z["a_g"]):
                    w.append(("ac", Z["a_g"][mi]))
                if b_ == 0:
                    t = dv(lambda e, mi=mi, e0=e0, bnk=bnk: e.tensor_copy(
                        acc[:, mi*2048 + e0: mi*2048 + e0 + 512], PS[bnk][:, 0:512]), waits=tuple(w))
                else:
                    t = dv(lambda e, mi=mi, e0=e0, bnk=bnk: e.tensor_tensor(
                        acc[:, mi*2048 + e0: mi*2048 + e0 + 512],
                        acc[:, mi*2048 + e0: mi*2048 + e0 + 512], PS[bnk][:, 0:512], ALU.add),
                        waits=tuple(w))
                Z["gadd"][(mi, setk, bnk)] = t
                ticks[pi] = t
            if setk == 1:
                for pi in range(2):
                    Z["pair_entry"][(pair, pi)] = [("dv", ticks[pi])]

        def emit_spill(p_i):
            for mi in range(NMIC):
                mt = p_i * NMIC + mi
                pair = (4, 5) if mi % 2 == 0 else (6, 7)
                w = [("dv", Z["gadd"][(mi, 1, pair[1])])]
                if Z["og_last"] is not None:
                    w.append(("oq", Z["og_last"]))
                a_g = ac(lambda e, mi=mi: e.activation(gstage[:], acc[:, mi*2048:(mi+1)*2048], AF.Silu),
                         waits=tuple(w))
                og = oq(lambda e, mt=mt: e.dma_start(out=g_scr[mt*128:(mt+1)*128, :], in_=gstage[:]),
                        waits=(("ac", a_g),))
                Z["a_g"][mi] = a_g
                Z["og_last"] = og

        def emit_down_pos(bp, s):
            if s == 0:
                emit_sp(bp, 0); emit_PT(bp, 0); emit_sp(bp, 1)
            elif s == 1:
                emit_dn(bp, 0, 0); emit_gadd(bp, 0, 0); emit_PT(bp, 1); emit_sp(bp, 2)
            elif s == 2:
                emit_dn(bp, 0, 1); emit_gadd(bp, 0, 1)
                emit_dn(bp, 1, 0); emit_gadd(bp, 1, 0)
            elif s == 3:
                emit_PT(bp, 2); emit_sp(bp, 3); emit_dn(bp, 1, 1); emit_gadd(bp, 1, 1)
            elif s == 4:
                emit_dn(bp, 2, 0); emit_gadd(bp, 2, 0); emit_PT(bp, 3)
            elif s == 5:
                emit_dn(bp, 2, 1); emit_gadd(bp, 2, 1)
                emit_dn(bp, 3, 0); emit_gadd(bp, 3, 0)
            elif s == 6:
                emit_dn(bp, 3, 1); emit_gadd(bp, 3, 1)
                if bp % NB == NB - 1:
                    emit_spill(bp // NB)

        DPRV = None
        slice_ctr = 0
        for bat in range(NBAT):
            p_i, b = divmod(bat, NB)
            m0 = p_i * NMIC
            hb = h_all[bat % 2]
            last_h = None
            for s in range(8):
                m, c = s // 2, s % 2
                slot = slice_ctr % SLOTS
                gsem = gsems[slot]
                c0 = m*2048 + 256*b + c*128
                w = [("pe", pend_gu[slot])] if slot in pend_gu else []
                gq(lambda e, slot=slot, c0=c0: e.dma_start(
                    out=wgbuf[slot].ap().rearrange("p (j i) -> p j i", i=128),
                    in_=wg_r[:, :, c0:c0+128]), gsem, waits=tuple(w))
                lt = gq(lambda e, slot=slot, c0=c0: e.dma_start(
                    out=wubuf[slot].ap().rearrange("p (j i) -> p j i", i=128),
                    in_=wu_r[:, :, c0:c0+128]), gsem)
                for mi in range(NMIC):
                    mt = m0 + mi
                    gbank = (mi % 2) * 2
                    ubank = gbank + 1
                    for j in range(NJ):
                        w = []
                        if j == 0:
                            if mi == 0:
                                w.append((gsem, lt))
                            w += guards.pop((gbank, c), [])
                        pe(lambda e, c=c, j=j, mt=mt, slot=slot, gbank=gbank:
                           e.matmul(PS[gbank][:, c*128:(c+1)*128],
                                    A1[:, j*2048 + mt*128: j*2048 + (mt+1)*128],
                                    wgbuf[slot][:, j*128:(j+1)*128],
                                    start=(j == 0), stop=(j == NJ - 1)), waits=tuple(w))
                        w2 = tuple(guards.pop((ubank, c), [])) if j == 0 else ()
                        gu_mm = pe(lambda e, c=c, j=j, mt=mt, slot=slot, ubank=ubank:
                                   e.matmul(PS[ubank][:, c*128:(c+1)*128],
                                            A1[:, j*2048 + mt*128: j*2048 + (mt+1)*128],
                                            wubuf[slot][:, j*128:(j+1)*128],
                                            start=(j == 0), stop=(j == NJ - 1)), waits=w2)
                    w_si = [("pe", gu_mm)]
                    if (mi % 2) in Z["tmpg_last"]:
                        w_si.append(("dv", Z["tmpg_last"][mi % 2]))
                    a_si = ac(lambda e, c=c, gbank=gbank, mi=mi: e.activation(
                        tmpg[mi % 2][:], PS[gbank][:, c*128:(c+1)*128], AF.Silu), waits=tuple(w_si))
                    hslice = hb[:, mi*1024 + m*256 + c*128: mi*1024 + m*256 + (c+1)*128]
                    last_h = dv(lambda e, hslice=hslice, c=c, ubank=ubank, mi=mi:
                                e.tensor_tensor(hslice, tmpg[mi % 2][:],
                                                PS[ubank][:, c*128:(c+1)*128], ALU.mult),
                                waits=(("ac", a_si),))
                    Z["tmpg_last"][mi % 2] = last_h
                    guards[(gbank, c)] = [("ac", a_si)]
                    guards[(ubank, c)] = [("dv", last_h)]
                pend_gu[slot] = gu_mm
                slice_ctr += 1
                if DPRV is not None:
                    emit_down_pos(DPRV, s)
                if s == 6:
                    wt = None
                    for ci in range(8):
                        row = 16 * (ci // 2) + 2 * b + (ci % 2)
                        w = [("pe", Z["dn_last"])] if Z["dn_last"] is not None else []
                        wt = yq(lambda e, ci=ci, row=row: e.dma_start(
                            out=wdbuf[ci][:], in_=wd_bf[row*128:(row+1)*128, :]),
                            "wcq0", waits=tuple(w))
                    Z["wd_cur"] = wt
            Z["h_done"][bat % 2] = last_h
            DPRV = bat
        for s in range(8):
            emit_down_pos(DPRV, s)

        if "g" in dbg:
            ov = P.c("oq")
            for mt in range(NTILE):
                lw = yq(lambda e, mt=mt: e.dma_start(out=gstage[:], in_=g_scr[mt*128:(mt+1)*128, :]),
                        "xq0", waits=(("oq", ov),))
                ov = oq(lambda e, mt=mt: e.dma_start(out=dbg_out["d_g"][mt*128:(mt+1)*128, :], in_=gstage[:]),
                        waits=(("xq0", lw),))
        barrier()
        p1stk.close()

    if ("p2" in phases) or ("p3" in phases):
        k2stk = stk.enter_context(ExitStack())
        kp3 = sb(k2stk, "kpack2", [128, S], BF16)
        BK3 = sb(k2stk, "BK_sb2", [128, 16 * 384], BF16)
        yq(lambda e: e.dma_start(out=kp3[:], in_=kp_scr), "cq")
        rl2 = yq(lambda e: e.dma_start(out=BK3[:], in_=bk_scr), "cq")

    if "p2" in phases:
        p2stk = stk.enter_context(ExitStack())
        wqh = [sb(p2stk, f"wqh{i}", [128, EMB], BF16) for i in range(2)]
        wqT = sb(p2stk, "wqT", [128, EMB], BF16)
        qs_col = sb(p2stk, "qs_col", [128, 16], F32)
        qsT = sb(p2stk, "qsT", [16, 128], F32)
        qtmax = sb(p2stk, "qtmax", [16, 1], F32)
        qtmin = sb(p2stk, "qtmin", [16, 1], F32)
        tb1 = sb(p2stk, "tb1", [16, 1], F32)
        tb2 = sb(p2stk, "tb2", [16, 1], F32)
        negc_row = sb(p2stk, "negc_row", [1, 16], F32)
        guards2 = {}
        pend_wq = {}
        pend_wqT = None
        for h in range(NHEADS):
            slot = h % 2
            gsem = "wgq0" if slot == 0 else "wgq1"
            w = [("pe", pend_wq[slot])] if slot in pend_wq else []
            lw = gq(lambda e, h=h, slot=slot: e.dma_start(out=wqh[slot][:], in_=Wq[h*128:(h+1)*128, :]),
                    gsem, waits=tuple(w))
            tevq = []
            for j in range(NJ):
                bnk = j % 2
                w = [(gsem, lw)] if j == 0 else []
                w += guards2.pop(("t", bnk), [])
                if pend_wqT is not None and j < 2:
                    w.append(("pe", pend_wqT))
                tk = pe(lambda e, j=j, bnk=bnk, slot=slot: e.transpose(
                    PSb[bnk][:, 0:128], wqh[slot][:, j*128:(j+1)*128], ident_sb[:]), waits=tuple(w))
                tevq.append(dv(lambda e, j=j, bnk=bnk: e.tensor_copy(wqT[:, j*128:(j+1)*128], PSb[bnk][:, 0:128]),
                               waits=(("pe", tk),)))
                guards2[("t", bnk)] = [("dv", tevq[-1])]
            for tc in range(4):
                bnk = 2 + tc % 2
                for j in range(NJ):
                    w = []
                    if j == 0:
                        w += guards2.pop(("q", bnk), []) + [("dv", tevq[-1])]
                    q_mm = pe(lambda e, j=j, tc=tc, h=h, bnk=bnk: e.matmul(
                        PS[bnk][:, 0:512], wqT[:, j*128:(j+1)*128],
                        A1[:, j*2048 + tc*512: j*2048 + (tc+1)*512],
                        start=(j == 0), stop=(j == NJ - 1)), waits=tuple(w))
                ev1 = dv(lambda e, tc=tc, h=h, bnk=bnk: e.tensor_copy(
                    A2[:, h*2048 + tc*512: h*2048 + (tc+1)*512],
                    PS[bnk][:, 0:512]), waits=(("pe", q_mm),))
                guards2[("q", bnk)] = [("dv", ev1)]
                pend_wqT = q_mm
            pend_wq[slot] = q_mm
            for mt in range(NTILE):
                w = [("dv", P.c("dv"))] if mt == 0 else []
                if mt == 0:
                    w += guards2.pop(("qs",), [])
                qs_mm = pe(lambda e, mt=mt, h=h: e.matmul(
                    PS[4][:, mt:mt+1], A2[:, h*2048 + mt*128: h*2048 + (mt+1)*128],
                    q2lrs[:, h:h+1], start=True, stop=True), waits=tuple(w))
            d_qs = dv(lambda e: e.tensor_copy(qs_col[:], PS[4][:, 0:16]), waits=(("pe", qs_mm),))
            guards2[("qs",)] = [("dv", d_qs)]
            dv(lambda e, h=h: e.tensor_scalar(qsb[:, h*16:(h+1)*16], qs_col[:], 0.0625, None, ALU.mult))
            d_q16 = P.c("dv")
            dv(lambda e, h=h: e.tensor_tensor(tnc[:], qsb[:, h*16:(h+1)*16], cmax_bc[:], ALU.mult))
            dv(lambda e, h=h: e.tensor_tensor(tnc2[:], qsb[:, h*16:(h+1)*16], cmin_bc[:], ALU.mult))
            dv(lambda e: e.tensor_tensor(tnc[:], tnc[:], tnc2[:], ALU.max))
            dv(lambda e, h=h: e.tensor_scalar(negc_bc[:, h*16:(h+1)*16], tnc[:], -1.0, -10.0,
                                              ALU.mult, ALU.add))
        if "q" in dbg:
            dq = P.c("dv")
            oq(lambda e: e.dma_start(out=dbg_out["d_q"][:, 0:NHEADS*2048], in_=A2[:, 0:NHEADS*2048]), waits=(("dv", dq),))
            oq(lambda e: e.dma_start(out=dbg_out["d_qsb"][:, 0:NHEADS*16], in_=qsb[:, 0:NHEADS*16]))
            oq(lambda e: e.dma_start(out=dbg_out["d_negc"][:, 0:NHEADS*16], in_=negc_bc[:, 0:NHEADS*16]))
        barrier()
        p2stk.close()

    if "p3" in phases:
        p3stk = stk.enter_context(ExitStack())
        A_sb = sb(p3stk, "A_sb", [128, S], BF16)
        Vx_sb = sb(p3stk, "Vx_sb", [128, S], BF16)
        probs = [sb(p3stk, f"probs{i}", [128, S], BF16) for i in range(2)]
        pts_all = sb(p3stk, "pts_all", [128, S], BF16)
        ctxt_f = sb(p3stk, "ctxt_f", [128, 128], F32)
        g3 = {("bk", 0): [("cq", rl2)], ("bk", 1): [("cq", rl2)]}
        for h in range(NHEADS):
            for grp in range(8):
                bnk = grp % 2
                w = g3.pop(("av", bnk), [])
                if h == 0 and grp < 2:
                    w += g3.pop(("bk", bnk), [])
                av_mm = pe(lambda e, h=h, grp=grp, bnk=bnk: e.matmul(
                    PS[bnk][:, 0:512], q2lt[0:64, h*128:(h+1)*128],
                    kp3[0:64, grp*512:(grp+1)*512], start=True, stop=True), waits=tuple(w))
                d_av = dv(lambda e, grp=grp, bnk=bnk: e.tensor_scalar(
                    A_sb[:, grp*512:(grp+1)*512], PS[bnk][:], 0.125, None, ALU.mult),
                    waits=(("pe", av_mm),))
                g3[("av", bnk)] = [("dv", d_av)]
            for grp in range(8):
                bnk = grp % 2
                for cc in range(4):
                    c = grp * 4 + cc
                    w = g3.pop(("av", bnk), []) if cc == 0 else []
                    av_mm = pe(lambda e, h=h, c=c, cc=cc, bnk=bnk: e.matmul(
                        PS[bnk][:, cc*128:(cc+1)*128], kp3[64:128, c*128:(c+1)*128],
                        vup_sb[64:128, h*128:(h+1)*128], start=(cc == 0), stop=(cc == 3)), waits=tuple(w))
                d_av = dv(lambda e, grp=grp, bnk=bnk: e.tensor_copy(
                    Vx_sb[:, grp*512:(grp+1)*512], PS[bnk][:]), waits=(("pe", av_mm),))
                g3[("av", bnk)] = [("dv", d_av)]
            def emit_scores_mm(h, j):
                E = EXT[j]
                ngrp = (E + 3) // 4
                pp = j % 2
                last_ex = None
                for grp in range(ngrp):
                    bnk = 2 + grp % 2
                    nc_in_g = min(4, E - grp * 4)
                    w = g3.pop(("sc", bnk), [])
                    if grp == 0:
                        w += g3.pop(("ptw", pp), [])
                        w += g3.pop(("sg", pp), [])
                    sc_mm = pe(lambda e, h=h, j=j, grp=grp, bnk=bnk, nc_in_g=nc_in_g: e.matmul(
                        PS[bnk][:, 0:nc_in_g*128],
                        A2[:, h*2048 + j*128:(h*2048 + (j+1)*128)],
                        A_sb[:, grp*512:grp*512 + nc_in_g*128], start=True, stop=True),
                        waits=tuple(w))
                    lo_g, hi_g = grp * 4, grp * 4 + nc_in_g
                    e0 = max(0, E - 3)
                    blo, bhi = max(lo_g, e0), hi_g
                    dvt = None
                    if blo < bhi:
                        woff3 = (3 - (E - e0)) * 128
                        dvt = dv(lambda e, h=h, j=j, blo=blo, bhi=bhi, lo_g=lo_g, e0=e0, woff3=woff3, grp=grp:
                                 e.scalar_tensor_tensor(
                                     PS[2 + grp % 2][:, (blo-lo_g)*128:(bhi-lo_g)*128],
                                     BK3[:, j*384 + woff3 + (blo-e0)*128: j*384 + woff3 + (bhi-e0)*128],
                                     qsb[:, h*16+j:h*16+j+1],
                                     PS[2 + grp % 2][:, (blo-lo_g)*128:(bhi-lo_g)*128],
                                     ALU.mult, ALU.add), waits=(("pe", sc_mm),))
                    if hi_g == E:
                        dvt = dv(lambda e, j=j, nc_in_g=nc_in_g, grp=grp: e.tensor_tensor(
                            PS[2 + grp % 2][:, (nc_in_g-2)*128: nc_in_g*128],
                            PS[2 + grp % 2][:, (nc_in_g-2)*128: nc_in_g*128],
                            caus2_sb[:], ALU.add),
                            waits=(("pe", sc_mm),) if dvt is None else ())
                    if j == 15:
                        dvt = dvr(lambda e, nc_in_g=nc_in_g, grp=grp: e.tensor_reduce(
                            negc15[:], PS[2 + grp % 2][:, 0:nc_in_g*128], AX.X, ALU.max),
                            waits=(("dv", dvt),) if dvt is not None else (("pe", sc_mm),))
                        dvt = dv(lambda e: e.tensor_scalar(negc15[:], negc15[:], -1.0, None, ALU.mult))
                        bias_ap = negc15
                    else:
                        bias_ap = None
                    w_exp = [("pe", sc_mm)] if dvt is None else [("dv", dvt)]
                    a_ex = ac(lambda e, h=h, j=j, grp=grp, nc_in_g=nc_in_g, bias_ap=bias_ap, pp=pp:
                              e.activation(
                                  probs[pp][:, grp*512: grp*512 + nc_in_g*128],
                                  PS[2 + grp % 2][:, 0:nc_in_g*128],
                                  AF.Exp, bias=(bias_ap[:] if bias_ap is not None else
                                                negc_bc[:, h*16+j:h*16+j+1]),
                                  accum_out=sumg[pp][:, grp:grp+1]),
                              waits=tuple(w_exp))
                    last_ex = a_ex
                    g3[("sc", bnk)] = [("ac", a_ex)] + ([("dv", dvt)] if dvt is not None else [])
                return last_ex

            def emit_rtail(h, j, a_last):
                E = EXT[j]
                ngrp = (E + 3) // 4
                pp = j % 2
                rd = dvr(lambda e, ngrp=ngrp, pp=pp: e.tensor_reduce(sumtot[pp][:], sumg[pp][:, 0:ngrp],
                                                                     AX.X, ALU.add),
                         waits=(("ac", a_last),))
                g3[("sg", pp)] = [("dv", rd)]
                ac(lambda e, pp=pp: e.activation(sumtot[pp][:], sumtot[pp][:], AF.Ln),
                   waits=(("dv", rd),))
                w_r = g3.pop(("rctx", pp), [])
                a_r = ac(lambda e, pp=pp: e.activation(rinv[pp][:], sumtot[pp][:], AF.Exp, scale=-1.0),
                         waits=tuple(w_r))
                g3[("rinv", pp)] = a_r

            def emit_pt(h, j, a_last):
                E = EXT[j]
                pp = j % 2
                last_tk = None
                d_pt = None
                for c in range(E):
                    tbnk = 4 + (c % 4)
                    w = list(g3.get(("pt", c % 4), []))
                    if c < 4:
                        w.append(("ac", a_last))
                    if c in (2, 3):
                        w += g3.get(("rctx", 1 - pp), [])
                    tk = pe(lambda e, c=c, tbnk=tbnk, pp=pp: e.transpose(
                        PSb[tbnk][:, 0:128], probs[pp][:, c*128:(c+1)*128], ident_sb[:]), waits=tuple(w))
                    d_pt = dv(lambda e, c=c, tbnk=tbnk: e.tensor_copy(pts_all[:, c*128:(c+1)*128],
                                                                      PSb[tbnk][:, 0:128]),
                              waits=(("pe", tk),))
                    g3[("pt", c % 4)] = [("dv", d_pt)]
                    last_tk = tk
                g3[("ptw", pp)] = [("pe", last_tk)]
                g3[("ptd", pp)] = d_pt

            def emit_pvmm(h, j):
                E = EXT[j]
                pp = j % 2
                R = 256 + pp * 128
                d_pt = g3.pop(("ptd", pp))
                lastmm = None
                for c in range(E):
                    bnkp = 6 + (c % 2)
                    w = ()
                    if c < 2:
                        w = tuple(g3.pop(("ctx", bnkp, R), []) + [("dv", d_pt)])
                    lastmm = pe(lambda e, c=c, bnkp=bnkp, R=R, E=E: e.matmul(
                        PS[bnkp][:, R:R+128], pts_all[:, c*128:(c+1)*128], Vx_sb[:, c*128:(c+1)*128],
                        start=(c < 2), stop=(c >= E - 2)), waits=w)
                a_r = g3[("rinv", pp)]
                dv(lambda e, R=R, pp=pp: e.tensor_scalar(ctxt_f[:], PS[7][:, R:R+128], rinv[pp][:],
                                                         None, ALU.mult),
                   waits=(("pe", lastmm), ("ac", a_r)))
                d_ctx = dv(lambda e, h=h, j=j, pp=pp, R=R: e.scalar_tensor_tensor(
                    A1[:, h*2048 + j*128: h*2048 + (j+1)*128], PS[6][:, R:R+128], rinv[pp][:],
                    ctxt_f[:], ALU.mult, ALU.add))
                g3[("ctx", 6, R)] = [("dv", d_ctx)]
                g3[("ctx", 7, R)] = [("dv", d_ctx)]
                g3[("rctx", pp)] = [("dv", d_ctx)]

            pend = None
            for j in range(NSLOTS):
                ax_j = emit_scores_mm(h, j)
                if pend is not None:
                    emit_pt(h, pend[0], pend[1])
                emit_rtail(h, j, ax_j)
                if pend is not None:
                    emit_pvmm(h, pend[0])
                pend = (j, ax_j)
            emit_pt(h, pend[0], pend[1])
            emit_pvmm(h, pend[0])

        if "ctx" in dbg:
            dq = P.c("dv")
            oq(lambda e: e.dma_start(out=dbg_out["d_ctx"][:, 0:NHEADS*2048], in_=A1[:, 0:NHEADS*2048]), waits=(("dv", dq),))
        barrier()
        p3stk.close()

    if ("p2" in phases) or ("p3" in phases):
        k2stk.close()

    if "p4" in phases:
        p4stk = stk.enter_context(ExitStack())
        woh = [sb(p4stk, f"woh{i}", [128, EMB], BF16) for i in range(2)]
        gbuf = [sb(p4stk, f"gbuf{i}", [128, EMB], BF16) for i in range(2)]
        xbuf = [sb(p4stk, f"xbuf{i}", [128, EMB], F32) for i in range(2)]
        obuf = [sb(p4stk, f"obuf{i}", [128, EMB], F32) for i in range(2)]
        tmp4 = sb(p4stk, "tmp4", [128, 512], F32)
        cstage = [sb(p4stk, f"cst{i}", [128, EMB], BF16) for i in range(2)]
        g4 = {}
        for ec in range(16):
            slot = ec % 2
            gsem = "wgq0" if slot == 0 else "wgq1"
            w = g4.pop(("wo", slot), [])
            lw = gq(lambda e, ec=ec, slot=slot: e.dma_start(out=woh[slot][:], in_=Wo[ec*128:(ec+1)*128, :]),
                    gsem, waits=tuple(w))
            last = None
            for hh in range(16):
                bnk = hh % 2
                w = [(gsem, lw)] if hh == 0 else []
                w += g4.pop(("t", bnk), [])
                tk = pe(lambda e, hh=hh, ec=ec, bnk=bnk, slot=slot: e.transpose(
                    PSb[bnk][:, 0:128], woh[slot][:, hh*128:(hh+1)*128], ident_sb[:]), waits=tuple(w))
                last = dv(lambda e, hh=hh, ec=ec, bnk=bnk: e.tensor_copy(
                    A2[:, hh*2048 + ec*128: hh*2048 + (ec+1)*128], PSb[bnk][:, 0:128]),
                    waits=(("pe", tk),))
                g4[("t", bnk)] = [("dv", last)]
            g4[("wo", slot)] = [("pe", tk)]
        d_wot = P.c("dv")
        for mt in range(NTILE):
            pbuf = mt % 2
            w = g4.pop(("g", pbuf), [])
            lg = yq(lambda e, mt=mt, pbuf=pbuf: e.dma_start(out=gbuf[pbuf][:], in_=g_scr[mt*128:(mt+1)*128, :]),
                    "xq0" if pbuf == 0 else "xq1", waits=tuple(w))
            lx = yq(lambda e, mt=mt, pbuf=pbuf: e.dma_start(out=xbuf[pbuf][:], in_=x_own[mt*128:(mt+1)*128, :]),
                    "xq0" if pbuf == 0 else "xq1")
            d_cst = None
            for hh in range(16):
                bnk = hh % 2
                w = g4.pop(("t", bnk), [])
                if hh < 2 and mt == 0:
                    w += [("dv", d_wot)]
                tk = pe(lambda e, hh=hh, mt=mt, bnk=bnk: e.transpose(
                    PSb[bnk][:, 0:128], A1[:, hh*2048 + mt*128: hh*2048 + (mt+1)*128],
                    ident_sb[:]), waits=tuple(w))
                d_cst = dv(lambda e, hh=hh, bnk=bnk, pbuf=pbuf: e.tensor_copy(
                    cstage[pbuf][:, hh*128:(hh+1)*128], PSb[bnk][:, 0:128]),
                    waits=(("pe", tk),))
                g4[("t", bnk)] = [("dv", d_cst)]
            for half in range(2):
                for bo in range(2):
                    bnk = 2 + bo
                    e0c = half * 1024 + bo * 512
                    for hh in range(16):
                        w = []
                        if hh == 0:
                            w = g4.pop(("s", bnk), [])
                            if bo == 0 and half == 0:
                                w += [("dv", d_cst)]
                                if mt == 0:
                                    w += [("dv", d_wot)]
                        s_mm = pe(lambda e, hh=hh, bnk=bnk, e0c=e0c, mt=mt, pbuf=pbuf: e.matmul(
                            PS[bnk][:, 0:512],
                            cstage[pbuf][:, hh*128:(hh+1)*128],
                            A2[:, hh*2048 + e0c: hh*2048 + e0c + 512],
                            start=(hh == 0), stop=(hh == 15)), waits=tuple(w))
                    g4[("smm", bnk)] = s_mm
                for bo in range(2):
                    bnk = 2 + bo
                    e0c = half * 1024 + bo * 512
                    w = [("pe", g4[("smm", bnk)]), ((("xq0" if pbuf == 0 else "xq1"), lx))]
                    dv(lambda e, bnk=bnk, e0c=e0c, pbuf=pbuf: e.tensor_tensor(
                        tmp4[:], PS[bnk][:], gbuf[pbuf][:, e0c:e0c+512], ALU.mult), waits=tuple(w))
                    d_cb = dv(lambda e, bnk=bnk, e0c=e0c, pbuf=pbuf: e.tensor_tensor(
                        obuf[pbuf][:, e0c:e0c+512], tmp4[:], xbuf[pbuf][:, e0c:e0c+512], ALU.add))
                    g4[("s", bnk)] = [("dv", d_cb)]
            so = oq(lambda e, mt=mt, pbuf=pbuf: e.dma_start(out=out[mt*128:(mt+1)*128, :], in_=obuf[pbuf][:]),
                    waits=(("dv", P.c("dv")),))
            g4[("g", pbuf)] = [("oq", so)]

    for s_ in ("oq", "xq0", "xq1", "cq", "qq", "vq", "kvq", "wgq0", "wgq1", "wgq2", "wdq", "wcq0", "wcq1"):
        P.op("y", lambda e, s_=s_: e.wait_ge(P.sems[s_], P.c(s_)))

    P.emit()
    return nc, dbg_out



import numpy as np
import ml_dtypes

def perm_idx():
    p = np.arange(8192)
    return 4 * (p % 2048) + p // 2048

def make_consts():
    i_, j_ = np.mgrid[0:128, 0:128]
    m_prev = (j_ >= i_ + 64).astype(np.float32)
    m_diag = ((i_ - j_ >= 0) & (i_ - j_ <= 64)).astype(np.float32)
    caus = np.where(j_ <= i_, 0.0, -1.0e30).astype(np.float32)
    z = np.zeros((128, 128), np.float32)
    negf = np.full((128, 128), -1.0e30, np.float32)
    band3 = {
        0: np.concatenate([z, m_prev, m_diag], axis=1),
        1: np.concatenate([m_prev, m_diag, z], axis=1),
    }
    caus2 = {
        0: np.concatenate([z, caus], axis=1),
        1: np.concatenate([caus, negf], axis=1),
    }
    ident = np.eye(128, dtype=ml_dtypes.bfloat16)
    ones_f = np.ones((1, 128), np.float32)
    extras = {
        "ones_b": np.ones((1, 128), ml_dtypes.bfloat16),
        "identf": np.eye(128, dtype=np.float32),
        "ident16": np.eye(16, dtype=np.float32),
    }
    return band3, caus2, ident, ones_f, extras

def own_tiles(role):
    if role == 0:
        return [31 - 2 * k for k in range(16)]
    return [30 - 2 * k for k in range(16)]

def core_inputs(core, x, Wq, Wk, Wv, q2l, vup, Wo, wg, wu, wd):
    b, role = core // 2, core % 2
    band3, caus2, ident, ones_f, extras = make_consts()
    idx = perm_idx()
    tiles = own_tiles(role)
    xs = np.ascontiguousarray(x[b])
    xo = np.ascontiguousarray(
        xs.reshape(32, 128, 2048)[tiles].reshape(2048, 2048))
    return {
        "x_seq": xs, "x_own": xo,
        "Wq": Wq, "Wk": Wk, "Wv": Wv, "q2l": q2l, "vup": vup, "Wo": Wo,
        "wg_p": np.ascontiguousarray(wg[:, idx]),
        "wu_p": np.ascontiguousarray(wu[:, idx]),
        "wd_p": np.ascontiguousarray(wd[idx, :]),
        "ident": ident, "band3": band3[role], "caus2": caus2[role],
        "ones_f": ones_f, **extras,
    }



_PROG_CACHE = {}

def kernel(L0_attn_hidden_states, L0_attn_Wq, L0_attn_Wk, L0_attn_Wv, L0_attn_q_to_latent,
           L0_attn_v_up, L0_attn_Wo, L0_ffn_w_gate, L0_ffn_w_up, L0_ffn_w_down):
    x = np.ascontiguousarray(np.asarray(L0_attn_hidden_states, dtype=np.float32))
    args = (np.ascontiguousarray(np.asarray(L0_attn_Wq, dtype=np.float32)),
            np.ascontiguousarray(np.asarray(L0_attn_Wk, dtype=np.float32)),
            np.ascontiguousarray(np.asarray(L0_attn_Wv, dtype=np.float32)),
            np.ascontiguousarray(np.asarray(L0_attn_q_to_latent, dtype=np.float32)),
            np.ascontiguousarray(np.asarray(L0_attn_v_up, dtype=np.float32)),
            np.ascontiguousarray(np.asarray(L0_attn_Wo, dtype=np.float32)),
            np.ascontiguousarray(np.asarray(L0_ffn_w_gate, dtype=np.float32)),
            np.ascontiguousarray(np.asarray(L0_ffn_w_up, dtype=np.float32)),
            np.ascontiguousarray(np.asarray(L0_ffn_w_down, dtype=np.float32)))
    if "nc" not in _PROG_CACHE:
        nc, _ = build_program(dbg=(), phases=("p0", "p1", "p2", "p3", "p4"))
        _PROG_CACHE["nc"] = nc
    nc = _PROG_CACHE["nc"]
    in_maps = [core_inputs(c, x, *args) for c in range(8)]
    res = run_bass_kernel_spmd(nc, in_maps, core_ids=list(range(8)))
    out = np.empty((4, 4096, 2048), dtype=np.float32)
    for c in range(8):
        b, role = c // 2, c % 2
        o = np.asarray(res.results[c]["out"]).reshape(16, 128, 2048)
        for k, tau in enumerate(own_tiles(role)):
            out[b, tau*128:(tau+1)*128, :] = o[k]
    return out


# version 17
# speedup vs baseline: 1.2884x; 1.0191x over previous
import numpy as np
import ml_dtypes
from concourse.bass_utils import run_bass_kernel_spmd

import numpy as np
import concourse.bass as bass
import concourse.mybir as mybir
from contextlib import ExitStack

F32 = mybir.dt.float32
BF16 = mybir.dt.bfloat16
AF = mybir.ActivationFunctionType
ALU = mybir.AluOpType
AX = mybir.AxisListType

H, D, L, EMB, II, S, TC = 16, 128, 64, 2048, 8192, 4096, 2048
NTILE = 16
NSEQT = 32
NJ = EMB // 128
EXT = [32 - 2 * j for j in range(16)]
NEG = -1.0e30
NPASS = 4
NHEADS = 16
NSLOTS = 16
NMIC = 4
NB = 8
MW = 256


class Prog:
    ENGS = ("t", "v", "s", "g", "y")

    def __init__(self, nc, stk):
        self.nc, self.stk = nc, stk
        self.q = {e: [] for e in self.ENGS}
        self.sems, self.cnt = {}, {}

    def sem(self, name):
        self.sems[name] = self.stk.enter_context(self.nc.semaphore(name))
        self.cnt[name] = 0

    import os
    CHAIN = {"v": "dv", "s": "ac"}
    if os.environ.get("KB_CHAIN_PE"):
        CHAIN = {"v": "dv", "s": "ac", "t": "pe"}

    def op(self, eng, fn, waits=(), inc=None, incby=1):
        waits = list(waits)
        ch = self.CHAIN.get(eng)
        if ch is not None and inc == ch and self.cnt[ch] > 0:
            waits.append((ch, self.cnt[ch]))
        val = None
        if inc is not None:
            self.cnt[inc] += incby
            val = self.cnt[inc]
        sems = self.sems
        wl = tuple((sems[s], v) for s, v in waits)
        s_inc = sems[inc] if inc is not None else None

        def run(e, fn=fn, wl=wl, s_inc=s_inc, incby=incby):
            for s, v in wl:
                e.wait_ge(s, v)
            inst = fn(e)
            if s_inc is not None:
                if inst is None:
                    inst = e.nop()
                inst.then_inc(s_inc, incby)
        self.q[eng].append(run)
        return val

    def c(self, name):
        return self.cnt[name]

    def emit(self):
        nc = self.nc
        with nc.Block() as block:
            @block.tensor
            def _(t):
                for f in self.q["t"]:
                    f(t)

            @block.vector
            def _(v):
                for f in self.q["v"]:
                    f(v)

            @block.scalar
            def _(s):
                for f in self.q["s"]:
                    f(s)

            @block.gpsimd
            def _(g):
                for f in self.q["g"]:
                    f(g)

            @block.sync
            def _(y):
                for f in self.q["y"]:
                    f(y)


def build_program(dbg=(), phases=("p0", "p1", "p2", "p3", "p4")):
    import os
    nc = bass.Bass("TRN2", target_bir_lowering=False,
                   detect_race_conditions=not os.environ.get("KB_NORACE"))
    stk = ExitStack()

    def din(name, shape, dt=F32):
        return nc.dram_tensor(name, shape, dt, kind="ExternalInput").ap()

    x_seq = din("x_seq", [S, EMB])
    x_own = din("x_own", [TC, EMB])
    Wq = din("Wq", [EMB, EMB])
    Wk = din("Wk", [L, EMB])
    Wv = din("Wv", [L, EMB])
    q2l = din("q2l", [H, D, L])
    vup = din("vup", [H, L, D])
    Wo = din("Wo", [EMB, EMB])
    wg_p = din("wg_p", [EMB, II])
    wu_p = din("wu_p", [EMB, II])
    wd_p = din("wd_p", [II, EMB])
    ident = din("ident", [128, 128], BF16)
    band3 = din("band3", [128, 384])
    caus2 = din("caus2", [128, 256])
    ones_f = din("ones_f", [1, 128])
    ones_b = din("ones_b", [1, 128], BF16)
    identf = din("identf", [128, 128])
    ident16 = din("ident16", [16, 16])

    out = nc.dram_tensor("out", [TC, EMB], F32, kind="ExternalOutput").ap()
    g_scr = nc.dram_tensor("g_scr", [TC, EMB], BF16, kind="Internal").ap()
    kp_scr = nc.dram_tensor("kp_scr", [128, S], BF16, kind="Internal").ap()
    bk_scr = nc.dram_tensor("bk_scr", [128, 16 * 384], BF16, kind="Internal").ap()
    wd_bf = nc.dram_tensor("wd_bf", [II, EMB], BF16, kind="Internal").ap()
    dbg_out = {}
    if "kpack" in dbg:
        dbg_out["d_kpack"] = nc.dram_tensor("d_kpack", [128, S], BF16, kind="ExternalOutput").ap()
        dbg_out["d_ksum"] = nc.dram_tensor("d_ksum", [1, S], BF16, kind="ExternalOutput").ap()
        dbg_out["d_xn"] = nc.dram_tensor("d_xn", [128, EMB], BF16, kind="ExternalOutput").ap()
    if "g" in dbg:
        dbg_out["d_g"] = nc.dram_tensor("d_g", [TC, EMB], BF16, kind="ExternalOutput").ap()
    if "q" in dbg:
        dbg_out["d_q"] = nc.dram_tensor("d_q", [128, H * TC], BF16, kind="ExternalOutput").ap()
        dbg_out["d_qsb"] = nc.dram_tensor("d_qsb", [128, 256], F32, kind="ExternalOutput").ap()
        dbg_out["d_negc"] = nc.dram_tensor("d_negc", [128, 256], F32, kind="ExternalOutput").ap()
    if "ctx" in dbg:
        dbg_out["d_ctx"] = nc.dram_tensor("d_ctx", [128, H * TC], BF16, kind="ExternalOutput").ap()

    pstk = stk.enter_context(ExitStack())

    def sb(st, name, shape, dt):
        return st.enter_context(nc.sbuf_tensor(name, shape, dt))

    A1 = sb(pstk, "A1", [128, 32768], BF16)
    A2 = sb(pstk, "A2", [128, 32768], BF16)
    A2f = A2.ap().bitcast(F32)

    ident_sb = sb(pstk, "ident_sb", [128, 128], BF16)
    band3_sb = sb(pstk, "band3_sb", [128, 384], F32)
    caus2_sb = sb(pstk, "caus2_sb", [128, 256], F32)
    ones_sb = sb(pstk, "ones_sb", [1, 128], F32)
    ones_bf_sb = sb(pstk, "ones_bf_sb", [1, 128], BF16)
    identf_sb = sb(pstk, "identf_sb", [128, 128], F32)
    ident16_sb = sb(pstk, "ident16_sb", [16, 16], F32)
    q2lt = sb(pstk, "q2lt", [64, H * 128], BF16)
    q2lrs = sb(pstk, "q2lrs", [128, H], BF16)
    vup_sb = sb(pstk, "vup_sb", [128, H * 128], BF16)
    qsb = sb(pstk, "qsb", [128, 256], F32)
    negc_bc = sb(pstk, "negc_bc", [128, 256], F32)
    sumg = [sb(pstk, f"sumg{i}", [128, 8], F32) for i in range(2)]
    sumtot = [sb(pstk, f"sumtot{i}", [128, 1], F32) for i in range(2)]
    rinv = [sb(pstk, f"rinv{i}", [128, 1], F32) for i in range(2)]
    negc15 = sb(pstk, "negc15", [128, 1], F32)
    cmaxr = sb(pstk, "cmaxr", [1, 16], F32)
    cminr = sb(pstk, "cminr", [1, 16], F32)
    cmax_bc = sb(pstk, "cmax_bc", [128, 16], F32)
    cmin_bc = sb(pstk, "cmin_bc", [128, 16], F32)
    tnc = sb(pstk, "tnc", [128, 16], F32)
    tnc2 = sb(pstk, "tnc2", [128, 16], F32)
    s1 = sb(pstk, "s1", [128, 1], F32)
    s2 = sb(pstk, "s2", [128, 1], F32)
    mvec = sb(pstk, "mvec", [128, 1], F32)
    msq = sb(pstk, "msq", [128, 1], F32)
    varv = sb(pstk, "varv", [128, 1], F32)
    rstd = sb(pstk, "rstd", [128, 1], F32)
    bias_ln = sb(pstk, "bias_ln", [128, 1], F32)

    PS = [pstk.enter_context(nc.psum_tensor(f"ps{i}", [128, 512], F32)) for i in range(8)]
    PSb = [p.ap().bitcast(BF16) for p in PS]

    kstk = stk.enter_context(ExitStack())
    kpack = sb(kstk, "kpack", [128, S], BF16)
    BK_sb = sb(kstk, "BK_sb", [128, 16 * 384], BF16)

    P = Prog(nc, stk)
    for s_ in ("pe", "dv", "ac", "xq0", "xq1", "cq", "qq", "vq", "kvq", "wgq0", "wgq1", "wgq2", "wdq", "oq", "wcq0", "wcq1", "bar"):
        P.sem(s_)

    def pe(fn, waits=()):
        return P.op("t", fn, waits=waits, inc="pe")

    def dv(fn, waits=()):
        return P.op("v", fn, waits=waits, inc="dv")

    def dvr(fn, waits=()):
        t = P.op("v", fn, waits=waits, inc="dv")
        P.op("v", lambda e, t=t: e.wait_ge(P.sems["dv"], t))
        return t

    def ac(fn, waits=()):
        return P.op("s", fn, waits=waits, inc="ac")

    def gq(fn, sem, waits=()):
        return P.op("g", fn, waits=waits, inc=sem, incby=16)

    def yq(fn, sem, waits=()):
        return P.op("y", fn, waits=waits, inc=sem, incby=16)

    def oq(fn, waits=()):
        return P.op("y", fn, waits=waits, inc="oq", incby=16)

    DMA_SEMS = ("xq0", "xq1", "cq", "qq", "vq", "kvq", "wgq0", "wgq1", "wgq2", "wdq", "oq", "wcq0", "wcq1")

    def barrier():
        tgt = P.cnt["bar"] + 5
        tail = {"t": ("pe",), "v": ("dv",), "s": ("ac",), "g": (), "y": DMA_SEMS}
        for e in Prog.ENGS:
            w = tuple((s_, P.c(s_)) for s_ in tail[e])
            P.op(e, lambda en: en.nop(), waits=w, inc="bar")
        for e in Prog.ENGS:
            P.op(e, lambda en, tgt=tgt: en.wait_ge(P.sems["bar"], tgt))

    yq(lambda e: e.dma_start(out=ident_sb[:], in_=ident), "cq")
    yq(lambda e: e.dma_start(out=band3_sb[:], in_=band3), "cq")
    yq(lambda e: e.dma_start(out=caus2_sb[:], in_=caus2), "cq")
    yq(lambda e: e.dma_start(out=ones_sb[:], in_=ones_f), "cq")
    yq(lambda e: e.dma_start(out=ones_bf_sb[:], in_=ones_b), "cq")
    yq(lambda e: e.dma_start(out=identf_sb[:], in_=identf), "cq")
    w_const = yq(lambda e: e.dma_start(out=ident16_sb[:], in_=ident16), "cq")

    p0stk = stk.enter_context(ExitStack())
    ksum = sb(p0stk, "ksum", [1, S], BF16)
    wkte = sb(p0stk, "wkte", [128, NJ * 65], BF16)
    wvt = sb(p0stk, "wvt", [128, NJ * 64], BF16)
    wk_bf = sb(p0stk, "wk_bf", [64, EMB], BF16)
    wv_bf = sb(p0stk, "wv_bf", [64, EMB], BF16)
    wkcs = sb(p0stk, "wkcs", [128, NJ], F32)
    gq(lambda e: e.dma_start(out=wk_bf[:], in_=Wk), "kvq")
    w_kv = gq(lambda e: e.dma_start(out=wv_bf[:], in_=Wv), "kvq")

    ev_prev = {}
    for j in range(NJ):
        b = j % 2
        w = [("kvq", w_kv), ("cq", w_const)] if j == 0 else []
        if j >= 2:
            w.append(("dv", ev_prev[j - 2]))
        tk = pe(lambda e, j=j, b=b: e.transpose(PSb[b][:, 0:64], wk_bf[0:64, j*128:(j+1)*128], ident_sb[0:64, 0:64]),
                waits=tuple(w))
        dv(lambda e, j=j, b=b: e.tensor_copy(wkte[:, j*65:j*65+64], PSb[b][:, 0:64]), waits=(("pe", tk),))
        dvr(lambda e, j=j, b=b: e.tensor_reduce(wkcs[:, j:j+1], PSb[b][:, 0:64], AX.X, ALU.add))
        ev_prev[j] = dv(lambda e, j=j: e.tensor_copy(wkte[:, j*65+64:j*65+65], wkcs[:, j:j+1]))
    for j in range(NJ):
        b = j % 2
        w = [("dv", ev_prev[j - 2])] if j >= 2 else [("dv", ev_prev[NJ - 1])]
        tk = pe(lambda e, j=j, b=b: e.transpose(PSb[b][:, 0:64], wv_bf[0:64, j*128:(j+1)*128], ident_sb[0:64, 0:64]),
                waits=tuple(w))
        ev_prev[j] = dv(lambda e, j=j, b=b: e.tensor_copy(wvt[:, j*64:(j+1)*64], PSb[b][:, 0:64]),
                        waits=(("pe", tk),))

    q2l_f = sb(p0stk, "q2l_f", [128, 64], F32)
    q2l_bf = sb(p0stk, "q2l_bf", [128, 64], BF16)
    rs_f = sb(p0stk, "rs_f", [128, 1], F32)
    last_q2l = ev_prev[NJ - 1]
    d1 = None
    for h in range(H):
        w_l = (("dv", d1),) if d1 is not None else ()
        lw = yq(lambda e, h=h: e.dma_start(out=q2l_f[:], in_=q2l[h]), "qq", waits=w_l)
        d1 = dv(lambda e: e.tensor_copy(q2l_bf[:], q2l_f[:]), waits=(("qq", lw), ("dv", last_q2l)))
        dvr(lambda e: e.tensor_reduce(rs_f[:], q2l_bf[:], AX.X, ALU.add))
        d2 = dv(lambda e, h=h: e.tensor_copy(q2lrs[:, h:h+1], rs_f[:]))
        tk = pe(lambda e, h=h: e.transpose(PSb[h % 2][0:64, 0:128], q2l_bf[:], ident_sb[:]), waits=(("dv", d2),))
        last_q2l = dv(lambda e, h=h: e.tensor_copy(q2lt[0:64, h*128:(h+1)*128], PSb[h % 2][0:64, 0:128]),
                      waits=(("pe", tk),))
    gq(lambda e: e.dma_start(out=vup_sb.ap()[64:128, :].rearrange("l (h d) -> l h d", d=128), in_=vup.rearrange("h l d -> l h d")), "vq")

    xstage = [sb(p0stk, "xs0", [128, EMB], F32), sb(p0stk, "xs1", [128, EMB], F32)]
    xn_st2 = [A2.ap()[:, 2048:4096], A2.ap()[:, 4096:6144]]
    xnfm2 = [A2.ap()[:, 6144:8192], A2.ap()[:, 8192:10240]]
    trash = A2.ap()[:, 0:2048]
    LNT = [{n: sb(p0stk, f"ln_{n}{k}", [128, 1], F32)
            for n in ("s1", "s2", "mvec", "msq", "varv", "rstd", "bias")} for k in range(2)]

    def ln_tile(src_ap, buf, prev):
        T = LNT[buf]
        xn_dst = xn_st2[buf]
        w_load = []
        if prev.get("axn2"):
            w_load.append(("ac", prev["axn2"]))
        if prev.get("s1red2"):
            w_load.append(("dv", prev["s1red2"]))
        xsem = "xq0" if buf == 0 else "xq1"
        lw = yq(lambda e, src_ap=src_ap, buf=buf: e.dma_start(out=xstage[buf][:], in_=src_ap),
                xsem, waits=tuple(w_load))
        s1red = dvr(lambda e, buf=buf, T=T: e.tensor_reduce(T["s1"][:], xstage[buf][:], AX.X, ALU.add),
                    waits=((xsem, lw),))
        w_sq = [(xsem, lw)]
        if prev.get("msq2"):
            w_sq.append(("dv", prev["msq2"]))
        a_sq = ac(lambda e, buf=buf, T=T: e.activation(trash[:], xstage[buf][:], AF.Square,
                                                       accum_out=T["s2"][:]),
                  waits=tuple(w_sq))
        dv(lambda e, T=T: e.tensor_scalar(T["mvec"][:], T["s1"][:], 1.0 / EMB, None, ALU.mult))
        msq_t = dv(lambda e, T=T: e.tensor_scalar(T["msq"][:], T["s2"][:], 1.0 / EMB, None, ALU.mult),
                   waits=(("ac", a_sq),))
        w_tt = [("ac", prev["a22"])] if prev.get("a22") else []
        dv(lambda e, T=T: e.tensor_tensor(T["varv"][:], T["mvec"][:], T["mvec"][:], ALU.mult),
           waits=tuple(w_tt))
        d_v = dv(lambda e, T=T: e.scalar_tensor_tensor(T["varv"][:], T["msq"][:], 1.0e-5, T["varv"][:],
                                                       ALU.add, ALU.subtract))
        ac(lambda e, T=T: e.activation(T["varv"][:], T["varv"][:], AF.Ln), waits=(("dv", d_v),))
        a2_ = ac(lambda e, T=T: e.activation(T["rstd"][:], T["varv"][:], AF.Exp, scale=-0.5))
        d2_ = dv(lambda e, T=T: e.scalar_tensor_tensor(T["bias"][:], T["mvec"][:], -1.0, T["rstd"][:],
                                                       ALU.mult, ALU.mult),
                 waits=(("ac", a2_),))
        w_xn = [("dv", d2_)]
        if prev.get("lastT2"):
            w_xn.append(("pe", prev["lastT2"]))
        axn = ac(lambda e, buf=buf, T=T, xn_dst=xn_dst: e.activation(
            xn_dst[:], xstage[buf][:], AF.Identity, bias=T["bias"][:], scale=T["rstd"][:]),
            waits=tuple(w_xn))
        return {"axn": axn, "s1red": s1red, "msq": msq_t, "a2": a2_, "lw": lw}

    if "p0" in phases:
        hist = []
        kv_prev = None
        ptguard = {}
        PTB = (2, 3, 6, 7)
        wdstage = [A2.ap()[:, 16384:18432], A2.ap()[:, 18432:20480]]
        wd_rr = wd_p.rearrange("(c p) e -> c p e", p=128)
        wd_st = {}

        def prev_of(i):
            p = {}
            if i >= 2:
                p["axn2"] = hist[i-2]["axn"]
                p["s1red2"] = hist[i-2]["s1red"]
                p["msq2"] = hist[i-2]["msq"]
                p["a22"] = hist[i-2]["a2"]
                p["lastT2"] = hist[i-2]["lastT"]
            return p

        def emit_ptkv(i):
            nonlocal kv_prev
            buf = i % 2
            tks = hist[i]
            axn = tks["axn"]
            seq = i < NSEQT
            it2 = i - NSEQT
            evs = []
            tk = None
            for j in range(NJ):
                b = PTB[j % 4]
                w = list(ptguard.get(j % 4, []))
                if j < 4:
                    w.append(("ac", axn))
                tk = pe(lambda e, j=j, b=b, buf=buf: e.transpose(
                    PSb[b][:, 0:128], xn_st2[buf][:, j*128:(j+1)*128], ident_sb[:]), waits=tuple(w))
                if seq:
                    ev = dv(lambda e, j=j, b=b, buf=buf: e.tensor_copy(
                        xnfm2[buf][:, j*128:(j+1)*128], PSb[b][:, 0:128]), waits=(("pe", tk),))
                else:
                    ev = dv(lambda e, j=j, b=b, it2=it2: e.tensor_copy(
                        A1[:, j*2048 + it2*128: j*2048 + (it2+1)*128], PSb[b][:, 0:128]),
                        waits=(("pe", tk),))
                ptguard[j % 4] = [("dv", ev)]
                evs.append(ev)
            tks["lastT"] = tk
            if seq:
                for j in range(NJ):
                    w = [("dv", evs[j])]
                    if j == 0 and kv_prev is not None:
                        w.append(("dv", kv_prev))
                    pe(lambda e, j=j, buf=buf: e.matmul(PS[4][0:65, 0:128], wkte[:, j*65:(j+1)*65],
                                                        xnfm2[buf][:, j*128:(j+1)*128],
                                                        start=(j == 0), stop=(j == NJ - 1)),
                       waits=tuple(w))
                    kv_t = pe(lambda e, j=j, buf=buf: e.matmul(PS[5][0:64, 0:128], wvt[:, j*64:(j+1)*64],
                                                               xnfm2[buf][:, j*128:(j+1)*128],
                                                               start=(j == 0), stop=(j == NJ - 1)))
                dv(lambda e, i=i: e.tensor_copy(kpack[0:64, i*128:(i+1)*128], PS[4][0:64, 0:128]),
                   waits=(("pe", kv_t),))
                dv(lambda e, i=i: e.tensor_copy(ksum[0:1, i*128:(i+1)*128], PS[4][64:65, 0:128]))
                kv_prev = dv(lambda e, i=i: e.tensor_copy(kpack[64:128, i*128:(i+1)*128], PS[5][0:64, 0:128]))

        def emit_cmax_bk():
            for j in range(16):
                w0 = max(0, EXT[j] - 3) * 128
                w1 = EXT[j] * 128
                dvr(lambda e, j=j, w0=w0, w1=w1: e.tensor_reduce(cmaxr[:, j:j+1], ksum[:, w0:w1], AX.X, ALU.max))
                dvr(lambda e, j=j, w0=w0, w1=w1: e.tensor_reduce(cminr[:, j:j+1], ksum[:, w0:w1], AX.X, ALU.min))
            d_cm = P.c("dv")
            pe(lambda e: e.matmul(PS[0][0:128, 0:16], ones_sb[:], cmaxr[:, 0:16], start=True, stop=True),
               waits=(("dv", d_cm),))
            t2 = pe(lambda e: e.matmul(PS[1][0:128, 0:16], ones_sb[:], cminr[:, 0:16], start=True, stop=True))
            dv(lambda e: e.tensor_copy(cmax_bc[:], PS[0][0:128, 0:16]), waits=(("pe", t2),))
            e2 = dv(lambda e: e.tensor_copy(cmin_bc[:], PS[1][0:128, 0:16]))
            g0 = {("bk", 0): [("dv", e2)], ("bk", 1): [("dv", e2)]}
            for j in range(16):
                e0 = max(0, EXT[j] - 3)
                nwc = EXT[j] - e0
                woff = (3 - nwc) * 128
                for wci in range(nwc):
                    w = g0.pop(("bk", wci % 2), [])
                    if j == 0 and wci == 0:
                        w += [("cq", P.c("cq"))]
                    bk_mm = pe(lambda e, j=j, wci=wci, e0=e0: e.matmul(
                        PS[0][0:128, 0:128] if wci % 2 == 0 else PS[1][0:128, 0:128],
                        ones_bf_sb[:], ksum[:, (e0+wci)*128:(e0+wci+1)*128],
                        start=True, stop=True), waits=tuple(w))
                    d_bk = dv(lambda e, j=j, wci=wci, woff=woff: e.tensor_tensor(
                        BK_sb[:, j*384 + woff + wci*128: j*384 + woff + (wci+1)*128],
                        band3_sb[:, woff + wci*128: woff + (wci+1)*128],
                        (PS[0][0:128, 0:128] if wci % 2 == 0 else PS[1][0:128, 0:128]), ALU.mult),
                        waits=(("pe", bk_mm),))
                    g0[("bk", wci % 2)] = [("dv", d_bk)]
                if woff > 0:
                    dv(lambda e, j=j, woff=woff: e.tensor_scalar(
                        BK_sb[:, j*384: j*384 + woff], band3_sb[:, 0:woff], 0.0, None, ALU.mult))
            d_sp0 = P.c("dv")
            oq(lambda e: e.dma_start(out=kp_scr, in_=kpack[:]), waits=(("dv", d_sp0),))
            oq(lambda e: e.dma_start(out=bk_scr, in_=BK_sb[:]))

        pend = None
        for i in range(NSEQT + NTILE):
            buf = i % 2
            src = x_seq[i*128:(i+1)*128, :] if i < NSEQT else \
                x_own[(i-NSEQT)*128:(i-NSEQT+1)*128, :]
            tks = ln_tile(src, buf, prev_of(i))
            hist.append(tks)
            if pend is not None:
                emit_ptkv(pend)
            if i == NSEQT:
                emit_cmax_bk()
            if i < NSEQT:
                for rr in range(2):
                    row = i * 2 + rr
                    wsem = "wdq" if row % 2 == 0 else "vq"
                    ssem = "wcq0" if row % 2 == 0 else "wcq1"
                    w_l = [(ssem, wd_st[row % 2])] if row >= 2 else []
                    wl_t = gq(lambda e, row=row: e.dma_start(out=wdstage[row % 2][:], in_=wd_rr[row]),
                              wsem, waits=tuple(w_l))
                    wd_st[row % 2] = yq(lambda e, row=row: e.dma_start(
                        out=wd_bf[row*128:(row+1)*128, :], in_=wdstage[row % 2][:]),
                        ssem, waits=((wsem, wl_t),))
            pend = i
        emit_ptkv(pend)

        if "kpack" in dbg:
            dq = P.c("dv")
            oq(lambda e: e.dma_start(out=dbg_out["d_kpack"], in_=kpack[:]), waits=(("dv", dq),))
            oq(lambda e: e.dma_start(out=dbg_out["d_ksum"], in_=ksum[:]))
            oq(lambda e: e.dma_start(out=dbg_out["d_xn"], in_=A1[:, 0:2048]))


    barrier()
    p0stk.close()
    kstk.close()

    if "p1" in phases:
        p1stk = stk.enter_context(ExitStack())
        acc = A2f
        tb = 16384
        h_all = [A2.ap()[:, tb:tb + 4096], A2.ap()[:, tb + 4096:tb + 8192]]
        habs = A2.ap()[:, tb + 8192:tb + 9216]
        hsp = A2.ap()[:, tb + 9216:tb + 10240]
        hsT = [A2.ap()[:, tb + 10240:tb + 11264], A2.ap()[:, tb + 11264:tb + 12288]]
        spt = [A2.ap()[:, tb + 12288 + k*256: tb + 12288 + (k+1)*256] for k in range(3)]
        gstage = A2.ap()[:, tb + 13056:tb + 15104]
        tmpg = [sb(p1stk, f"tmpg{i}", [128, 128], F32) for i in range(2)]
        SLOTS = 3
        gsems = ("wgq0", "wgq1", "wgq2")
        wgu = [sb(p1stk, f"wgu{i}", [128, NJ * 256], BF16) for i in range(SLOTS)]
        wdbuf = [sb(p1stk, f"wdb{i}", [128, EMB], BF16) for i in range(8)]

        wg_r = wg_p.rearrange("(j p) i -> p j i", p=128)
        wu_r = wu_p.rearrange("(j p) i -> p j i", p=128)

        NBAT = NPASS * NB
        pend_gu = {}
        guards = {}
        Z = {"prev_sp": None, "prev_tev": None, "h_done": {}, "sp_done": {},
             "pt_done": {}, "gadd": {}, "pair_entry": {}, "wd_cur": None,
             "dn_last": None, "a_g": {}, "og_last": None, "tmpg_last": {}}

        def emit_sp(bp, mi):
            hb = h_all[bp % 2]
            w_ab = [("dv", Z["h_done"][bp % 2])]
            if Z["prev_sp"] is not None:
                w_ab.append(("dv", Z["prev_sp"]))
            a_ab = ac(lambda e, hb=hb, mi=mi: e.activation(habs[:], hb[:, mi*1024:(mi+1)*1024], AF.Abs),
                      waits=tuple(w_ab))
            avs = [habs[:, m_*256:(m_+1)*256] for m_ in range(4)]
            hvs = [hb[:, mi*1024 + m_*256: mi*1024 + (m_+1)*256] for m_ in range(4)]
            dv(lambda e, avs=avs: e.tensor_tensor(spt[0][:], avs[0], avs[1], ALU.min), waits=(("ac", a_ab),))
            dv(lambda e, avs=avs: e.tensor_tensor(spt[1][:], avs[0], avs[1], ALU.max))
            dv(lambda e, avs=avs: e.tensor_tensor(spt[2][:], avs[2], avs[3], ALU.min))
            dv(lambda e: e.tensor_tensor(spt[0][:], spt[0][:], spt[2][:], ALU.max))
            dv(lambda e, avs=avs: e.tensor_tensor(spt[2][:], avs[2], avs[3], ALU.max))
            dv(lambda e: e.tensor_tensor(spt[1][:], spt[1][:], spt[2][:], ALU.min))
            dv(lambda e: e.tensor_tensor(spt[0][:], spt[0][:], spt[1][:], ALU.max))
            sp_d = None
            for m_ in range(4):
                dv(lambda e, a=avs[m_]: e.tensor_tensor(spt[1][:], a, spt[0][:], ALU.is_ge))
                w_sp = ()
                if m_ == 0 and Z["prev_tev"] is not None:
                    w_sp = (("pe", Z["prev_tev"]),)
                sp_d = dv(lambda e, hh=hvs[m_], m_=m_: e.tensor_tensor(hsp[:, m_*256:(m_+1)*256],
                                                                       hh, spt[1][:], ALU.mult), waits=w_sp)
            Z["prev_sp"] = sp_d
            Z["sp_done"][mi] = sp_d

        def emit_PT(bp, mi):
            pair = (4, 5) if mi % 2 == 0 else (6, 7)
            pte = {}
            tk = None
            for cb in range(8):
                bnk = pair[cb % 2]
                w = []
                if cb < 2:
                    w.append(("dv", Z["sp_done"][mi]))
                    w += Z["pair_entry"].pop((pair, cb % 2), [])
                else:
                    w.append(("dv", pte[cb - 2]))
                tk = pe(lambda e, cb=cb, bnk=bnk: e.transpose(PSb[bnk][:, 0:128],
                                                              hsp[:, cb*128:(cb+1)*128], ident_sb[:]),
                        waits=tuple(w))
                pte[cb] = dv(lambda e, cb=cb, bnk=bnk, mi=mi: e.tensor_copy(
                    hsT[mi % 2][:, cb*128:(cb+1)*128], PSb[bnk][:, 0:128]), waits=(("pe", tk),))
            Z["prev_tev"] = tk
            Z["pt_done"][mi] = pte[7]

        def emit_dn(bp, mi, setk):
            pair = (4, 5) if mi % 2 == 0 else (6, 7)
            last = None
            for pi in range(2):
                bnk = pair[pi]
                ch0 = (setk*8 + pi*4) * 128
                for ci in range(8):
                    w = []
                    if ci == 0:
                        if setk == 0:
                            if pi == 0:
                                w.append(("dv", Z["pt_done"][mi]))
                        else:
                            w.append(("dv", Z["gadd"][(mi, 0, bnk)]))
                        if mi == 0 and setk == 0 and pi == 0:
                            w.append(("wcq0", Z["wd_cur"]))
                    last = pe(lambda e, mi=mi, ci=ci, bnk=bnk, ch0=ch0:
                              e.matmul(PS[bnk][:, 0:512],
                                       hsT[mi % 2][:, ci*128:(ci+1)*128],
                                       wdbuf[ci][:, ch0:ch0+512],
                                       start=(ci == 0), stop=(ci == 7)), waits=tuple(w))
            Z["dn_last"] = last

        def emit_gadd(bp, mi, setk):
            pair = (4, 5) if mi % 2 == 0 else (6, 7)
            b_ = bp % NB
            ticks = {}
            for pi in range(2):
                bnk = pair[pi]
                e0 = setk*1024 + pi*512
                w = []
                if pi == 0:
                    w.append(("pe", Z["dn_last"]))
                if b_ == 0 and setk == 0 and pi == 0 and (mi in Z["a_g"]):
                    w.append(("ac", Z["a_g"][mi]))
                if b_ == 0:
                    t = dv(lambda e, mi=mi, e0=e0, bnk=bnk: e.tensor_copy(
                        acc[:, mi*2048 + e0: mi*2048 + e0 + 512], PS[bnk][:, 0:512]), waits=tuple(w))
                else:
                    t = dv(lambda e, mi=mi, e0=e0, bnk=bnk: e.tensor_tensor(
                        acc[:, mi*2048 + e0: mi*2048 + e0 + 512],
                        acc[:, mi*2048 + e0: mi*2048 + e0 + 512], PS[bnk][:, 0:512], ALU.add),
                        waits=tuple(w))
                Z["gadd"][(mi, setk, bnk)] = t
                ticks[pi] = t
            if setk == 1:
                for pi in range(2):
                    Z["pair_entry"][(pair, pi)] = [("dv", ticks[pi])]

        def emit_spill(p_i):
            for mi in range(NMIC):
                mt = p_i * NMIC + mi
                pair = (4, 5) if mi % 2 == 0 else (6, 7)
                w = [("dv", Z["gadd"][(mi, 1, pair[1])])]
                if Z["og_last"] is not None:
                    w.append(("oq", Z["og_last"]))
                a_g = ac(lambda e, mi=mi: e.activation(gstage[:], acc[:, mi*2048:(mi+1)*2048], AF.Silu),
                         waits=tuple(w))
                og = oq(lambda e, mt=mt: e.dma_start(out=g_scr[mt*128:(mt+1)*128, :], in_=gstage[:]),
                        waits=(("ac", a_g),))
                Z["a_g"][mi] = a_g
                Z["og_last"] = og

        def emit_down_pos(bp, s):
            if s == 0:
                emit_sp(bp, 0); emit_PT(bp, 0); emit_sp(bp, 1)
            elif s == 1:
                emit_dn(bp, 0, 0); emit_gadd(bp, 0, 0); emit_PT(bp, 1); emit_sp(bp, 2)
            elif s == 2:
                emit_dn(bp, 0, 1); emit_gadd(bp, 0, 1)
                emit_dn(bp, 1, 0); emit_gadd(bp, 1, 0)
            elif s == 3:
                emit_PT(bp, 2); emit_sp(bp, 3); emit_dn(bp, 1, 1); emit_gadd(bp, 1, 1)
            elif s == 4:
                emit_dn(bp, 2, 0); emit_gadd(bp, 2, 0); emit_PT(bp, 3)
            elif s == 5:
                emit_dn(bp, 2, 1); emit_gadd(bp, 2, 1)
                emit_dn(bp, 3, 0); emit_gadd(bp, 3, 0)
            elif s == 6:
                emit_dn(bp, 3, 1); emit_gadd(bp, 3, 1)
                if bp % NB == NB - 1:
                    emit_spill(bp // NB)

        DPRV = None
        slice_ctr = 0
        for bat in range(NBAT):
            p_i, b = divmod(bat, NB)
            m0 = p_i * NMIC
            hb = h_all[bat % 2]
            last_h = None
            for s in range(8):
                m, c = s // 2, s % 2
                slot = slice_ctr % SLOTS
                gsem = gsems[slot]
                c0 = m*2048 + 256*b + c*128
                w = [("pe", pend_gu[slot])] if slot in pend_gu else []
                gq(lambda e, slot=slot, c0=c0: e.dma_start(
                    out=wgu[slot].ap().rearrange("p (j i) -> p j i", i=256)[:, :, 0:128],
                    in_=wg_r[:, :, c0:c0+128]), gsem, waits=tuple(w))
                lt = gq(lambda e, slot=slot, c0=c0: e.dma_start(
                    out=wgu[slot].ap().rearrange("p (j i) -> p j i", i=256)[:, :, 128:256],
                    in_=wu_r[:, :, c0:c0+128]), gsem)
                for mi in range(NMIC):
                    mt = m0 + mi
                    bnk = mi
                    for j in range(NJ):
                        w = []
                        if j == 0:
                            if mi == 0:
                                w.append((gsem, lt))
                            w += guards.pop((bnk, c), [])
                        gu_mm = pe(lambda e, c=c, j=j, mt=mt, slot=slot, bnk=bnk:
                                   e.matmul(PS[bnk][:, c*256:(c+1)*256],
                                            A1[:, j*2048 + mt*128: j*2048 + (mt+1)*128],
                                            wgu[slot][:, j*256:(j+1)*256],
                                            start=(j == 0), stop=(j == NJ - 1)), waits=tuple(w))
                    w_si = [("pe", gu_mm)]
                    if (mi % 2) in Z["tmpg_last"]:
                        w_si.append(("dv", Z["tmpg_last"][mi % 2]))
                    a_si = ac(lambda e, c=c, bnk=bnk, mi=mi: e.activation(
                        tmpg[mi % 2][:], PS[bnk][:, c*256:c*256+128], AF.Silu), waits=tuple(w_si))
                    hslice = hb[:, mi*1024 + m*256 + c*128: mi*1024 + m*256 + (c+1)*128]
                    last_h = dv(lambda e, hslice=hslice, c=c, bnk=bnk, mi=mi:
                                e.tensor_tensor(hslice, tmpg[mi % 2][:],
                                                PS[bnk][:, c*256+128:(c+1)*256], ALU.mult),
                                waits=(("ac", a_si),))
                    Z["tmpg_last"][mi % 2] = last_h
                    guards[(bnk, c)] = [("dv", last_h)]
                pend_gu[slot] = gu_mm
                slice_ctr += 1
                if DPRV is not None:
                    emit_down_pos(DPRV, s)
                if s == 6:
                    wt = None
                    for ci in range(8):
                        row = 16 * (ci // 2) + 2 * b + (ci % 2)
                        w = [("pe", Z["dn_last"])] if Z["dn_last"] is not None else []
                        wt = yq(lambda e, ci=ci, row=row: e.dma_start(
                            out=wdbuf[ci][:], in_=wd_bf[row*128:(row+1)*128, :]),
                            "wcq0", waits=tuple(w))
                    Z["wd_cur"] = wt
            Z["h_done"][bat % 2] = last_h
            DPRV = bat
        for s in range(8):
            emit_down_pos(DPRV, s)

        if "g" in dbg:
            ov = P.c("oq")
            for mt in range(NTILE):
                lw = yq(lambda e, mt=mt: e.dma_start(out=gstage[:], in_=g_scr[mt*128:(mt+1)*128, :]),
                        "xq0", waits=(("oq", ov),))
                ov = oq(lambda e, mt=mt: e.dma_start(out=dbg_out["d_g"][mt*128:(mt+1)*128, :], in_=gstage[:]),
                        waits=(("xq0", lw),))
        barrier()
        p1stk.close()

    if ("p2" in phases) or ("p3" in phases):
        k2stk = stk.enter_context(ExitStack())
        kp3 = sb(k2stk, "kpack2", [128, S], BF16)
        BK3 = sb(k2stk, "BK_sb2", [128, 16 * 384], BF16)
        yq(lambda e: e.dma_start(out=kp3[:], in_=kp_scr), "cq")
        rl2 = yq(lambda e: e.dma_start(out=BK3[:], in_=bk_scr), "cq")

    if "p2" in phases:
        p2stk = stk.enter_context(ExitStack())
        wqh = [sb(p2stk, f"wqh{i}", [128, EMB], BF16) for i in range(2)]
        wqT = sb(p2stk, "wqT", [128, EMB], BF16)
        qs_col = sb(p2stk, "qs_col", [128, 16], F32)
        qsT = sb(p2stk, "qsT", [16, 128], F32)
        qtmax = sb(p2stk, "qtmax", [16, 1], F32)
        qtmin = sb(p2stk, "qtmin", [16, 1], F32)
        tb1 = sb(p2stk, "tb1", [16, 1], F32)
        tb2 = sb(p2stk, "tb2", [16, 1], F32)
        negc_row = sb(p2stk, "negc_row", [1, 16], F32)
        guards2 = {}
        pend_wq = {}
        pend_wqT = None
        for h in range(NHEADS):
            slot = h % 2
            gsem = "wgq0" if slot == 0 else "wgq1"
            w = [("pe", pend_wq[slot])] if slot in pend_wq else []
            lw = gq(lambda e, h=h, slot=slot: e.dma_start(out=wqh[slot][:], in_=Wq[h*128:(h+1)*128, :]),
                    gsem, waits=tuple(w))
            tevq = []
            for j in range(NJ):
                bnk = j % 2
                w = [(gsem, lw)] if j == 0 else []
                w += guards2.pop(("t", bnk), [])
                if pend_wqT is not None and j < 2:
                    w.append(("pe", pend_wqT))
                tk = pe(lambda e, j=j, bnk=bnk, slot=slot: e.transpose(
                    PSb[bnk][:, 0:128], wqh[slot][:, j*128:(j+1)*128], ident_sb[:]), waits=tuple(w))
                tevq.append(dv(lambda e, j=j, bnk=bnk: e.tensor_copy(wqT[:, j*128:(j+1)*128], PSb[bnk][:, 0:128]),
                               waits=(("pe", tk),)))
                guards2[("t", bnk)] = [("dv", tevq[-1])]
            for tc in range(4):
                bnk = 2 + tc % 2
                for j in range(NJ):
                    w = []
                    if j == 0:
                        w += guards2.pop(("q", bnk), []) + [("dv", tevq[-1])]
                    q_mm = pe(lambda e, j=j, tc=tc, h=h, bnk=bnk: e.matmul(
                        PS[bnk][:, 0:512], wqT[:, j*128:(j+1)*128],
                        A1[:, j*2048 + tc*512: j*2048 + (tc+1)*512],
                        start=(j == 0), stop=(j == NJ - 1)), waits=tuple(w))
                ev1 = dv(lambda e, tc=tc, h=h, bnk=bnk: e.tensor_copy(
                    A2[:, h*2048 + tc*512: h*2048 + (tc+1)*512],
                    PS[bnk][:, 0:512]), waits=(("pe", q_mm),))
                guards2[("q", bnk)] = [("dv", ev1)]
                pend_wqT = q_mm
            pend_wq[slot] = q_mm
            for mt in range(NTILE):
                w = [("dv", P.c("dv"))] if mt == 0 else []
                if mt == 0:
                    w += guards2.pop(("qs",), [])
                qs_mm = pe(lambda e, mt=mt, h=h: e.matmul(
                    PS[4][:, mt:mt+1], A2[:, h*2048 + mt*128: h*2048 + (mt+1)*128],
                    q2lrs[:, h:h+1], start=True, stop=True), waits=tuple(w))
            d_qs = dv(lambda e: e.tensor_copy(qs_col[:], PS[4][:, 0:16]), waits=(("pe", qs_mm),))
            guards2[("qs",)] = [("dv", d_qs)]
            dv(lambda e, h=h: e.tensor_scalar(qsb[:, h*16:(h+1)*16], qs_col[:], 0.0625, None, ALU.mult))
            d_q16 = P.c("dv")
            dv(lambda e, h=h: e.tensor_tensor(tnc[:], qsb[:, h*16:(h+1)*16], cmax_bc[:], ALU.mult))
            dv(lambda e, h=h: e.tensor_tensor(tnc2[:], qsb[:, h*16:(h+1)*16], cmin_bc[:], ALU.mult))
            dv(lambda e: e.tensor_tensor(tnc[:], tnc[:], tnc2[:], ALU.max))
            dv(lambda e, h=h: e.tensor_scalar(negc_bc[:, h*16:(h+1)*16], tnc[:], -1.0, -10.0,
                                              ALU.mult, ALU.add))
        if "q" in dbg:
            dq = P.c("dv")
            oq(lambda e: e.dma_start(out=dbg_out["d_q"][:, 0:NHEADS*2048], in_=A2[:, 0:NHEADS*2048]), waits=(("dv", dq),))
            oq(lambda e: e.dma_start(out=dbg_out["d_qsb"][:, 0:NHEADS*16], in_=qsb[:, 0:NHEADS*16]))
            oq(lambda e: e.dma_start(out=dbg_out["d_negc"][:, 0:NHEADS*16], in_=negc_bc[:, 0:NHEADS*16]))
        barrier()
        p2stk.close()

    if "p3" in phases:
        p3stk = stk.enter_context(ExitStack())
        A_sb = sb(p3stk, "A_sb", [128, S], BF16)
        Vx_sb = sb(p3stk, "Vx_sb", [128, S], BF16)
        probs = [sb(p3stk, f"probs{i}", [128, S], BF16) for i in range(2)]
        pts_all = sb(p3stk, "pts_all", [128, S], BF16)
        ctxt_f = sb(p3stk, "ctxt_f", [128, 128], F32)
        g3 = {("bk", 0): [("cq", rl2)], ("bk", 1): [("cq", rl2)]}
        for h in range(NHEADS):
            for grp in range(8):
                bnk = grp % 2
                w = g3.pop(("av", bnk), [])
                if h == 0 and grp < 2:
                    w += g3.pop(("bk", bnk), [])
                av_mm = pe(lambda e, h=h, grp=grp, bnk=bnk: e.matmul(
                    PS[bnk][:, 0:512], q2lt[0:64, h*128:(h+1)*128],
                    kp3[0:64, grp*512:(grp+1)*512], start=True, stop=True), waits=tuple(w))
                d_av = dv(lambda e, grp=grp, bnk=bnk: e.tensor_scalar(
                    A_sb[:, grp*512:(grp+1)*512], PS[bnk][:], 0.125, None, ALU.mult),
                    waits=(("pe", av_mm),))
                g3[("av", bnk)] = [("dv", d_av)]
            for grp in range(8):
                bnk = grp % 2
                for cc in range(4):
                    c = grp * 4 + cc
                    w = g3.pop(("av", bnk), []) if cc == 0 else []
                    av_mm = pe(lambda e, h=h, c=c, cc=cc, bnk=bnk: e.matmul(
                        PS[bnk][:, cc*128:(cc+1)*128], kp3[64:128, c*128:(c+1)*128],
                        vup_sb[64:128, h*128:(h+1)*128], start=(cc == 0), stop=(cc == 3)), waits=tuple(w))
                d_av = dv(lambda e, grp=grp, bnk=bnk: e.tensor_copy(
                    Vx_sb[:, grp*512:(grp+1)*512], PS[bnk][:]), waits=(("pe", av_mm),))
                g3[("av", bnk)] = [("dv", d_av)]
            def emit_scores_mm(h, j):
                E = EXT[j]
                ngrp = (E + 3) // 4
                pp = j % 2
                last_ex = None
                for grp in range(ngrp):
                    bnk = 2 + grp % 2
                    nc_in_g = min(4, E - grp * 4)
                    w = g3.pop(("sc", bnk), [])
                    if grp == 0:
                        w += g3.pop(("ptw", pp), [])
                        w += g3.pop(("sg", pp), [])
                    sc_mm = pe(lambda e, h=h, j=j, grp=grp, bnk=bnk, nc_in_g=nc_in_g: e.matmul(
                        PS[bnk][:, 0:nc_in_g*128],
                        A2[:, h*2048 + j*128:(h*2048 + (j+1)*128)],
                        A_sb[:, grp*512:grp*512 + nc_in_g*128], start=True, stop=True),
                        waits=tuple(w))
                    lo_g, hi_g = grp * 4, grp * 4 + nc_in_g
                    e0 = max(0, E - 3)
                    blo, bhi = max(lo_g, e0), hi_g
                    dvt = None
                    if blo < bhi:
                        woff3 = (3 - (E - e0)) * 128
                        dvt = dv(lambda e, h=h, j=j, blo=blo, bhi=bhi, lo_g=lo_g, e0=e0, woff3=woff3, grp=grp:
                                 e.scalar_tensor_tensor(
                                     PS[2 + grp % 2][:, (blo-lo_g)*128:(bhi-lo_g)*128],
                                     BK3[:, j*384 + woff3 + (blo-e0)*128: j*384 + woff3 + (bhi-e0)*128],
                                     qsb[:, h*16+j:h*16+j+1],
                                     PS[2 + grp % 2][:, (blo-lo_g)*128:(bhi-lo_g)*128],
                                     ALU.mult, ALU.add), waits=(("pe", sc_mm),))
                    if hi_g == E:
                        dvt = dv(lambda e, j=j, nc_in_g=nc_in_g, grp=grp: e.tensor_tensor(
                            PS[2 + grp % 2][:, (nc_in_g-2)*128: nc_in_g*128],
                            PS[2 + grp % 2][:, (nc_in_g-2)*128: nc_in_g*128],
                            caus2_sb[:], ALU.add),
                            waits=(("pe", sc_mm),) if dvt is None else ())
                    if j == 15:
                        dvt = dvr(lambda e, nc_in_g=nc_in_g, grp=grp: e.tensor_reduce(
                            negc15[:], PS[2 + grp % 2][:, 0:nc_in_g*128], AX.X, ALU.max),
                            waits=(("dv", dvt),) if dvt is not None else (("pe", sc_mm),))
                        dvt = dv(lambda e: e.tensor_scalar(negc15[:], negc15[:], -1.0, None, ALU.mult))
                        bias_ap = negc15
                    else:
                        bias_ap = None
                    w_exp = [("pe", sc_mm)] if dvt is None else [("dv", dvt)]
                    a_ex = ac(lambda e, h=h, j=j, grp=grp, nc_in_g=nc_in_g, bias_ap=bias_ap, pp=pp:
                              e.activation(
                                  probs[pp][:, grp*512: grp*512 + nc_in_g*128],
                                  PS[2 + grp % 2][:, 0:nc_in_g*128],
                                  AF.Exp, bias=(bias_ap[:] if bias_ap is not None else
                                                negc_bc[:, h*16+j:h*16+j+1]),
                                  accum_out=sumg[pp][:, grp:grp+1]),
                              waits=tuple(w_exp))
                    last_ex = a_ex
                    g3[("sc", bnk)] = [("ac", a_ex)] + ([("dv", dvt)] if dvt is not None else [])
                return last_ex

            def emit_rtail(h, j, a_last):
                E = EXT[j]
                ngrp = (E + 3) // 4
                pp = j % 2
                rd = dvr(lambda e, ngrp=ngrp, pp=pp: e.tensor_reduce(sumtot[pp][:], sumg[pp][:, 0:ngrp],
                                                                     AX.X, ALU.add),
                         waits=(("ac", a_last),))
                g3[("sg", pp)] = [("dv", rd)]
                ac(lambda e, pp=pp: e.activation(sumtot[pp][:], sumtot[pp][:], AF.Ln),
                   waits=(("dv", rd),))
                w_r = g3.pop(("rctx", pp), [])
                a_r = ac(lambda e, pp=pp: e.activation(rinv[pp][:], sumtot[pp][:], AF.Exp, scale=-1.0),
                         waits=tuple(w_r))
                g3[("rinv", pp)] = a_r

            def emit_pt(h, j, a_last):
                E = EXT[j]
                pp = j % 2
                last_tk = None
                d_pt = None
                for c in range(E):
                    tbnk = 4 + (c % 4)
                    w = list(g3.get(("pt", c % 4), []))
                    if c < 4:
                        w.append(("ac", a_last))
                    if c in (2, 3):
                        w += g3.get(("rctx", 1 - pp), [])
                    tk = pe(lambda e, c=c, tbnk=tbnk, pp=pp: e.transpose(
                        PSb[tbnk][:, 0:128], probs[pp][:, c*128:(c+1)*128], ident_sb[:]), waits=tuple(w))
                    d_pt = dv(lambda e, c=c, tbnk=tbnk: e.tensor_copy(pts_all[:, c*128:(c+1)*128],
                                                                      PSb[tbnk][:, 0:128]),
                              waits=(("pe", tk),))
                    g3[("pt", c % 4)] = [("dv", d_pt)]
                    last_tk = tk
                g3[("ptw", pp)] = [("pe", last_tk)]
                g3[("ptd", pp)] = d_pt

            def emit_pvmm(h, j):
                E = EXT[j]
                pp = j % 2
                R = 256 + pp * 128
                d_pt = g3.pop(("ptd", pp))
                lastmm = None
                for c in range(E):
                    bnkp = 6 + (c % 2)
                    w = ()
                    if c < 2:
                        w = tuple(g3.pop(("ctx", bnkp, R), []) + [("dv", d_pt)])
                    lastmm = pe(lambda e, c=c, bnkp=bnkp, R=R, E=E: e.matmul(
                        PS[bnkp][:, R:R+128], pts_all[:, c*128:(c+1)*128], Vx_sb[:, c*128:(c+1)*128],
                        start=(c < 2), stop=(c >= E - 2)), waits=w)
                a_r = g3[("rinv", pp)]
                dv(lambda e, R=R, pp=pp: e.tensor_scalar(ctxt_f[:], PS[7][:, R:R+128], rinv[pp][:],
                                                         None, ALU.mult),
                   waits=(("pe", lastmm), ("ac", a_r)))
                d_ctx = dv(lambda e, h=h, j=j, pp=pp, R=R: e.scalar_tensor_tensor(
                    A1[:, h*2048 + j*128: h*2048 + (j+1)*128], PS[6][:, R:R+128], rinv[pp][:],
                    ctxt_f[:], ALU.mult, ALU.add))
                g3[("ctx", 6, R)] = [("dv", d_ctx)]
                g3[("ctx", 7, R)] = [("dv", d_ctx)]
                g3[("rctx", pp)] = [("dv", d_ctx)]

            pend = None
            for j in range(NSLOTS):
                ax_j = emit_scores_mm(h, j)
                if pend is not None:
                    emit_pt(h, pend[0], pend[1])
                emit_rtail(h, j, ax_j)
                if pend is not None:
                    emit_pvmm(h, pend[0])
                pend = (j, ax_j)
            emit_pt(h, pend[0], pend[1])
            emit_pvmm(h, pend[0])

        if "ctx" in dbg:
            dq = P.c("dv")
            oq(lambda e: e.dma_start(out=dbg_out["d_ctx"][:, 0:NHEADS*2048], in_=A1[:, 0:NHEADS*2048]), waits=(("dv", dq),))
        barrier()
        p3stk.close()

    if ("p2" in phases) or ("p3" in phases):
        k2stk.close()

    if "p4" in phases:
        p4stk = stk.enter_context(ExitStack())
        woh = [sb(p4stk, f"woh{i}", [128, EMB], BF16) for i in range(2)]
        gbuf = [sb(p4stk, f"gbuf{i}", [128, EMB], BF16) for i in range(2)]
        xbuf = [sb(p4stk, f"xbuf{i}", [128, EMB], F32) for i in range(2)]
        obuf = [sb(p4stk, f"obuf{i}", [128, EMB], F32) for i in range(2)]
        tmp4 = sb(p4stk, "tmp4", [128, 512], F32)
        cstage = [sb(p4stk, f"cst{i}", [128, EMB], BF16) for i in range(2)]
        g4 = {}
        for ec in range(16):
            slot = ec % 2
            gsem = "wgq0" if slot == 0 else "wgq1"
            w = g4.pop(("wo", slot), [])
            lw = gq(lambda e, ec=ec, slot=slot: e.dma_start(out=woh[slot][:], in_=Wo[ec*128:(ec+1)*128, :]),
                    gsem, waits=tuple(w))
            last = None
            for hh in range(16):
                bnk = hh % 2
                w = [(gsem, lw)] if hh == 0 else []
                w += g4.pop(("t", bnk), [])
                tk = pe(lambda e, hh=hh, ec=ec, bnk=bnk, slot=slot: e.transpose(
                    PSb[bnk][:, 0:128], woh[slot][:, hh*128:(hh+1)*128], ident_sb[:]), waits=tuple(w))
                last = dv(lambda e, hh=hh, ec=ec, bnk=bnk: e.tensor_copy(
                    A2[:, hh*2048 + ec*128: hh*2048 + (ec+1)*128], PSb[bnk][:, 0:128]),
                    waits=(("pe", tk),))
                g4[("t", bnk)] = [("dv", last)]
            g4[("wo", slot)] = [("pe", tk)]
        d_wot = P.c("dv")
        for mt in range(NTILE):
            pbuf = mt % 2
            w = g4.pop(("g", pbuf), [])
            lg = yq(lambda e, mt=mt, pbuf=pbuf: e.dma_start(out=gbuf[pbuf][:], in_=g_scr[mt*128:(mt+1)*128, :]),
                    "xq0" if pbuf == 0 else "xq1", waits=tuple(w))
            lx = yq(lambda e, mt=mt, pbuf=pbuf: e.dma_start(out=xbuf[pbuf][:], in_=x_own[mt*128:(mt+1)*128, :]),
                    "xq0" if pbuf == 0 else "xq1")
            d_cst = None
            for hh in range(16):
                bnk = hh % 2
                w = g4.pop(("t", bnk), [])
                if hh < 2 and mt == 0:
                    w += [("dv", d_wot)]
                tk = pe(lambda e, hh=hh, mt=mt, bnk=bnk: e.transpose(
                    PSb[bnk][:, 0:128], A1[:, hh*2048 + mt*128: hh*2048 + (mt+1)*128],
                    ident_sb[:]), waits=tuple(w))
                d_cst = dv(lambda e, hh=hh, bnk=bnk, pbuf=pbuf: e.tensor_copy(
                    cstage[pbuf][:, hh*128:(hh+1)*128], PSb[bnk][:, 0:128]),
                    waits=(("pe", tk),))
                g4[("t", bnk)] = [("dv", d_cst)]
            for half in range(2):
                for bo in range(2):
                    bnk = 2 + bo
                    e0c = half * 1024 + bo * 512
                    for hh in range(16):
                        w = []
                        if hh == 0:
                            w = g4.pop(("s", bnk), [])
                            if bo == 0 and half == 0:
                                w += [("dv", d_cst)]
                                if mt == 0:
                                    w += [("dv", d_wot)]
                        s_mm = pe(lambda e, hh=hh, bnk=bnk, e0c=e0c, mt=mt, pbuf=pbuf: e.matmul(
                            PS[bnk][:, 0:512],
                            cstage[pbuf][:, hh*128:(hh+1)*128],
                            A2[:, hh*2048 + e0c: hh*2048 + e0c + 512],
                            start=(hh == 0), stop=(hh == 15)), waits=tuple(w))
                    g4[("smm", bnk)] = s_mm
                for bo in range(2):
                    bnk = 2 + bo
                    e0c = half * 1024 + bo * 512
                    w = [("pe", g4[("smm", bnk)]), ((("xq0" if pbuf == 0 else "xq1"), lx))]
                    dv(lambda e, bnk=bnk, e0c=e0c, pbuf=pbuf: e.tensor_tensor(
                        tmp4[:], PS[bnk][:], gbuf[pbuf][:, e0c:e0c+512], ALU.mult), waits=tuple(w))
                    d_cb = dv(lambda e, bnk=bnk, e0c=e0c, pbuf=pbuf: e.tensor_tensor(
                        obuf[pbuf][:, e0c:e0c+512], tmp4[:], xbuf[pbuf][:, e0c:e0c+512], ALU.add))
                    g4[("s", bnk)] = [("dv", d_cb)]
            so = oq(lambda e, mt=mt, pbuf=pbuf: e.dma_start(out=out[mt*128:(mt+1)*128, :], in_=obuf[pbuf][:]),
                    waits=(("dv", P.c("dv")),))
            g4[("g", pbuf)] = [("oq", so)]

    for s_ in ("oq", "xq0", "xq1", "cq", "qq", "vq", "kvq", "wgq0", "wgq1", "wgq2", "wdq", "wcq0", "wcq1"):
        P.op("y", lambda e, s_=s_: e.wait_ge(P.sems[s_], P.c(s_)))

    P.emit()
    return nc, dbg_out



import numpy as np
import ml_dtypes

def perm_idx():
    p = np.arange(8192)
    return 4 * (p % 2048) + p // 2048

def make_consts():
    i_, j_ = np.mgrid[0:128, 0:128]
    m_prev = (j_ >= i_ + 64).astype(np.float32)
    m_diag = ((i_ - j_ >= 0) & (i_ - j_ <= 64)).astype(np.float32)
    caus = np.where(j_ <= i_, 0.0, -1.0e30).astype(np.float32)
    z = np.zeros((128, 128), np.float32)
    negf = np.full((128, 128), -1.0e30, np.float32)
    band3 = {
        0: np.concatenate([z, m_prev, m_diag], axis=1),
        1: np.concatenate([m_prev, m_diag, z], axis=1),
    }
    caus2 = {
        0: np.concatenate([z, caus], axis=1),
        1: np.concatenate([caus, negf], axis=1),
    }
    ident = np.eye(128, dtype=ml_dtypes.bfloat16)
    ones_f = np.ones((1, 128), np.float32)
    extras = {
        "ones_b": np.ones((1, 128), ml_dtypes.bfloat16),
        "identf": np.eye(128, dtype=np.float32),
        "ident16": np.eye(16, dtype=np.float32),
    }
    return band3, caus2, ident, ones_f, extras

def own_tiles(role):
    if role == 0:
        return [31 - 2 * k for k in range(16)]
    return [30 - 2 * k for k in range(16)]

def core_inputs(core, x, Wq, Wk, Wv, q2l, vup, Wo, wg, wu, wd):
    b, role = core // 2, core % 2
    band3, caus2, ident, ones_f, extras = make_consts()
    idx = perm_idx()
    tiles = own_tiles(role)
    xs = np.ascontiguousarray(x[b])
    xo = np.ascontiguousarray(
        xs.reshape(32, 128, 2048)[tiles].reshape(2048, 2048))
    return {
        "x_seq": xs, "x_own": xo,
        "Wq": Wq, "Wk": Wk, "Wv": Wv, "q2l": q2l, "vup": vup, "Wo": Wo,
        "wg_p": np.ascontiguousarray(wg[:, idx]),
        "wu_p": np.ascontiguousarray(wu[:, idx]),
        "wd_p": np.ascontiguousarray(wd[idx, :]),
        "ident": ident, "band3": band3[role], "caus2": caus2[role],
        "ones_f": ones_f, **extras,
    }



_PROG_CACHE = {}

def kernel(L0_attn_hidden_states, L0_attn_Wq, L0_attn_Wk, L0_attn_Wv, L0_attn_q_to_latent,
           L0_attn_v_up, L0_attn_Wo, L0_ffn_w_gate, L0_ffn_w_up, L0_ffn_w_down):
    x = np.ascontiguousarray(np.asarray(L0_attn_hidden_states, dtype=np.float32))
    args = (np.ascontiguousarray(np.asarray(L0_attn_Wq, dtype=np.float32)),
            np.ascontiguousarray(np.asarray(L0_attn_Wk, dtype=np.float32)),
            np.ascontiguousarray(np.asarray(L0_attn_Wv, dtype=np.float32)),
            np.ascontiguousarray(np.asarray(L0_attn_q_to_latent, dtype=np.float32)),
            np.ascontiguousarray(np.asarray(L0_attn_v_up, dtype=np.float32)),
            np.ascontiguousarray(np.asarray(L0_attn_Wo, dtype=np.float32)),
            np.ascontiguousarray(np.asarray(L0_ffn_w_gate, dtype=np.float32)),
            np.ascontiguousarray(np.asarray(L0_ffn_w_up, dtype=np.float32)),
            np.ascontiguousarray(np.asarray(L0_ffn_w_down, dtype=np.float32)))
    if "nc" not in _PROG_CACHE:
        nc, _ = build_program(dbg=(), phases=("p0", "p1", "p2", "p3", "p4"))
        _PROG_CACHE["nc"] = nc
    nc = _PROG_CACHE["nc"]
    in_maps = [core_inputs(c, x, *args) for c in range(8)]
    res = run_bass_kernel_spmd(nc, in_maps, core_ids=list(range(8)))
    out = np.empty((4, 4096, 2048), dtype=np.float32)
    for c in range(8):
        b, role = c // 2, c % 2
        o = np.asarray(res.results[c]["out"]).reshape(16, 128, 2048)
        for k, tau in enumerate(own_tiles(role)):
            out[b, tau*128:(tau+1)*128, :] = o[k]
    return out


# version 18 (final)
# speedup vs baseline: 1.3552x; 1.0519x over previous
import numpy as np
import ml_dtypes
from concourse.bass_utils import run_bass_kernel_spmd

import numpy as np
import concourse.bass as bass
import concourse.mybir as mybir
from contextlib import ExitStack

F32 = mybir.dt.float32
BF16 = mybir.dt.bfloat16
AF = mybir.ActivationFunctionType
ALU = mybir.AluOpType
AX = mybir.AxisListType

H, D, L, EMB, II, S, TC = 16, 128, 64, 2048, 8192, 4096, 2048
NTILE = 16
NSEQT = 32
NJ = EMB // 128
EXT = [32 - 2 * j for j in range(16)]
NEG = -1.0e30
NPASS = 4
NHEADS = 16
NSLOTS = 16
NMIC = 4
NB = 8
MW = 256


class Prog:
    ENGS = ("t", "v", "s", "g", "y")

    def __init__(self, nc, stk):
        self.nc, self.stk = nc, stk
        self.q = {e: [] for e in self.ENGS}
        self.sems, self.cnt = {}, {}

    def sem(self, name):
        self.sems[name] = self.stk.enter_context(self.nc.semaphore(name))
        self.cnt[name] = 0

    import os
    CHAIN = {"v": "dv", "s": "ac"}
    if os.environ.get("KB_CHAIN_PE"):
        CHAIN = {"v": "dv", "s": "ac", "t": "pe"}

    def op(self, eng, fn, waits=(), inc=None, incby=1):
        waits = list(waits)
        ch = self.CHAIN.get(eng)
        if ch is not None and inc == ch and self.cnt[ch] > 0:
            waits.append((ch, self.cnt[ch]))
        val = None
        if inc is not None:
            self.cnt[inc] += incby
            val = self.cnt[inc]
        sems = self.sems
        wl = tuple((sems[s], v) for s, v in waits)
        s_inc = sems[inc] if inc is not None else None

        def run(e, fn=fn, wl=wl, s_inc=s_inc, incby=incby):
            for s, v in wl:
                e.wait_ge(s, v)
            inst = fn(e)
            if s_inc is not None:
                if inst is None:
                    inst = e.nop()
                inst.then_inc(s_inc, incby)
        self.q[eng].append(run)
        return val

    def c(self, name):
        return self.cnt[name]

    def emit(self):
        nc = self.nc
        with nc.Block() as block:
            @block.tensor
            def _(t):
                for f in self.q["t"]:
                    f(t)

            @block.vector
            def _(v):
                for f in self.q["v"]:
                    f(v)

            @block.scalar
            def _(s):
                for f in self.q["s"]:
                    f(s)

            @block.gpsimd
            def _(g):
                for f in self.q["g"]:
                    f(g)

            @block.sync
            def _(y):
                for f in self.q["y"]:
                    f(y)


def build_program(dbg=(), phases=("p0", "p1", "p2", "p3", "p4")):
    import os
    nc = bass.Bass("TRN2", target_bir_lowering=False,
                   detect_race_conditions=not os.environ.get("KB_NORACE"))
    stk = ExitStack()

    def din(name, shape, dt=F32):
        return nc.dram_tensor(name, shape, dt, kind="ExternalInput").ap()

    x_seq = din("x_seq", [S, EMB])
    x_own = din("x_own", [TC, EMB])
    Wq = din("Wq", [EMB, EMB])
    Wk = din("Wk", [L, EMB])
    Wv = din("Wv", [L, EMB])
    q2l = din("q2l", [H, D, L])
    vup = din("vup", [H, L, D])
    Wo = din("Wo", [EMB, EMB])
    wg_p = din("wg_p", [EMB, II])
    wu_p = din("wu_p", [EMB, II])
    wd_p = din("wd_p", [II, EMB])
    ident = din("ident", [128, 128], BF16)
    band3 = din("band3", [128, 384])
    caus2 = din("caus2", [128, 256])
    ones_f = din("ones_f", [1, 128])
    ones_b = din("ones_b", [1, 128], BF16)
    identf = din("identf", [128, 128])
    ident16 = din("ident16", [16, 16])

    out = nc.dram_tensor("out", [TC, EMB], F32, kind="ExternalOutput").ap()
    g_scr = nc.dram_tensor("g_scr", [TC, EMB], BF16, kind="Internal").ap()
    kp_scr = nc.dram_tensor("kp_scr", [128, S], BF16, kind="Internal").ap()
    bk_scr = nc.dram_tensor("bk_scr", [128, 16 * 384], BF16, kind="Internal").ap()
    wd_bf = nc.dram_tensor("wd_bf", [II, EMB], BF16, kind="Internal").ap()
    dbg_out = {}
    if "kpack" in dbg:
        dbg_out["d_kpack"] = nc.dram_tensor("d_kpack", [128, S], BF16, kind="ExternalOutput").ap()
        dbg_out["d_ksum"] = nc.dram_tensor("d_ksum", [1, S], BF16, kind="ExternalOutput").ap()
        dbg_out["d_xn"] = nc.dram_tensor("d_xn", [128, EMB], BF16, kind="ExternalOutput").ap()
    if "g" in dbg:
        dbg_out["d_g"] = nc.dram_tensor("d_g", [TC, EMB], BF16, kind="ExternalOutput").ap()
    if "q" in dbg:
        dbg_out["d_q"] = nc.dram_tensor("d_q", [128, H * TC], BF16, kind="ExternalOutput").ap()
        dbg_out["d_qsb"] = nc.dram_tensor("d_qsb", [128, 256], F32, kind="ExternalOutput").ap()
        dbg_out["d_negc"] = nc.dram_tensor("d_negc", [128, 256], F32, kind="ExternalOutput").ap()
    if "ctx" in dbg:
        dbg_out["d_ctx"] = nc.dram_tensor("d_ctx", [128, H * TC], BF16, kind="ExternalOutput").ap()

    pstk = stk.enter_context(ExitStack())

    def sb(st, name, shape, dt):
        return st.enter_context(nc.sbuf_tensor(name, shape, dt))

    A1 = sb(pstk, "A1", [128, 32768], BF16)
    A2 = sb(pstk, "A2", [128, 32768], BF16)
    A2f = A2.ap().bitcast(F32)

    ident_sb = sb(pstk, "ident_sb", [128, 128], BF16)
    band3_sb = sb(pstk, "band3_sb", [128, 384], F32)
    caus2_sb = sb(pstk, "caus2_sb", [128, 256], F32)
    ones_sb = sb(pstk, "ones_sb", [1, 128], F32)
    ones_bf_sb = sb(pstk, "ones_bf_sb", [1, 128], BF16)
    identf_sb = sb(pstk, "identf_sb", [128, 128], F32)
    ident16_sb = sb(pstk, "ident16_sb", [16, 16], F32)
    q2lt = sb(pstk, "q2lt", [64, H * 128], BF16)
    q2lrs = sb(pstk, "q2lrs", [128, H], BF16)
    vup_sb = sb(pstk, "vup_sb", [128, H * 128], BF16)
    qsb = sb(pstk, "qsb", [128, 256], F32)
    negc_bc = sb(pstk, "negc_bc", [128, 256], F32)
    sumg = [sb(pstk, f"sumg{i}", [128, 8], F32) for i in range(2)]
    sumtot = [sb(pstk, f"sumtot{i}", [128, 1], F32) for i in range(2)]
    rinv = [sb(pstk, f"rinv{i}", [128, 1], F32) for i in range(2)]
    negc15 = sb(pstk, "negc15", [128, 1], F32)
    cmaxr = sb(pstk, "cmaxr", [1, 16], F32)
    cminr = sb(pstk, "cminr", [1, 16], F32)
    cmax_bc = sb(pstk, "cmax_bc", [128, 16], F32)
    cmin_bc = sb(pstk, "cmin_bc", [128, 16], F32)
    tnc = sb(pstk, "tnc", [128, 16], F32)
    tnc2 = sb(pstk, "tnc2", [128, 16], F32)
    s1 = sb(pstk, "s1", [128, 1], F32)
    s2 = sb(pstk, "s2", [128, 1], F32)
    mvec = sb(pstk, "mvec", [128, 1], F32)
    msq = sb(pstk, "msq", [128, 1], F32)
    varv = sb(pstk, "varv", [128, 1], F32)
    rstd = sb(pstk, "rstd", [128, 1], F32)
    bias_ln = sb(pstk, "bias_ln", [128, 1], F32)

    PS = [pstk.enter_context(nc.psum_tensor(f"ps{i}", [128, 512], F32)) for i in range(8)]
    PSb = [p.ap().bitcast(BF16) for p in PS]

    kstk = stk.enter_context(ExitStack())
    kpack = sb(kstk, "kpack", [128, S], BF16)
    BK_sb = sb(kstk, "BK_sb", [128, 16 * 384], BF16)

    P = Prog(nc, stk)
    for s_ in ("pe", "dv", "ac", "xq0", "xq1", "cq", "qq", "vq", "kvq", "wgq0", "wgq1", "wgq2", "wdq", "oq", "wcq0", "wcq1", "bar"):
        P.sem(s_)

    def pe(fn, waits=()):
        return P.op("t", fn, waits=waits, inc="pe")

    def dv(fn, waits=()):
        return P.op("v", fn, waits=waits, inc="dv")

    def dvr(fn, waits=()):
        t = P.op("v", fn, waits=waits, inc="dv")
        P.op("v", lambda e, t=t: e.wait_ge(P.sems["dv"], t))
        return t

    def ac(fn, waits=()):
        return P.op("s", fn, waits=waits, inc="ac")

    def gq(fn, sem, waits=()):
        return P.op("g", fn, waits=waits, inc=sem, incby=16)

    def yq(fn, sem, waits=()):
        return P.op("y", fn, waits=waits, inc=sem, incby=16)

    def oq(fn, waits=()):
        return P.op("y", fn, waits=waits, inc="oq", incby=16)

    DMA_SEMS = ("xq0", "xq1", "cq", "qq", "vq", "kvq", "wgq0", "wgq1", "wgq2", "wdq", "oq", "wcq0", "wcq1")

    def barrier():
        tgt = P.cnt["bar"] + 5
        tail = {"t": ("pe",), "v": ("dv",), "s": ("ac",), "g": (), "y": DMA_SEMS}
        for e in Prog.ENGS:
            w = tuple((s_, P.c(s_)) for s_ in tail[e])
            P.op(e, lambda en: en.nop(), waits=w, inc="bar")
        for e in Prog.ENGS:
            P.op(e, lambda en, tgt=tgt: en.wait_ge(P.sems["bar"], tgt))

    yq(lambda e: e.dma_start(out=ident_sb[:], in_=ident), "cq")
    yq(lambda e: e.dma_start(out=band3_sb[:], in_=band3), "cq")
    yq(lambda e: e.dma_start(out=caus2_sb[:], in_=caus2), "cq")
    yq(lambda e: e.dma_start(out=ones_sb[:], in_=ones_f), "cq")
    yq(lambda e: e.dma_start(out=ones_bf_sb[:], in_=ones_b), "cq")
    yq(lambda e: e.dma_start(out=identf_sb[:], in_=identf), "cq")
    w_const = yq(lambda e: e.dma_start(out=ident16_sb[:], in_=ident16), "cq")

    p0stk = stk.enter_context(ExitStack())
    ksum = sb(p0stk, "ksum", [1, S], BF16)
    wkte = sb(p0stk, "wkte", [128, NJ * 65], BF16)
    wvt = sb(p0stk, "wvt", [128, NJ * 64], BF16)
    wk_bf = sb(p0stk, "wk_bf", [64, EMB], BF16)
    wv_bf = sb(p0stk, "wv_bf", [64, EMB], BF16)
    wkcs = sb(p0stk, "wkcs", [128, NJ], F32)
    gq(lambda e: e.dma_start(out=wk_bf[:], in_=Wk), "kvq")
    w_kv = gq(lambda e: e.dma_start(out=wv_bf[:], in_=Wv), "kvq")

    ev_prev = {}
    for j in range(NJ):
        b = j % 2
        w = [("kvq", w_kv), ("cq", w_const)] if j == 0 else []
        if j >= 2:
            w.append(("dv", ev_prev[j - 2]))
        tk = pe(lambda e, j=j, b=b: e.transpose(PSb[b][:, 0:64], wk_bf[0:64, j*128:(j+1)*128], ident_sb[0:64, 0:64]),
                waits=tuple(w))
        dv(lambda e, j=j, b=b: e.tensor_copy(wkte[:, j*65:j*65+64], PSb[b][:, 0:64]), waits=(("pe", tk),))
        dvr(lambda e, j=j, b=b: e.tensor_reduce(wkcs[:, j:j+1], PSb[b][:, 0:64], AX.X, ALU.add))
        ev_prev[j] = dv(lambda e, j=j: e.tensor_copy(wkte[:, j*65+64:j*65+65], wkcs[:, j:j+1]))
    for j in range(NJ):
        b = j % 2
        w = [("dv", ev_prev[j - 2])] if j >= 2 else [("dv", ev_prev[NJ - 1])]
        tk = pe(lambda e, j=j, b=b: e.transpose(PSb[b][:, 0:64], wv_bf[0:64, j*128:(j+1)*128], ident_sb[0:64, 0:64]),
                waits=tuple(w))
        ev_prev[j] = dv(lambda e, j=j, b=b: e.tensor_copy(wvt[:, j*64:(j+1)*64], PSb[b][:, 0:64]),
                        waits=(("pe", tk),))

    q2l_f = sb(p0stk, "q2l_f", [128, 64], F32)
    q2l_bf = sb(p0stk, "q2l_bf", [128, 64], BF16)
    rs_f = sb(p0stk, "rs_f", [128, 1], F32)
    last_q2l = ev_prev[NJ - 1]
    d1 = None
    for h in range(H):
        w_l = (("dv", d1),) if d1 is not None else ()
        lw = yq(lambda e, h=h: e.dma_start(out=q2l_f[:], in_=q2l[h]), "qq", waits=w_l)
        d1 = dv(lambda e: e.tensor_copy(q2l_bf[:], q2l_f[:]), waits=(("qq", lw), ("dv", last_q2l)))
        dvr(lambda e: e.tensor_reduce(rs_f[:], q2l_bf[:], AX.X, ALU.add))
        d2 = dv(lambda e, h=h: e.tensor_copy(q2lrs[:, h:h+1], rs_f[:]))
        tk = pe(lambda e, h=h: e.transpose(PSb[h % 2][0:64, 0:128], q2l_bf[:], ident_sb[:]), waits=(("dv", d2),))
        last_q2l = dv(lambda e, h=h: e.tensor_copy(q2lt[0:64, h*128:(h+1)*128], PSb[h % 2][0:64, 0:128]),
                      waits=(("pe", tk),))
    gq(lambda e: e.dma_start(out=vup_sb.ap()[64:128, :].rearrange("l (h d) -> l h d", d=128), in_=vup.rearrange("h l d -> l h d")), "vq")

    xstage = [sb(p0stk, "xs0", [128, EMB], F32), sb(p0stk, "xs1", [128, EMB], F32)]
    xn_st2 = [A2.ap()[:, 2048:4096], A2.ap()[:, 4096:6144]]
    xnfm2 = [A2.ap()[:, 6144:8192], A2.ap()[:, 8192:10240]]
    trash = A2.ap()[:, 0:2048]
    LNT = [{n: sb(p0stk, f"ln_{n}{k}", [128, 1], F32)
            for n in ("s1", "s2", "mvec", "msq", "varv", "rstd", "bias")} for k in range(2)]

    def ln_tile(src_ap, buf, prev):
        T = LNT[buf]
        xn_dst = xn_st2[buf]
        w_load = []
        if prev.get("axn2"):
            w_load.append(("ac", prev["axn2"]))
        if prev.get("s1red2"):
            w_load.append(("dv", prev["s1red2"]))
        xsem = "xq0" if buf == 0 else "xq1"
        lw = yq(lambda e, src_ap=src_ap, buf=buf: e.dma_start(out=xstage[buf][:], in_=src_ap),
                xsem, waits=tuple(w_load))
        s1red = dvr(lambda e, buf=buf, T=T: e.tensor_reduce(T["s1"][:], xstage[buf][:], AX.X, ALU.add),
                    waits=((xsem, lw),))
        w_sq = [(xsem, lw)]
        if prev.get("msq2"):
            w_sq.append(("dv", prev["msq2"]))
        a_sq = ac(lambda e, buf=buf, T=T: e.activation(trash[:], xstage[buf][:], AF.Square,
                                                       accum_out=T["s2"][:]),
                  waits=tuple(w_sq))
        dv(lambda e, T=T: e.tensor_scalar(T["mvec"][:], T["s1"][:], 1.0 / EMB, None, ALU.mult))
        msq_t = dv(lambda e, T=T: e.tensor_scalar(T["msq"][:], T["s2"][:], 1.0 / EMB, None, ALU.mult),
                   waits=(("ac", a_sq),))
        w_tt = [("ac", prev["a22"])] if prev.get("a22") else []
        dv(lambda e, T=T: e.tensor_tensor(T["varv"][:], T["mvec"][:], T["mvec"][:], ALU.mult),
           waits=tuple(w_tt))
        d_v = dv(lambda e, T=T: e.scalar_tensor_tensor(T["varv"][:], T["msq"][:], 1.0e-5, T["varv"][:],
                                                       ALU.add, ALU.subtract))
        ac(lambda e, T=T: e.activation(T["varv"][:], T["varv"][:], AF.Ln), waits=(("dv", d_v),))
        a2_ = ac(lambda e, T=T: e.activation(T["rstd"][:], T["varv"][:], AF.Exp, scale=-0.5))
        d2_ = dv(lambda e, T=T: e.scalar_tensor_tensor(T["bias"][:], T["mvec"][:], -1.0, T["rstd"][:],
                                                       ALU.mult, ALU.mult),
                 waits=(("ac", a2_),))
        w_xn = [("dv", d2_)]
        if prev.get("lastT2"):
            w_xn.append(("pe", prev["lastT2"]))
        axn = ac(lambda e, buf=buf, T=T, xn_dst=xn_dst: e.activation(
            xn_dst[:], xstage[buf][:], AF.Identity, bias=T["bias"][:], scale=T["rstd"][:]),
            waits=tuple(w_xn))
        return {"axn": axn, "s1red": s1red, "msq": msq_t, "a2": a2_, "lw": lw}

    if "p0" in phases:
        hist = []
        kv_prev = None
        ptguard = {}
        PTB = (2, 3, 6, 7)
        wdstage = [A2.ap()[:, 16384:18432], A2.ap()[:, 18432:20480]]
        wd_rr = wd_p.rearrange("(c p) e -> c p e", p=128)
        wd_st = {}

        def prev_of(i):
            p = {}
            if i >= 2:
                p["axn2"] = hist[i-2]["axn"]
                p["s1red2"] = hist[i-2]["s1red"]
                p["msq2"] = hist[i-2]["msq"]
                p["a22"] = hist[i-2]["a2"]
                p["lastT2"] = hist[i-2]["lastT"]
            return p

        def emit_ptkv(i):
            nonlocal kv_prev
            buf = i % 2
            tks = hist[i]
            axn = tks["axn"]
            seq = i < NSEQT
            it2 = i - NSEQT
            evs = []
            tk = None
            for j in range(NJ):
                b = PTB[j % 4]
                w = list(ptguard.get(j % 4, []))
                if j < 4:
                    w.append(("ac", axn))
                tk = pe(lambda e, j=j, b=b, buf=buf: e.transpose(
                    PSb[b][:, 0:128], xn_st2[buf][:, j*128:(j+1)*128], ident_sb[:]), waits=tuple(w))
                if seq:
                    ev = dv(lambda e, j=j, b=b, buf=buf: e.tensor_copy(
                        xnfm2[buf][:, j*128:(j+1)*128], PSb[b][:, 0:128]), waits=(("pe", tk),))
                else:
                    ev = dv(lambda e, j=j, b=b, it2=it2: e.tensor_copy(
                        A1[:, j*2048 + it2*128: j*2048 + (it2+1)*128], PSb[b][:, 0:128]),
                        waits=(("pe", tk),))
                ptguard[j % 4] = [("dv", ev)]
                evs.append(ev)
            tks["lastT"] = tk
            if seq:
                for j in range(NJ):
                    w = [("dv", evs[j])]
                    if j == 0 and kv_prev is not None:
                        w.append(("dv", kv_prev))
                    pe(lambda e, j=j, buf=buf: e.matmul(PS[4][0:65, 0:128], wkte[:, j*65:(j+1)*65],
                                                        xnfm2[buf][:, j*128:(j+1)*128],
                                                        start=(j == 0), stop=(j == NJ - 1)),
                       waits=tuple(w))
                    kv_t = pe(lambda e, j=j, buf=buf: e.matmul(PS[5][0:64, 0:128], wvt[:, j*64:(j+1)*64],
                                                               xnfm2[buf][:, j*128:(j+1)*128],
                                                               start=(j == 0), stop=(j == NJ - 1)))
                dv(lambda e, i=i: e.tensor_copy(kpack[0:64, i*128:(i+1)*128], PS[4][0:64, 0:128]),
                   waits=(("pe", kv_t),))
                dv(lambda e, i=i: e.tensor_copy(ksum[0:1, i*128:(i+1)*128], PS[4][64:65, 0:128]))
                kv_prev = dv(lambda e, i=i: e.tensor_copy(kpack[64:128, i*128:(i+1)*128], PS[5][0:64, 0:128]))

        def emit_cmax_bk():
            for j in range(16):
                w0 = max(0, EXT[j] - 3) * 128
                w1 = EXT[j] * 128
                dvr(lambda e, j=j, w0=w0, w1=w1: e.tensor_reduce(cmaxr[:, j:j+1], ksum[:, w0:w1], AX.X, ALU.max))
                dvr(lambda e, j=j, w0=w0, w1=w1: e.tensor_reduce(cminr[:, j:j+1], ksum[:, w0:w1], AX.X, ALU.min))
            d_cm = P.c("dv")
            pe(lambda e: e.matmul(PS[0][0:128, 0:16], ones_sb[:], cmaxr[:, 0:16], start=True, stop=True),
               waits=(("dv", d_cm),))
            t2 = pe(lambda e: e.matmul(PS[1][0:128, 0:16], ones_sb[:], cminr[:, 0:16], start=True, stop=True))
            dv(lambda e: e.tensor_copy(cmax_bc[:], PS[0][0:128, 0:16]), waits=(("pe", t2),))
            e2 = dv(lambda e: e.tensor_copy(cmin_bc[:], PS[1][0:128, 0:16]))
            g0 = {("bk", 0): [("dv", e2)], ("bk", 1): [("dv", e2)]}
            for j in range(16):
                e0 = max(0, EXT[j] - 3)
                nwc = EXT[j] - e0
                woff = (3 - nwc) * 128
                for wci in range(nwc):
                    w = g0.pop(("bk", wci % 2), [])
                    if j == 0 and wci == 0:
                        w += [("cq", P.c("cq"))]
                    bk_mm = pe(lambda e, j=j, wci=wci, e0=e0: e.matmul(
                        PS[0][0:128, 0:128] if wci % 2 == 0 else PS[1][0:128, 0:128],
                        ones_bf_sb[:], ksum[:, (e0+wci)*128:(e0+wci+1)*128],
                        start=True, stop=True), waits=tuple(w))
                    d_bk = dv(lambda e, j=j, wci=wci, woff=woff: e.tensor_tensor(
                        BK_sb[:, j*384 + woff + wci*128: j*384 + woff + (wci+1)*128],
                        band3_sb[:, woff + wci*128: woff + (wci+1)*128],
                        (PS[0][0:128, 0:128] if wci % 2 == 0 else PS[1][0:128, 0:128]), ALU.mult),
                        waits=(("pe", bk_mm),))
                    g0[("bk", wci % 2)] = [("dv", d_bk)]
                if woff > 0:
                    dv(lambda e, j=j, woff=woff: e.tensor_scalar(
                        BK_sb[:, j*384: j*384 + woff], band3_sb[:, 0:woff], 0.0, None, ALU.mult))
            d_sp0 = P.c("dv")
            oq(lambda e: e.dma_start(out=kp_scr, in_=kpack[:]), waits=(("dv", d_sp0),))
            oq(lambda e: e.dma_start(out=bk_scr, in_=BK_sb[:]))

        pend = None
        for i in range(NSEQT + NTILE):
            buf = i % 2
            src = x_seq[i*128:(i+1)*128, :] if i < NSEQT else \
                x_own[(i-NSEQT)*128:(i-NSEQT+1)*128, :]
            tks = ln_tile(src, buf, prev_of(i))
            hist.append(tks)
            if pend is not None:
                emit_ptkv(pend)
            if i == NSEQT:
                emit_cmax_bk()
            if i < NSEQT:
                for rr in range(2):
                    row = i * 2 + rr
                    wsem = "wdq" if row % 2 == 0 else "vq"
                    ssem = "wcq0" if row % 2 == 0 else "wcq1"
                    w_l = [(ssem, wd_st[row % 2])] if row >= 2 else []
                    wl_t = gq(lambda e, row=row: e.dma_start(out=wdstage[row % 2][:], in_=wd_rr[row]),
                              wsem, waits=tuple(w_l))
                    wd_st[row % 2] = yq(lambda e, row=row: e.dma_start(
                        out=wd_bf[row*128:(row+1)*128, :], in_=wdstage[row % 2][:]),
                        ssem, waits=((wsem, wl_t),))
            pend = i
        emit_ptkv(pend)

        if "kpack" in dbg:
            dq = P.c("dv")
            oq(lambda e: e.dma_start(out=dbg_out["d_kpack"], in_=kpack[:]), waits=(("dv", dq),))
            oq(lambda e: e.dma_start(out=dbg_out["d_ksum"], in_=ksum[:]))
            oq(lambda e: e.dma_start(out=dbg_out["d_xn"], in_=A1[:, 0:2048]))


    barrier()
    p0stk.close()
    kstk.close()

    if "p1" in phases:
        p1stk = stk.enter_context(ExitStack())
        acc = A2f
        tb = 16384
        h_all = [A2.ap()[:, tb:tb + 4096], A2.ap()[:, tb + 4096:tb + 8192]]
        habs = A2.ap()[:, tb + 8192:tb + 9216]
        hsp = A2.ap()[:, tb + 9216:tb + 10240]
        hsT = [A2.ap()[:, tb + 10240:tb + 11264], A2.ap()[:, tb + 11264:tb + 12288]]
        spt = [A2.ap()[:, tb + 12288 + k*256: tb + 12288 + (k+1)*256] for k in range(3)]
        gstage = A2.ap()[:, tb + 13056:tb + 15104]
        tmpg = [sb(p1stk, f"tmpg{i}", [128, 128], F32) for i in range(2)]
        SLOTS = 3
        gsems = ("wgq0", "wgq1", "wgq2")
        wgu = [sb(p1stk, f"wgu{i}", [128, NJ * 256], BF16) for i in range(SLOTS)]
        wdbuf = [sb(p1stk, f"wdb{i}", [128, EMB], BF16) for i in range(8)]

        wg_r = wg_p.rearrange("(j p) i -> p j i", p=128)
        wu_r = wu_p.rearrange("(j p) i -> p j i", p=128)

        NBAT = NPASS * NB
        pend_gu = {}
        guards = {}
        Z = {"prev_sp": None, "prev_tev": None, "h_done": {}, "sp_done": {},
             "pt_done": {}, "gadd": {}, "pair_entry": {}, "wd_cur": None,
             "dn_last": None, "a_g": {}, "og_last": None, "tmpg_last": {}}

        def emit_sp(bp, mi):
            hb = h_all[bp % 2]
            w_ab = [("dv", Z["h_done"][bp % 2])]
            if Z["prev_sp"] is not None:
                w_ab.append(("dv", Z["prev_sp"]))
            a_ab = ac(lambda e, hb=hb, mi=mi: e.activation(habs[:], hb[:, mi*1024:(mi+1)*1024], AF.Abs),
                      waits=tuple(w_ab))
            avs = [habs[:, m_*256:(m_+1)*256] for m_ in range(4)]
            hvs = [hb[:, mi*1024 + m_*256: mi*1024 + (m_+1)*256] for m_ in range(4)]
            dv(lambda e, avs=avs: e.tensor_tensor(spt[0][:], avs[0], avs[1], ALU.min), waits=(("ac", a_ab),))
            dv(lambda e, avs=avs: e.tensor_tensor(spt[1][:], avs[0], avs[1], ALU.max))
            dv(lambda e, avs=avs: e.tensor_tensor(spt[2][:], avs[2], avs[3], ALU.min))
            dv(lambda e: e.tensor_tensor(spt[0][:], spt[0][:], spt[2][:], ALU.max))
            dv(lambda e, avs=avs: e.tensor_tensor(spt[2][:], avs[2], avs[3], ALU.max))
            dv(lambda e: e.tensor_tensor(spt[1][:], spt[1][:], spt[2][:], ALU.min))
            dv(lambda e: e.tensor_tensor(spt[0][:], spt[0][:], spt[1][:], ALU.max))
            sp_d = None
            for m_ in range(4):
                dv(lambda e, a=avs[m_]: e.tensor_tensor(spt[1][:], a, spt[0][:], ALU.is_ge))
                w_sp = ()
                if m_ == 0 and Z["prev_tev"] is not None:
                    w_sp = (("pe", Z["prev_tev"]),)
                sp_d = dv(lambda e, hh=hvs[m_], m_=m_: e.tensor_tensor(hsp[:, m_*256:(m_+1)*256],
                                                                       hh, spt[1][:], ALU.mult), waits=w_sp)
            Z["prev_sp"] = sp_d
            Z["sp_done"][mi] = sp_d

        def emit_PT(bp, mi):
            pair = (4, 5) if mi % 2 == 0 else (6, 7)
            pte = {}
            tk = None
            for cb in range(8):
                bnk = pair[cb % 2]
                w = []
                if cb < 2:
                    w.append(("dv", Z["sp_done"][mi]))
                    w += Z["pair_entry"].pop((pair, cb % 2), [])
                else:
                    w.append(pte[cb - 2])
                tk = pe(lambda e, cb=cb, bnk=bnk: e.transpose(PSb[bnk][:, 0:128],
                                                              hsp[:, cb*128:(cb+1)*128], ident_sb[:]),
                        waits=tuple(w))
                if cb % 2 == 0:
                    pte[cb] = ("dv", dv(lambda e, cb=cb, bnk=bnk, mi=mi: e.tensor_copy(
                        hsT[mi % 2][:, cb*128:(cb+1)*128], PSb[bnk][:, 0:128]), waits=(("pe", tk),)))
                else:
                    pte[cb] = ("ac", ac(lambda e, cb=cb, bnk=bnk, mi=mi: e.activation(
                        hsT[mi % 2][:, cb*128:(cb+1)*128], PSb[bnk][:, 0:128], AF.Copy),
                        waits=(("pe", tk),)))
            Z["prev_tev"] = tk
            Z["pt_done"][mi] = [pte[6], pte[7]]

        def emit_dn(bp, mi, setk):
            pair = (4, 5) if mi % 2 == 0 else (6, 7)
            last = None
            for pi in range(2):
                bnk = pair[pi]
                ch0 = (setk*8 + pi*4) * 128
                for ci in range(8):
                    w = []
                    if ci == 0:
                        if setk == 0:
                            if pi == 0:
                                w += Z["pt_done"][mi]
                        else:
                            w.append(("dv", Z["gadd"][(mi, 0, bnk)]))
                        if mi == 0 and setk == 0 and pi == 0:
                            w.append(("wcq0", Z["wd_cur"]))
                    last = pe(lambda e, mi=mi, ci=ci, bnk=bnk, ch0=ch0:
                              e.matmul(PS[bnk][:, 0:512],
                                       hsT[mi % 2][:, ci*128:(ci+1)*128],
                                       wdbuf[ci][:, ch0:ch0+512],
                                       start=(ci == 0), stop=(ci == 7)), waits=tuple(w))
            Z["dn_last"] = last

        def emit_gadd(bp, mi, setk):
            pair = (4, 5) if mi % 2 == 0 else (6, 7)
            b_ = bp % NB
            ticks = {}
            for pi in range(2):
                bnk = pair[pi]
                e0 = setk*1024 + pi*512
                w = []
                if pi == 0:
                    w.append(("pe", Z["dn_last"]))
                if b_ == 0 and setk == 0 and pi == 0 and (mi in Z["a_g"]):
                    w.append(("ac", Z["a_g"][mi]))
                if b_ == 0:
                    t = dv(lambda e, mi=mi, e0=e0, bnk=bnk: e.tensor_copy(
                        acc[:, mi*2048 + e0: mi*2048 + e0 + 512], PS[bnk][:, 0:512]), waits=tuple(w))
                else:
                    t = dv(lambda e, mi=mi, e0=e0, bnk=bnk: e.tensor_tensor(
                        acc[:, mi*2048 + e0: mi*2048 + e0 + 512],
                        acc[:, mi*2048 + e0: mi*2048 + e0 + 512], PS[bnk][:, 0:512], ALU.add),
                        waits=tuple(w))
                Z["gadd"][(mi, setk, bnk)] = t
                ticks[pi] = t
            if setk == 1:
                for pi in range(2):
                    Z["pair_entry"][(pair, pi)] = [("dv", ticks[pi])]

        def emit_spill(p_i):
            for mi in range(NMIC):
                mt = p_i * NMIC + mi
                pair = (4, 5) if mi % 2 == 0 else (6, 7)
                w = [("dv", Z["gadd"][(mi, 1, pair[1])])]
                if Z["og_last"] is not None:
                    w.append(("oq", Z["og_last"]))
                a_g = ac(lambda e, mi=mi: e.activation(gstage[:], acc[:, mi*2048:(mi+1)*2048], AF.Silu),
                         waits=tuple(w))
                og = oq(lambda e, mt=mt: e.dma_start(out=g_scr[mt*128:(mt+1)*128, :], in_=gstage[:]),
                        waits=(("ac", a_g),))
                Z["a_g"][mi] = a_g
                Z["og_last"] = og

        def emit_down_pos(bp, s):
            if s == 0:
                emit_sp(bp, 0); emit_PT(bp, 0); emit_sp(bp, 1)
            elif s == 1:
                emit_dn(bp, 0, 0); emit_gadd(bp, 0, 0); emit_PT(bp, 1); emit_sp(bp, 2)
            elif s == 2:
                emit_dn(bp, 0, 1); emit_gadd(bp, 0, 1)
                emit_dn(bp, 1, 0); emit_gadd(bp, 1, 0)
            elif s == 3:
                emit_PT(bp, 2); emit_sp(bp, 3); emit_dn(bp, 1, 1); emit_gadd(bp, 1, 1)
            elif s == 4:
                emit_dn(bp, 2, 0); emit_gadd(bp, 2, 0); emit_PT(bp, 3)
            elif s == 5:
                emit_dn(bp, 2, 1); emit_gadd(bp, 2, 1)
                emit_dn(bp, 3, 0); emit_gadd(bp, 3, 0)
            elif s == 6:
                emit_dn(bp, 3, 1); emit_gadd(bp, 3, 1)
                if bp % NB == NB - 1:
                    emit_spill(bp // NB)

        DPRV = None
        slice_ctr = 0
        for bat in range(NBAT):
            p_i, b = divmod(bat, NB)
            m0 = p_i * NMIC
            hb = h_all[bat % 2]
            last_h = None
            for s in range(8):
                m, c = s // 2, s % 2
                slot = slice_ctr % SLOTS
                gsem = gsems[slot]
                c0 = m*2048 + 256*b + c*128
                w = [("pe", pend_gu[slot])] if slot in pend_gu else []
                gq(lambda e, slot=slot, c0=c0: e.dma_start(
                    out=wgu[slot].ap().rearrange("p (j i) -> p j i", i=256)[:, :, 0:128],
                    in_=wg_r[:, :, c0:c0+128]), gsem, waits=tuple(w))
                lt = gq(lambda e, slot=slot, c0=c0: e.dma_start(
                    out=wgu[slot].ap().rearrange("p (j i) -> p j i", i=256)[:, :, 128:256],
                    in_=wu_r[:, :, c0:c0+128]), gsem)
                for mi in range(NMIC):
                    mt = m0 + mi
                    bnk = mi
                    for j in range(NJ):
                        w = []
                        if j == 0:
                            if mi == 0:
                                w.append((gsem, lt))
                            w += guards.pop((bnk, c), [])
                        gu_mm = pe(lambda e, c=c, j=j, mt=mt, slot=slot, bnk=bnk:
                                   e.matmul(PS[bnk][:, c*256:(c+1)*256],
                                            A1[:, j*2048 + mt*128: j*2048 + (mt+1)*128],
                                            wgu[slot][:, j*256:(j+1)*256],
                                            start=(j == 0), stop=(j == NJ - 1)), waits=tuple(w))
                    w_si = [("pe", gu_mm)]
                    if (mi % 2) in Z["tmpg_last"]:
                        w_si.append(("dv", Z["tmpg_last"][mi % 2]))
                    a_si = ac(lambda e, c=c, bnk=bnk, mi=mi: e.activation(
                        tmpg[mi % 2][:], PS[bnk][:, c*256:c*256+128], AF.Silu), waits=tuple(w_si))
                    hslice = hb[:, mi*1024 + m*256 + c*128: mi*1024 + m*256 + (c+1)*128]
                    last_h = dv(lambda e, hslice=hslice, c=c, bnk=bnk, mi=mi:
                                e.tensor_tensor(hslice, tmpg[mi % 2][:],
                                                PS[bnk][:, c*256+128:(c+1)*256], ALU.mult),
                                waits=(("ac", a_si),))
                    Z["tmpg_last"][mi % 2] = last_h
                    guards[(bnk, c)] = [("dv", last_h)]
                pend_gu[slot] = gu_mm
                slice_ctr += 1
                if DPRV is not None:
                    emit_down_pos(DPRV, s)
                if s == 6:
                    wt = None
                    for ci in range(8):
                        row = 16 * (ci // 2) + 2 * b + (ci % 2)
                        w = [("pe", Z["dn_last"])] if Z["dn_last"] is not None else []
                        wt = yq(lambda e, ci=ci, row=row: e.dma_start(
                            out=wdbuf[ci][:], in_=wd_bf[row*128:(row+1)*128, :]),
                            "wcq0", waits=tuple(w))
                    Z["wd_cur"] = wt
            Z["h_done"][bat % 2] = last_h
            DPRV = bat
        for s in range(8):
            emit_down_pos(DPRV, s)

        if "g" in dbg:
            ov = P.c("oq")
            for mt in range(NTILE):
                lw = yq(lambda e, mt=mt: e.dma_start(out=gstage[:], in_=g_scr[mt*128:(mt+1)*128, :]),
                        "xq0", waits=(("oq", ov),))
                ov = oq(lambda e, mt=mt: e.dma_start(out=dbg_out["d_g"][mt*128:(mt+1)*128, :], in_=gstage[:]),
                        waits=(("xq0", lw),))
        barrier()
        p1stk.close()

    if ("p2" in phases) or ("p3" in phases):
        k2stk = stk.enter_context(ExitStack())
        kp3 = sb(k2stk, "kpack2", [128, S], BF16)
        BK3 = sb(k2stk, "BK_sb2", [128, 16 * 384], BF16)
        yq(lambda e: e.dma_start(out=kp3[:], in_=kp_scr), "cq")
        rl2 = yq(lambda e: e.dma_start(out=BK3[:], in_=bk_scr), "cq")

    if "p2" in phases:
        p2stk = stk.enter_context(ExitStack())
        wqh = [sb(p2stk, f"wqh{i}", [128, EMB], BF16) for i in range(2)]
        wqT = sb(p2stk, "wqT", [128, EMB], BF16)
        qs_col = sb(p2stk, "qs_col", [128, 16], F32)
        qsT = sb(p2stk, "qsT", [16, 128], F32)
        qtmax = sb(p2stk, "qtmax", [16, 1], F32)
        qtmin = sb(p2stk, "qtmin", [16, 1], F32)
        tb1 = sb(p2stk, "tb1", [16, 1], F32)
        tb2 = sb(p2stk, "tb2", [16, 1], F32)
        negc_row = sb(p2stk, "negc_row", [1, 16], F32)
        guards2 = {}
        pend_wq = {}
        pend_wqT = None
        for h in range(NHEADS):
            slot = h % 2
            gsem = "wgq0" if slot == 0 else "wgq1"
            w = [("pe", pend_wq[slot])] if slot in pend_wq else []
            lw = gq(lambda e, h=h, slot=slot: e.dma_start(out=wqh[slot][:], in_=Wq[h*128:(h+1)*128, :]),
                    gsem, waits=tuple(w))
            tevq = []
            for j in range(NJ):
                bnk = j % 2
                w = [(gsem, lw)] if j == 0 else []
                w += guards2.pop(("t", bnk), [])
                if pend_wqT is not None and j < 2:
                    w.append(("pe", pend_wqT))
                tk = pe(lambda e, j=j, bnk=bnk, slot=slot: e.transpose(
                    PSb[bnk][:, 0:128], wqh[slot][:, j*128:(j+1)*128], ident_sb[:]), waits=tuple(w))
                tevq.append(dv(lambda e, j=j, bnk=bnk: e.tensor_copy(wqT[:, j*128:(j+1)*128], PSb[bnk][:, 0:128]),
                               waits=(("pe", tk),)))
                guards2[("t", bnk)] = [("dv", tevq[-1])]
            for tc in range(4):
                bnk = 2 + tc % 2
                for j in range(NJ):
                    w = []
                    if j == 0:
                        w += guards2.pop(("q", bnk), []) + [("dv", tevq[-1])]
                    q_mm = pe(lambda e, j=j, tc=tc, h=h, bnk=bnk: e.matmul(
                        PS[bnk][:, 0:512], wqT[:, j*128:(j+1)*128],
                        A1[:, j*2048 + tc*512: j*2048 + (tc+1)*512],
                        start=(j == 0), stop=(j == NJ - 1)), waits=tuple(w))
                ev1 = dv(lambda e, tc=tc, h=h, bnk=bnk: e.tensor_copy(
                    A2[:, h*2048 + tc*512: h*2048 + (tc+1)*512],
                    PS[bnk][:, 0:512]), waits=(("pe", q_mm),))
                guards2[("q", bnk)] = [("dv", ev1)]
                pend_wqT = q_mm
            pend_wq[slot] = q_mm
            for mt in range(NTILE):
                w = [("dv", P.c("dv"))] if mt == 0 else []
                if mt == 0:
                    w += guards2.pop(("qs",), [])
                qs_mm = pe(lambda e, mt=mt, h=h: e.matmul(
                    PS[4][:, mt:mt+1], A2[:, h*2048 + mt*128: h*2048 + (mt+1)*128],
                    q2lrs[:, h:h+1], start=True, stop=True), waits=tuple(w))
            d_qs = dv(lambda e: e.tensor_copy(qs_col[:], PS[4][:, 0:16]), waits=(("pe", qs_mm),))
            guards2[("qs",)] = [("dv", d_qs)]
            dv(lambda e, h=h: e.tensor_scalar(qsb[:, h*16:(h+1)*16], qs_col[:], 0.0625, None, ALU.mult))
            d_q16 = P.c("dv")
            dv(lambda e, h=h: e.tensor_tensor(tnc[:], qsb[:, h*16:(h+1)*16], cmax_bc[:], ALU.mult))
            dv(lambda e, h=h: e.tensor_tensor(tnc2[:], qsb[:, h*16:(h+1)*16], cmin_bc[:], ALU.mult))
            dv(lambda e: e.tensor_tensor(tnc[:], tnc[:], tnc2[:], ALU.max))
            dv(lambda e, h=h: e.tensor_scalar(negc_bc[:, h*16:(h+1)*16], tnc[:], -1.0, -10.0,
                                              ALU.mult, ALU.add))
        if "q" in dbg:
            dq = P.c("dv")
            oq(lambda e: e.dma_start(out=dbg_out["d_q"][:, 0:NHEADS*2048], in_=A2[:, 0:NHEADS*2048]), waits=(("dv", dq),))
            oq(lambda e: e.dma_start(out=dbg_out["d_qsb"][:, 0:NHEADS*16], in_=qsb[:, 0:NHEADS*16]))
            oq(lambda e: e.dma_start(out=dbg_out["d_negc"][:, 0:NHEADS*16], in_=negc_bc[:, 0:NHEADS*16]))
        barrier()
        p2stk.close()

    if "p3" in phases:
        p3stk = stk.enter_context(ExitStack())
        A_sb = sb(p3stk, "A_sb", [128, S], BF16)
        Vx_sb = sb(p3stk, "Vx_sb", [128, S], BF16)
        probs = [sb(p3stk, f"probs{i}", [128, S], BF16) for i in range(2)]
        pts_all = sb(p3stk, "pts_all", [128, S], BF16)
        ctxt_f = sb(p3stk, "ctxt_f", [128, 128], F32)
        g3 = {("bk", 0): [("cq", rl2)], ("bk", 1): [("cq", rl2)]}
        for h in range(NHEADS):
            for grp in range(8):
                bnk = grp % 2
                w = g3.pop(("av", bnk), [])
                if h == 0 and grp < 2:
                    w += g3.pop(("bk", bnk), [])
                av_mm = pe(lambda e, h=h, grp=grp, bnk=bnk: e.matmul(
                    PS[bnk][:, 0:512], q2lt[0:64, h*128:(h+1)*128],
                    kp3[0:64, grp*512:(grp+1)*512], start=True, stop=True), waits=tuple(w))
                d_av = dv(lambda e, grp=grp, bnk=bnk: e.tensor_scalar(
                    A_sb[:, grp*512:(grp+1)*512], PS[bnk][:], 0.125, None, ALU.mult),
                    waits=(("pe", av_mm),))
                g3[("av", bnk)] = [("dv", d_av)]
            for grp in range(8):
                bnk = grp % 2
                for cc in range(4):
                    c = grp * 4 + cc
                    w = g3.pop(("av", bnk), []) if cc == 0 else []
                    av_mm = pe(lambda e, h=h, c=c, cc=cc, bnk=bnk: e.matmul(
                        PS[bnk][:, cc*128:(cc+1)*128], kp3[64:128, c*128:(c+1)*128],
                        vup_sb[64:128, h*128:(h+1)*128], start=(cc == 0), stop=(cc == 3)), waits=tuple(w))
                d_av = dv(lambda e, grp=grp, bnk=bnk: e.tensor_copy(
                    Vx_sb[:, grp*512:(grp+1)*512], PS[bnk][:]), waits=(("pe", av_mm),))
                g3[("av", bnk)] = [("dv", d_av)]
            def emit_scores_mm(h, j):
                E = EXT[j]
                ngrp = (E + 3) // 4
                pp = j % 2
                last_ex = None
                for grp in range(ngrp):
                    bnk = 2 + grp % 2
                    nc_in_g = min(4, E - grp * 4)
                    w = g3.pop(("sc", bnk), [])
                    if grp == 0:
                        w += g3.pop(("ptw", pp), [])
                        w += g3.pop(("sg", pp), [])
                    sc_mm = pe(lambda e, h=h, j=j, grp=grp, bnk=bnk, nc_in_g=nc_in_g: e.matmul(
                        PS[bnk][:, 0:nc_in_g*128],
                        A2[:, h*2048 + j*128:(h*2048 + (j+1)*128)],
                        A_sb[:, grp*512:grp*512 + nc_in_g*128], start=True, stop=True),
                        waits=tuple(w))
                    lo_g, hi_g = grp * 4, grp * 4 + nc_in_g
                    e0 = max(0, E - 3)
                    blo, bhi = max(lo_g, e0), hi_g
                    dvt = None
                    if blo < bhi:
                        woff3 = (3 - (E - e0)) * 128
                        dvt = dv(lambda e, h=h, j=j, blo=blo, bhi=bhi, lo_g=lo_g, e0=e0, woff3=woff3, grp=grp:
                                 e.scalar_tensor_tensor(
                                     PS[2 + grp % 2][:, (blo-lo_g)*128:(bhi-lo_g)*128],
                                     BK3[:, j*384 + woff3 + (blo-e0)*128: j*384 + woff3 + (bhi-e0)*128],
                                     qsb[:, h*16+j:h*16+j+1],
                                     PS[2 + grp % 2][:, (blo-lo_g)*128:(bhi-lo_g)*128],
                                     ALU.mult, ALU.add), waits=(("pe", sc_mm),))
                    if hi_g == E:
                        dvt = dv(lambda e, j=j, nc_in_g=nc_in_g, grp=grp: e.tensor_tensor(
                            PS[2 + grp % 2][:, (nc_in_g-2)*128: nc_in_g*128],
                            PS[2 + grp % 2][:, (nc_in_g-2)*128: nc_in_g*128],
                            caus2_sb[:], ALU.add),
                            waits=(("pe", sc_mm),) if dvt is None else ())
                    if j == 15:
                        dvt = dvr(lambda e, nc_in_g=nc_in_g, grp=grp: e.tensor_reduce(
                            negc15[:], PS[2 + grp % 2][:, 0:nc_in_g*128], AX.X, ALU.max),
                            waits=(("dv", dvt),) if dvt is not None else (("pe", sc_mm),))
                        dvt = dv(lambda e: e.tensor_scalar(negc15[:], negc15[:], -1.0, None, ALU.mult))
                        bias_ap = negc15
                    else:
                        bias_ap = None
                    w_exp = [("pe", sc_mm)] if dvt is None else [("dv", dvt)]
                    a_ex = ac(lambda e, h=h, j=j, grp=grp, nc_in_g=nc_in_g, bias_ap=bias_ap, pp=pp:
                              e.activation(
                                  probs[pp][:, grp*512: grp*512 + nc_in_g*128],
                                  PS[2 + grp % 2][:, 0:nc_in_g*128],
                                  AF.Exp, bias=(bias_ap[:] if bias_ap is not None else
                                                negc_bc[:, h*16+j:h*16+j+1]),
                                  accum_out=sumg[pp][:, grp:grp+1]),
                              waits=tuple(w_exp))
                    last_ex = a_ex
                    g3[("sc", bnk)] = [("ac", a_ex)] + ([("dv", dvt)] if dvt is not None else [])
                return last_ex

            def emit_rtail(h, j, a_last):
                E = EXT[j]
                ngrp = (E + 3) // 4
                pp = j % 2
                rd = dvr(lambda e, ngrp=ngrp, pp=pp: e.tensor_reduce(sumtot[pp][:], sumg[pp][:, 0:ngrp],
                                                                     AX.X, ALU.add),
                         waits=(("ac", a_last),))
                g3[("sg", pp)] = [("dv", rd)]
                ac(lambda e, pp=pp: e.activation(sumtot[pp][:], sumtot[pp][:], AF.Ln),
                   waits=(("dv", rd),))
                w_r = g3.pop(("rctx", pp), [])
                a_r = ac(lambda e, pp=pp: e.activation(rinv[pp][:], sumtot[pp][:], AF.Exp, scale=-1.0),
                         waits=tuple(w_r))
                g3[("rinv", pp)] = a_r

            def emit_pt(h, j, a_last):
                E = EXT[j]
                pp = j % 2
                last_tk = None
                ticks = {}
                for c in range(E):
                    tbnk = 4 + (c % 4)
                    w = list(g3.get(("pt", c % 4), []))
                    if c < 4:
                        w.append(("ac", a_last))
                    if c in (2, 3):
                        w += g3.get(("rctx", 1 - pp), [])
                    tk = pe(lambda e, c=c, tbnk=tbnk, pp=pp: e.transpose(
                        PSb[tbnk][:, 0:128], probs[pp][:, c*128:(c+1)*128], ident_sb[:]), waits=tuple(w))
                    if c % 2 == 0:
                        t = dv(lambda e, c=c, tbnk=tbnk: e.tensor_copy(pts_all[:, c*128:(c+1)*128],
                                                                       PSb[tbnk][:, 0:128]),
                               waits=(("pe", tk),))
                        g3[("pt", c % 4)] = [("dv", t)]
                        ticks["dv"] = t
                    else:
                        t = ac(lambda e, c=c, tbnk=tbnk: e.activation(pts_all[:, c*128:(c+1)*128],
                                                                      PSb[tbnk][:, 0:128], AF.Copy),
                               waits=(("pe", tk),))
                        g3[("pt", c % 4)] = [("ac", t)]
                        ticks["ac"] = t
                    last_tk = tk
                g3[("ptw", pp)] = [("pe", last_tk)]
                g3[("ptd", pp)] = [(k, v) for k, v in ticks.items()]

            def emit_pvmm(h, j):
                E = EXT[j]
                pp = j % 2
                R = 256 + pp * 128
                d_pt = g3.pop(("ptd", pp))
                lastmm = None
                for c in range(E):
                    bnkp = 6 + (c % 2)
                    w = ()
                    if c < 2:
                        w = tuple(g3.pop(("ctx", bnkp, R), []) + (d_pt if c == 0 else []))
                    lastmm = pe(lambda e, c=c, bnkp=bnkp, R=R, E=E: e.matmul(
                        PS[bnkp][:, R:R+128], pts_all[:, c*128:(c+1)*128], Vx_sb[:, c*128:(c+1)*128],
                        start=(c < 2), stop=(c >= E - 2)), waits=w)
                a_r = g3[("rinv", pp)]
                dv(lambda e, R=R, pp=pp: e.tensor_scalar(ctxt_f[:], PS[7][:, R:R+128], rinv[pp][:],
                                                         None, ALU.mult),
                   waits=(("pe", lastmm), ("ac", a_r)))
                d_ctx = dv(lambda e, h=h, j=j, pp=pp, R=R: e.scalar_tensor_tensor(
                    A1[:, h*2048 + j*128: h*2048 + (j+1)*128], PS[6][:, R:R+128], rinv[pp][:],
                    ctxt_f[:], ALU.mult, ALU.add))
                g3[("ctx", 6, R)] = [("dv", d_ctx)]
                g3[("ctx", 7, R)] = [("dv", d_ctx)]
                g3[("rctx", pp)] = [("dv", d_ctx)]

            pend = None
            for j in range(NSLOTS):
                if pend is not None:
                    emit_pt(h, pend[0], pend[1])
                ax_j = emit_scores_mm(h, j)
                emit_rtail(h, j, ax_j)
                if pend is not None:
                    emit_pvmm(h, pend[0])
                pend = (j, ax_j)
            emit_pt(h, pend[0], pend[1])
            emit_pvmm(h, pend[0])

        if "ctx" in dbg:
            dq = P.c("dv")
            oq(lambda e: e.dma_start(out=dbg_out["d_ctx"][:, 0:NHEADS*2048], in_=A1[:, 0:NHEADS*2048]), waits=(("dv", dq),))
        barrier()
        p3stk.close()

    if ("p2" in phases) or ("p3" in phases):
        k2stk.close()

    if "p4" in phases:
        p4stk = stk.enter_context(ExitStack())
        woh = [sb(p4stk, f"woh{i}", [128, EMB], BF16) for i in range(2)]
        gbuf = [sb(p4stk, f"gbuf{i}", [128, EMB], BF16) for i in range(2)]
        xbuf = [sb(p4stk, f"xbuf{i}", [128, EMB], F32) for i in range(2)]
        obuf = [sb(p4stk, f"obuf{i}", [128, EMB], F32) for i in range(2)]
        tmp4 = sb(p4stk, "tmp4", [128, 512], F32)
        cstage = [sb(p4stk, f"cst{i}", [128, EMB], BF16) for i in range(2)]
        g4 = {}
        for ec in range(16):
            slot = ec % 2
            gsem = "wgq0" if slot == 0 else "wgq1"
            w = g4.pop(("wo", slot), [])
            lw = gq(lambda e, ec=ec, slot=slot: e.dma_start(out=woh[slot][:], in_=Wo[ec*128:(ec+1)*128, :]),
                    gsem, waits=tuple(w))
            last = None
            for hh in range(16):
                bnk = hh % 2
                w = [(gsem, lw)] if hh == 0 else []
                w += g4.pop(("t", bnk), [])
                tk = pe(lambda e, hh=hh, ec=ec, bnk=bnk, slot=slot: e.transpose(
                    PSb[bnk][:, 0:128], woh[slot][:, hh*128:(hh+1)*128], ident_sb[:]), waits=tuple(w))
                last = dv(lambda e, hh=hh, ec=ec, bnk=bnk: e.tensor_copy(
                    A2[:, hh*2048 + ec*128: hh*2048 + (ec+1)*128], PSb[bnk][:, 0:128]),
                    waits=(("pe", tk),))
                g4[("t", bnk)] = [("dv", last)]
            g4[("wo", slot)] = [("pe", tk)]
        d_wot = P.c("dv")
        for mt in range(NTILE):
            pbuf = mt % 2
            w = g4.pop(("g", pbuf), [])
            lg = yq(lambda e, mt=mt, pbuf=pbuf: e.dma_start(out=gbuf[pbuf][:], in_=g_scr[mt*128:(mt+1)*128, :]),
                    "xq0" if pbuf == 0 else "xq1", waits=tuple(w))
            lx = yq(lambda e, mt=mt, pbuf=pbuf: e.dma_start(out=xbuf[pbuf][:], in_=x_own[mt*128:(mt+1)*128, :]),
                    "xq0" if pbuf == 0 else "xq1")
            d_cst = None
            for hh in range(16):
                bnk = hh % 2
                w = g4.pop(("t", bnk), [])
                if hh < 2 and mt == 0:
                    w += [("dv", d_wot)]
                tk = pe(lambda e, hh=hh, mt=mt, bnk=bnk: e.transpose(
                    PSb[bnk][:, 0:128], A1[:, hh*2048 + mt*128: hh*2048 + (mt+1)*128],
                    ident_sb[:]), waits=tuple(w))
                d_cst = dv(lambda e, hh=hh, bnk=bnk, pbuf=pbuf: e.tensor_copy(
                    cstage[pbuf][:, hh*128:(hh+1)*128], PSb[bnk][:, 0:128]),
                    waits=(("pe", tk),))
                g4[("t", bnk)] = [("dv", d_cst)]
            for half in range(2):
                for bo in range(2):
                    bnk = 2 + bo
                    e0c = half * 1024 + bo * 512
                    for hh in range(16):
                        w = []
                        if hh == 0:
                            w = g4.pop(("s", bnk), [])
                            if bo == 0 and half == 0:
                                w += [("dv", d_cst)]
                                if mt == 0:
                                    w += [("dv", d_wot)]
                        s_mm = pe(lambda e, hh=hh, bnk=bnk, e0c=e0c, mt=mt, pbuf=pbuf: e.matmul(
                            PS[bnk][:, 0:512],
                            cstage[pbuf][:, hh*128:(hh+1)*128],
                            A2[:, hh*2048 + e0c: hh*2048 + e0c + 512],
                            start=(hh == 0), stop=(hh == 15)), waits=tuple(w))
                    g4[("smm", bnk)] = s_mm
                for bo in range(2):
                    bnk = 2 + bo
                    e0c = half * 1024 + bo * 512
                    w = [("pe", g4[("smm", bnk)]), ((("xq0" if pbuf == 0 else "xq1"), lx))]
                    dv(lambda e, bnk=bnk, e0c=e0c, pbuf=pbuf: e.tensor_tensor(
                        tmp4[:], PS[bnk][:], gbuf[pbuf][:, e0c:e0c+512], ALU.mult), waits=tuple(w))
                    d_cb = dv(lambda e, bnk=bnk, e0c=e0c, pbuf=pbuf: e.tensor_tensor(
                        obuf[pbuf][:, e0c:e0c+512], tmp4[:], xbuf[pbuf][:, e0c:e0c+512], ALU.add))
                    g4[("s", bnk)] = [("dv", d_cb)]
            so = oq(lambda e, mt=mt, pbuf=pbuf: e.dma_start(out=out[mt*128:(mt+1)*128, :], in_=obuf[pbuf][:]),
                    waits=(("dv", P.c("dv")),))
            g4[("g", pbuf)] = [("oq", so)]

    for s_ in ("oq", "xq0", "xq1", "cq", "qq", "vq", "kvq", "wgq0", "wgq1", "wgq2", "wdq", "wcq0", "wcq1"):
        P.op("y", lambda e, s_=s_: e.wait_ge(P.sems[s_], P.c(s_)))

    P.emit()
    return nc, dbg_out



import numpy as np
import ml_dtypes

def perm_idx():
    p = np.arange(8192)
    return 4 * (p % 2048) + p // 2048

def make_consts():
    i_, j_ = np.mgrid[0:128, 0:128]
    m_prev = (j_ >= i_ + 64).astype(np.float32)
    m_diag = ((i_ - j_ >= 0) & (i_ - j_ <= 64)).astype(np.float32)
    caus = np.where(j_ <= i_, 0.0, -1.0e30).astype(np.float32)
    z = np.zeros((128, 128), np.float32)
    negf = np.full((128, 128), -1.0e30, np.float32)
    band3 = {
        0: np.concatenate([z, m_prev, m_diag], axis=1),
        1: np.concatenate([m_prev, m_diag, z], axis=1),
    }
    caus2 = {
        0: np.concatenate([z, caus], axis=1),
        1: np.concatenate([caus, negf], axis=1),
    }
    ident = np.eye(128, dtype=ml_dtypes.bfloat16)
    ones_f = np.ones((1, 128), np.float32)
    extras = {
        "ones_b": np.ones((1, 128), ml_dtypes.bfloat16),
        "identf": np.eye(128, dtype=np.float32),
        "ident16": np.eye(16, dtype=np.float32),
    }
    return band3, caus2, ident, ones_f, extras

def own_tiles(role):
    if role == 0:
        return [31 - 2 * k for k in range(16)]
    return [30 - 2 * k for k in range(16)]

def core_inputs(core, x, Wq, Wk, Wv, q2l, vup, Wo, wg, wu, wd):
    b, role = core // 2, core % 2
    band3, caus2, ident, ones_f, extras = make_consts()
    idx = perm_idx()
    tiles = own_tiles(role)
    xs = np.ascontiguousarray(x[b])
    xo = np.ascontiguousarray(
        xs.reshape(32, 128, 2048)[tiles].reshape(2048, 2048))
    return {
        "x_seq": xs, "x_own": xo,
        "Wq": Wq, "Wk": Wk, "Wv": Wv, "q2l": q2l, "vup": vup, "Wo": Wo,
        "wg_p": np.ascontiguousarray(wg[:, idx]),
        "wu_p": np.ascontiguousarray(wu[:, idx]),
        "wd_p": np.ascontiguousarray(wd[idx, :]),
        "ident": ident, "band3": band3[role], "caus2": caus2[role],
        "ones_f": ones_f, **extras,
    }



_PROG_CACHE = {}

def kernel(L0_attn_hidden_states, L0_attn_Wq, L0_attn_Wk, L0_attn_Wv, L0_attn_q_to_latent,
           L0_attn_v_up, L0_attn_Wo, L0_ffn_w_gate, L0_ffn_w_up, L0_ffn_w_down):
    x = np.ascontiguousarray(np.asarray(L0_attn_hidden_states, dtype=np.float32))
    args = (np.ascontiguousarray(np.asarray(L0_attn_Wq, dtype=np.float32)),
            np.ascontiguousarray(np.asarray(L0_attn_Wk, dtype=np.float32)),
            np.ascontiguousarray(np.asarray(L0_attn_Wv, dtype=np.float32)),
            np.ascontiguousarray(np.asarray(L0_attn_q_to_latent, dtype=np.float32)),
            np.ascontiguousarray(np.asarray(L0_attn_v_up, dtype=np.float32)),
            np.ascontiguousarray(np.asarray(L0_attn_Wo, dtype=np.float32)),
            np.ascontiguousarray(np.asarray(L0_ffn_w_gate, dtype=np.float32)),
            np.ascontiguousarray(np.asarray(L0_ffn_w_up, dtype=np.float32)),
            np.ascontiguousarray(np.asarray(L0_ffn_w_down, dtype=np.float32)))
    if "nc" not in _PROG_CACHE:
        nc, _ = build_program(dbg=(), phases=("p0", "p1", "p2", "p3", "p4"))
        _PROG_CACHE["nc"] = nc
    nc = _PROG_CACHE["nc"]
    in_maps = [core_inputs(c, x, *args) for c in range(8)]
    res = run_bass_kernel_spmd(nc, in_maps, core_ids=list(range(8)))
    out = np.empty((4, 4096, 2048), dtype=np.float32)
    for c in range(8):
        b, role = c // 2, c % 2
        o = np.asarray(res.results[c]["out"]).reshape(16, 128, 2048)
        for k, tau in enumerate(own_tiles(role)):
            out[b, tau*128:(tau+1)*128, :] = o[k]
    return out
